# version 28
# baseline (speedup 1.0000x reference)
import math
import contextlib
import numpy as np
import concourse.bass as bass
import concourse.mybir as mybir
from concourse.bass_utils import run_bass_kernel_spmd

F32 = mybir.dt.float32
BF16 = mybir.dt.bfloat16
AF = mybir.ActivationFunctionType
OP = mybir.AluOpType
AX = mybir.AxisListType.X

SEQ, OWN, DM = 4096, 2048, 1024
NG = 8
LAMBDA_INIT = 0.8 - 0.6 * math.exp(-0.3 * 12)
ONE_M_LI = 1.0 - LAMBDA_INIT
QK_SCALE = 32.0 ** -0.25
EPS = 1e-5
TCH = 1024
NCH = OWN // TCH

ENGINES = ("sync", "tensor", "scalar", "vector", "gpsimd")


class Sem:
    __slots__ = ("h", "n")

    def __init__(self, h):
        self.h = h
        self.n = 0


class Prog:

    CHAINED = ("vector", "scalar", "gpsimd")

    def __init__(self, chain_sems=None):
        self.q = {e: [] for e in ENGINES}
        self.last_wait = {e: {} for e in ENGINES}
        self.chain = chain_sems or {}

    def do(self, eng, fn, waits=(), incs=(), dma=False):
        cs = self.chain.get(eng)
        waits = list(waits)
        incs = list(incs)
        if cs is not None and fn is not None:
            waits.append((cs, cs.n))
            if not dma:
                incs.append((cs, 1))
        lw = self.last_wait[eng]
        ws = []
        for s, v in waits:
            if v <= 0:
                continue
            k = id(s)
            if lw.get(k, -1) >= v:
                continue
            lw[k] = v
            ws.append((s, v))
        self.q[eng].append((ws, fn, tuple(incs)))
        for s, k in incs:
            s.n += k

    def barrier(self, marks):
        for e in ENGINES:
            self.do(e, None, waits=list(marks))

    def replay(self, block):
        def mk(name):
            def body(eng):
                for ws, fn, incs in self.q[name]:
                    for s, v in ws:
                        eng.wait_ge(s.h, v)
                    if fn is None:
                        if incs:
                            ins = eng.nop()
                            for s, k in incs:
                                ins.then_inc(s.h, k)
                        continue
                    ins = fn(eng)
                    for s, k in incs:
                        ins.then_inc(s.h, k)
            return body
        block.sync(mk("sync"))
        block.tensor(mk("tensor"))
        block.scalar(mk("scalar"))
        block.vector(mk("vector"))
        block.gpsimd(mk("gpsimd"))


class Stream:

    def __init__(self, prog, eng, sems):
        self.prog = prog
        self.eng = eng
        self.sems = sems
        self.idx = 0

    def issue(self, fn, waits=()):
        s = self.sems[self.idx % 2]
        self.idx += 1
        w = list(waits) + [(s, s.n)]
        self.prog.do(self.eng, fn, waits=w, incs=[(s, 16)], dma=True)
        return (s, s.n)

    def fin(self):
        return [(s, s.n) for s in self.sems]


def build(tap=()):
    nc = bass.Bass("TRN2", target_bir_lowering=False)

    x_d = nc.dram_tensor("x", [SEQ, DM], F32, kind="ExternalInput").ap()
    wq_d = nc.dram_tensor("wq4", [8, 128, 1024], F32, kind="ExternalInput").ap()
    wk_d = nc.dram_tensor("wk4", [8, 128, 1024], F32, kind="ExternalInput").ap()
    wv_d = nc.dram_tensor("wv4", [8, 128, 1024], F32, kind="ExternalInput").ap()
    wo_d = nc.dram_tensor("wo4", [8, 128, 1024], F32, kind="ExternalInput").ap()
    w1_d = nc.dram_tensor("w14", [32, 128, 1024], F32, kind="ExternalInput").ap()
    w2_d = nc.dram_tensor("w24", [32, 128, 1024], F32, kind="ExternalInput").ap()
    h1_d = nc.dram_tensor("h14", [32, 128, 4096], F32, kind="ExternalInput").ap()
    h2_d = nc.dram_tensor("h24", [32, 128, 4096], F32, kind="ExternalInput").ap()
    w3_d = nc.dram_tensor("w34", [8, 128, 4096], F32, kind="ExternalInput").ap()
    s1_d = nc.dram_tensor("s1c", [128, 32], F32, kind="ExternalInput").ap()
    s2_d = nc.dram_tensor("s2c", [128, 32], F32, kind="ExternalInput").ap()
    h1s_d = nc.dram_tensor("h1sc", [128, 32], F32, kind="ExternalInput").ap()
    h2s_d = nc.dram_tensor("h2sc", [128, 32], F32, kind="ExternalInput").ap()
    w3s_d = nc.dram_tensor("w3sc", [128, 8], F32, kind="ExternalInput").ap()
    g_d = nc.dram_tensor("g128", [128, 1], F32, kind="ExternalInput").ap()
    lv_d = nc.dram_tensor("lamvec", [1, 128], F32, kind="ExternalInput").ap()
    out_d = nc.dram_tensor("out", [OWN, DM], F32, kind="ExternalOutput").ap()
    scr_d = nc.dram_tensor("attn_scr", [DM, OWN], F32, kind="Internal").ap()
    taps = {}
    for t in tap:
        shp = {"xn_own": [128, 16384], "xn_oth": [128, 16384],
               "attn_cat": [128, 16384], "attn_scr": [DM, OWN],
               "a1": [128, 32 * TCH], "kg": [128, 4096],
               "qg": [128, 2048], "vgt": [128, 32 * 130],
               "bias": [128, 2], "o0": [65, 512], "o1": [65, 512],
               "xnc": [128, 8 * TCH], "a2": [128, 32 * TCH],
               "outT": [128, 8 * TCH]}[t]
        taps[t] = nc.dram_tensor("tap_" + t, shp, F32, kind="ExternalOutput").ap()

    A = nc.alloc_sbuf_tensor
    identb = A("identb", [128, 128], BF16).ap()
    s1s = A("s1s", [128, 32], F32).ap()
    s2s = A("s2s", [128, 32], F32).ap()
    h1ss = A("h1ss", [128, 32], F32).ap()
    h2ss = A("h2ss", [128, 32], F32).ap()
    w3ss = A("w3ss", [128, 8], F32).ap()
    g128 = A("g128s", [128, 1], F32).ap()
    g128s = A("g128sc", [128, 1], F32).ap()
    lvs = A("lvs", [1, 128], F32).ap()
    epsb = A("epsb", [128, 1], F32).ap()
    ones64f = A("ones64f", [1, 64], BF16).ap()
    ones64c = A("ones64c", [64, 1], BF16).ap()
    ones128f = A("ones128f", [1, 128], F32).ap()
    maskp = A("maskp", [128, 1], BF16).ap()
    smll = A("smll", [1, 64], F32).ap()
    lamA = A("lamA", [1, 1], F32).ap()
    bias_gp = A("bias_gp", [128, 2], F32).ap()
    xn_own = A("xn_own", [128, 8 * OWN], BF16).ap()
    w12b = A("w12b", [128, 4 * 1024], BF16).ap()

    stack = contextlib.ExitStack()
    sem = {n: Sem(stack.enter_context(nc.semaphore(n))) for n in
           ("t", "a", "v", "g", "fin", "cv", "ca", "cg",
            "wd0", "wd1", "xd0", "xd1", "sw0", "sw1", "ow0", "ow1")}
    prog = Prog(chain_sems={"vector": sem["cv"], "scalar": sem["ca"],
                            "gpsimd": sem["cg"]})
    sem["v"], sem["a"], sem["g"] = sem["cv"], sem["ca"], sem["cg"]
    st, sa, sv, sg = sem["t"], sem["a"], sem["v"], sem["g"]
    wdma = Stream(prog, "gpsimd", [sem["wd0"], sem["wd1"]])
    xdma = Stream(prog, "sync", [sem["xd0"], sem["xd1"]])
    sdma = Stream(prog, "sync", [sem["sw0"], sem["sw1"]])
    odma = Stream(prog, "sync", [sem["ow0"], sem["ow1"]])

    for ap in (identb,):
        prog.do("gpsimd", lambda e, ap=ap: e.memset(ap, 0.0))
        prog.do("gpsimd", lambda e, ap=ap: e.affine_select(
            out=ap, in_=ap, compare_op=OP.not_equal, fill=1.0, base=0,
            pattern=[[-1, 128]], channel_multiplier=1))
    for ap, val in ((epsb, EPS), (ones64f, 1.0), (ones64c, 1.0),
                    (ones128f, 1.0), (maskp, 1.0)):
        prog.do("gpsimd", lambda e, ap=ap, val=val: e.memset(ap, val))
    prog.do("gpsimd", lambda e: e.nop())
    m_init = sg.n
    for dst, src in ((s1s, s1_d), (s2s, s2_d), (h1ss, h1s_d), (h2ss, h2s_d),
                     (w3ss, w3s_d), (g128, g_d), (lvs, lv_d)):
        wdma.issue(lambda e, d=dst, s=src: e.dma_start(d, s))
    prog.do("gpsimd", None, waits=wdma.fin())
    prog.do("gpsimd", lambda e: e.nop())
    m_consts = sg.n
    def emit_lam_g128():
        prog.do("scalar",
                lambda e: e.activation(g128s, g128, AF.Copy, scale=ONE_M_LI),
                waits=[(sg, m_consts)])
        mg = sa.n
        prog.do("vector", lambda e: e.tensor_mul(smll[:, 0:32], lvs[:, 0:32],
                                                 lvs[:, 32:64]),
                waits=[(sg, m_consts)])
        prog.do("vector", lambda e: e.tensor_mul(smll[:, 32:64], lvs[:, 64:96],
                                                 lvs[:, 96:128]))
        prog.do("vector", lambda e: e.reduce_sum(smll[:, 0:1], smll[:, 0:32],
                                                 axis=AX))
        prog.do("vector", lambda e: e.reduce_sum(smll[:, 1:2], smll[:, 32:64],
                                                 axis=AX))
        m1 = sv.n
        prog.do("scalar", lambda e: e.activation(smll[:, 2:3], smll[:, 0:1],
                                                 AF.Exp), waits=[(sv, m1)])
        prog.do("scalar", lambda e: e.activation(smll[:, 3:4], smll[:, 1:2],
                                                 AF.Exp))
        m2 = sa.n
        prog.do("vector", lambda e: e.tensor_sub(smll[:, 4:5], smll[:, 2:3],
                                                 smll[:, 3:4]), waits=[(sa, m2)])
        prog.do("vector", lambda e: e.tensor_single_scalar(
            lamA, smll[:, 4:5], LAMBDA_INIT, OP.add))
        return mg

    def emit_ln(bufs, row0, dest_writer, ln_v_marks, tr_a_marks, extra_tr_waits,
                trp, extra_dma_waits=()):
        xb, xc, sq, xnt, rr = bufs
        war = [(sv, ln_v_marks[-2])] if len(ln_v_marks) >= 2 else []
        war += list(extra_dma_waits)
        mk_x = xdma.issue(lambda e, xb=xb, row0=row0: e.dma_start(
            xb, x_d[row0:row0 + 128, :]), waits=war)
        prog.do("vector", lambda e, xb=xb, r=rr[0]: e.reduce_sum(r, xb, axis=AX),
                waits=[mk_x])
        prog.do("vector", lambda e, r=rr[1], s=rr[0]: e.tensor_single_scalar(
            r, s, -1.0 / DM, OP.mult))
        prog.do("vector", lambda e, xc=xc, xb=xb, r=rr[1]: e.tensor_scalar(
            xc, xb, r, None, OP.add))
        prog.do("vector", lambda e, sq=sq, xc=xc: e.tensor_mul(sq, xc, xc))
        prog.do("vector", lambda e, r=rr[2], sq=sq: e.reduce_sum(r, sq, axis=AX),
                )
        ln_v_marks.append(sv.n)
        mv = sv.n
        prog.do("scalar", lambda e, r=rr[2]: e.activation(
            r, r, AF.Ln, bias=epsb, scale=1.0 / DM),
            waits=[(sv, mv), (sg, m_init)])
        prog.do("scalar", lambda e, d=rr[3], s=rr[2]: e.activation(
            d, s, AF.Exp, scale=-0.5))
        ma = sa.n
        prog.do("vector", lambda e, d=xnt, xc=xc, r=rr[3]: e.tensor_scalar(
            d, xc, r, None, OP.mult), waits=[(sa, ma)])
        mn = sv.n
        ln_v_marks[-1] = sv.n
        for c in range(8):
            jp = c % 2
            war2 = [(sa, tr_a_marks[-2])] if len(tr_a_marks) >= 2 else []
            prog.do("tensor", lambda e, pt=trp[jp], xnt=xnt, c=c: e.transpose(
                pt, xnt[:, 128 * c:128 * (c + 1)], identb),
                waits=[(sv, mn), (sg, m_init)] + war2 + list(extra_tr_waits),
                incs=[(st, 1)])
            mt = st.n
            dst = dest_writer(c)
            prog.do("scalar", lambda e, d=dst, pt=trp[jp]: e.activation(
                d, pt, AF.Copy), waits=[(st, mt)])
            tr_a_marks.append(sa.n)

    if True:
        scA = stack.enter_context(contextlib.ExitStack())
        _sb = lambda n, s, d: scA.enter_context(nc.sbuf_tensor(n, s, d)).ap()
        xn_oth = _sb("xn_oth", [128, 8 * OWN], BF16)
        attn_cat = _sb("attn_cat", [128, 8 * OWN], BF16)
        wob_all = _sb("wob_all", [128, 8 * 1024], BF16)
        rows = _sb("rows", [1, 5 * 512], F32)
        rowsb = _sb("rowsb", [1, 2 * 512], BF16)
        wqb = [_sb(f"wqb{i}", [128, 1024], BF16) for i in range(2)]
        wkb = [_sb(f"wkb{i}", [128, 1024], BF16) for i in range(2)]
        wvb = [_sb(f"wvb{i}", [128, 1024], BF16) for i in range(2)]
        wotmp = [_sb(f"wotmp{i}", [128, 1024], BF16) for i in range(2)]
        psA = scA.enter_context(nc.psum_tensor("psA", [128, 1024], F32)).ap()
        psB = scA.enter_context(nc.psum_tensor("psB", [128, 1024], F32)).ap()
        psO = scA.enter_context(nc.psum_tensor("psO", [128, 1024], F32)).ap()
        psS = scA.enter_context(nc.psum_tensor("psS", [128, 1024], F32)).ap()
        S01 = [psA, psB]
        po0 = psO[0:65, 0:512]
        po1 = psO[0:65, 512:1024]
        pb = psO[0:128, 0:1]
        sp64 = psS[0:64, 0:512]
        sp1 = psS[0:1, 0:512]
        pwarm = psS[:, 512:1024]
        tr_psum = [psB[:, 0:64].bitcast(BF16), psB[:, 512:576].bitcast(BF16)]
        ppj = [psA[:, 0:512], psA[:, 512:1024]]

        grp_fetch = {}
        grp_lastuse = {}

        def fetch_group(g):
            war = []
            if (g - 2) in grp_lastuse:
                war = [(st, grp_lastuse[g - 2])]
            par = g % 2
            return (wdma.issue(lambda e, g=g: e.dma_start(wqb[par], wq_d[g]), war),
                    wdma.issue(lambda e, g=g: e.dma_start(wkb[par], wk_d[g]), war),
                    wdma.issue(lambda e, g=g: e.dma_start(wvb[par], wv_d[g]), war),
                    wdma.issue(lambda e, g=g: e.dma_start(wotmp[par], wo_d[g]), war))

        grp_fetch[0] = fetch_group(0)

        ln_v, tr_a = [], []
        sc0 = contextlib.ExitStack()
        with sc0:
            p0x = sc0.enter_context(nc.sbuf_tensor("ph0x", [128, 2048], F32)).ap()
            p0c = sc0.enter_context(nc.sbuf_tensor("ph0c", [128, 2048], F32)).ap()
            p0s = sc0.enter_context(nc.sbuf_tensor("ph0s", [128, 2048], F32)).ap()
            p0n = sc0.enter_context(nc.sbuf_tensor("ph0n", [128, 2048], BF16)).ap()
            p0r = sc0.enter_context(nc.sbuf_tensor("ph0r", [128, 8], F32)).ap()
            xn_tile_a = []
            for tt in range(32):
                par = tt % 2
                bufs = (p0x[:, 1024 * par:1024 * (par + 1)],
                        p0c[:, 1024 * par:1024 * (par + 1)],
                        p0s[:, 1024 * par:1024 * (par + 1)],
                        p0n[:, 1024 * par:1024 * (par + 1)],
                        [p0r[:, 4 * par + i:4 * par + i + 1] for i in range(4)])
                half = xn_own if tt < 16 else xn_oth
                tl = tt % 16

                def dw(c, half=half, tl=tl):
                    return half[:, OWN * c + 128 * tl:OWN * c + 128 * (tl + 1)]
                emit_ln(bufs, 128 * tt, dw, ln_v, tr_a, [], tr_psum)
                xn_tile_a.append(sa.n)
        m_xn_a = sa.n
        m_ph0_v = sv.n
        m_g128s = emit_lam_g128()
        if "xn_own" in taps:
            wdma.issue(lambda e: e.dma_start(taps["xn_own"], xn_own),
                       waits=[(sa, m_xn_a)])
        if "xn_oth" in taps:
            wdma.issue(lambda e: e.dma_start(taps["xn_oth"], xn_oth),
                       waits=[(sa, m_xn_a)])

        scG = scA.enter_context(contextlib.ExitStack())
        if True:
            kg = scG.enter_context(nc.sbuf_tensor("kg", [128, 4096], BF16)).ap()
            qg = scG.enter_context(nc.sbuf_tensor("qg", [128, 2048], BF16)).ap()
            vg = scG.enter_context(nc.sbuf_tensor("vg", [128, 4096], BF16)).ap()
            vgt = scG.enter_context(nc.sbuf_tensor("vgt", [128, 32 * 130], BF16)).ap()
            ksq = scG.enter_context(nc.sbuf_tensor("ksq", [128, 4096], BF16)).ap()
            e01 = [scG.enter_context(nc.sbuf_tensor(f"e01{i}", [128, 1024], BF16)).ap()
                   for i in range(3)]
            o0s = scG.enter_context(nc.sbuf_tensor("o0s", [65, 512], F32)).ap()
            o1s = scG.enter_context(nc.sbuf_tensor("o1s", [65, 512], F32)).ap()
            cs = scG.enter_context(nc.sbuf_tensor("cs", [64, 512], F32)).ap()
            t1s = scG.enter_context(nc.sbuf_tensor("t1s", [64, 512], F32)).ap()
            csq = scG.enter_context(nc.sbuf_tensor("csq", [64, 512], BF16)).ap()
            wosb = scG.enter_context(nc.sbuf_tensor("wosb", [128, 1024], F32)).ap()

            prog.do("gpsimd", lambda e: e.memset(vgt, 1.0),
                    waits=[(sa, m_xn_a), (sv, m_ph0_v)])
            m_vgt_ones = sg.n

            pending = []
            marks = {}
            cat_qv = {}
            mk_exp = {}
            mk_av = {}
            m_evac_prev = 0
            pj_a = []
            vt_v = []

            for g in range(NG):
                if g + 1 < NG:
                    grp_fetch[g + 1] = fetch_group(g + 1)
                par = g % 2
                mk_q, mk_k, mk_v, mk_o = grp_fetch[g]

                prog.do("vector", lambda e, g=g, w=wotmp[par]: e.tensor_scalar(
                    wob_all[:, 1024 * g:1024 * (g + 1)], w, g128s, None, OP.mult),
                    waits=[mk_o, (sa, m_g128s)])

                if g > 0:
                    prog.do("tensor", None, waits=[(st, grp_lastuse[g - 1])])

                def proj(wb, mk_w, nchunks, scale, dst):
                    for tc in range(nchunks):
                        pj = ppj[tc % 2]
                        src = xn_own if (nchunks == 4 or tc < 4) else xn_oth
                        toff = 512 * (tc if (nchunks == 4 or tc < 4) else tc - 4)
                        half_off = 0 if (nchunks == 4 or tc < 4) else 16
                        need_tile = half_off + (toff // 512) * 4 + 3
                        m_xn_need = xn_tile_a[need_tile]
                        war = [(sa, pj_a[-2])] if len(pj_a) >= 2 else []
                        w0 = []
                        if g > 0 and tc == 0:
                            w0 = [(sa, mk_exp.get(0, 0)), (sa, mk_exp.get(1, 0))]
                        for ic in range(8):
                            prog.do("tensor", lambda e, pj=pj, wb=wb, ic=ic,
                                    src=src, toff=toff: e.matmul(
                                        pj, wb[:, 128 * ic:128 * (ic + 1)],
                                        src[:, OWN * ic + toff:OWN * ic + toff + 512],
                                        start=(ic == 0), stop=(ic == 7)),
                                    waits=([mk_w, (sa, m_xn_need)] + war + w0)
                                    if ic == 0 else [],
                                    incs=[(st, 1)] if ic == 7 else [])
                        mt = st.n
                        dsl = dst[:, 512 * tc:512 * (tc + 1)]
                        if scale is None:
                            prog.do("scalar", lambda e, d=dsl, pj=pj:
                                    e.activation(d, pj, AF.Copy),
                                    waits=[(st, mt)])
                        else:
                            prog.do("scalar", lambda e, d=dsl, pj=pj, sc=scale:
                                    e.activation(d, pj, AF.Copy, scale=sc),
                                    waits=[(st, mt)])
                        pj_a.append(sa.n)
                    return sa.n

                m_k_a = proj(wkb[par], mk_k, 8, QK_SCALE, kg)
                m_v_a = proj(wvb[par], mk_v, 8, None, vg)
                m_q_a = proj(wqb[par], mk_q, 4, QK_SCALE, qg)

                for kt in range(32):
                    jp = kt % 2
                    war2 = [(sv, vt_v[-2])] if len(vt_v) >= 2 else []
                    prog.do("tensor", lambda e, pt=tr_psum[jp], kt=kt:
                            e.transpose(pt, vg[:, 128 * kt:128 * (kt + 1)],
                                        identb),
                            waits=[(sa, m_v_a), (sg, m_vgt_ones)] + war2,
                            incs=[(st, 1)])
                    mt = st.n
                    dst = vgt[:, 130 * kt:130 * kt + 130].rearrange(
                        "p (v d) -> p v d", v=2, d=65)[:, :, 0:64]
                    srcv = tr_psum[jp].rearrange("p (v d) -> p v d", v=2, d=64)
                    del jp
                    prog.do("vector", lambda e, d=dst, s=srcv: e.tensor_copy(d, s),
                            waits=[(st, mt)])
                    vt_v.append(sv.n)
                m_vgt_v = sv.n
                m_vt_t = st.n

                prog.do("vector", lambda e: e.tensor_mul(ksq, kg, kg),
                        waits=[(sa, m_k_a)])
                prog.do("vector", lambda e: e.tensor_mul(
                    vg[:, 0:2048], qg, qg),
                    waits=[(sa, m_q_a), (st, m_vt_t)])
                m_sq_v = sv.n

                ms_v = []
                scans = [(ksq, 0, 8, 8, 10, 0), (vg, 0, 4, 9, 11, 0),
                         (ksq, 0, 8, 13, 15, 1), (vg, 0, 4, 14, 16, 1)]
                steps = []
                for si, (srcap, base, nch, oc, tc_, p_) in enumerate(scans):
                    for c8 in range(nch):
                        steps.append((si, c8))
                order = []
                c8pos = [0] * 4
                while len(order) < len(steps):
                    for si in range(4):
                        nch = scans[si][2]
                        if c8pos[si] < nch:
                            order.append((si, c8pos[si]))
                            c8pos[si] += 1
                gi = 0
                for si, c8 in order:
                    srcap, base, nch, oc, tc_, p_ = scans[si]
                    pp = (psS[0:1, 0:512] if gi % 2 == 0
                          else psS[0:1, 512:1024])
                    gi += 1
                    war = [(sv, ms_v[-2])] if len(ms_v) >= 2 else []
                    prog.do("tensor", lambda e, pp=pp, c8=c8, p=p_,
                            base=base, srcap=srcap: e.matmul(
                                pp, maskp[64 * p:64 * (p + 1), :],
                                srcap[64 * p:64 * (p + 1),
                                      base + 512 * c8:base + 512 * (c8 + 1)],
                                start=True, stop=True),
                            waits=[(sv, m_sq_v),
                                   (sv, marks.get("c7", 0))] + war,
                            incs=[(st, 1)])
                    mt = st.n
                    if c8 == 0:
                        prog.do("vector", lambda e, pp=pp, oc=oc:
                                e.reduce_max(smll[:, oc:oc + 1], pp, axis=AX),
                                waits=[(st, mt)])
                    else:
                        prog.do("vector", lambda e, pp=pp, tc_=tc_:
                                e.reduce_max(smll[:, tc_:tc_ + 1], pp, axis=AX),
                                waits=[(st, mt)])
                        prog.do("vector", lambda e, oc=oc, tc_=tc_:
                                e.tensor_max(smll[:, oc:oc + 1],
                                             smll[:, oc:oc + 1],
                                             smll[:, tc_:tc_ + 1]))
                    ms_v.append(sv.n)
                prog.do("vector", lambda e: e.tensor_add(
                    smll[:, 11:12], smll[:, 8:9], smll[:, 9:10]))
                prog.do("vector", lambda e: e.tensor_single_scalar(
                    smll[:, 12:13], smll[:, 11:12], -0.5, OP.mult))
                prog.do("vector", lambda e: e.tensor_add(
                    smll[:, 17:18], smll[:, 13:14], smll[:, 14:15]))
                prog.do("vector", lambda e: e.tensor_single_scalar(
                    smll[:, 18:19], smll[:, 17:18], -0.5, OP.mult))
                m_mh_v = sv.n
                marks["stat_v"] = sv.n
                bias_a = {}
                for p_ in range(2):
                    col = 12 if p_ == 0 else 18
                    prog.do("tensor", lambda e, col=col: e.matmul(
                        pb, ones128f, smll[:, col:col + 1],
                        start=True, stop=True),
                        waits=[(sv, m_mh_v), (sv, m_evac_prev), (sg, m_init),
                               (sa, bias_a.get(0, 0))],
                        incs=[(st, 1)])
                    mt = st.n
                    prog.do("scalar", lambda e, p=p_: e.activation(
                        bias_gp[:, p:p + 1], pb, AF.Copy),
                        waits=[(st, mt)] + marks.pop("biastap", []))
                    bias_a[p_] = sa.n
                if "bias" in taps and g == 0:
                    mkb = sdma.issue(lambda e: e.dma_start(
                        taps["bias"], bias_gp), waits=[(sa, bias_a[1])])
                    marks["biastap"] = [mkb]

                def emit_warm():
                    prog.do("tensor", lambda e: e.matmul(
                        pwarm, identb, xn_own[:, 0:512],
                        start=True, stop=True),
                        waits=[(sv, marks.get("stat_v", 0)), (sa, m_xn_a),
                               (sg, m_init)])

                for p in range(2):
                    m_bias_a = bias_a[1]

                    def emit_av(kt, eb, p=p):
                        vsl = vgt[:, 130 * kt + 65 * p:
                                  130 * kt + 65 * (p + 1)]
                        avw = [(sa, mk_e_rdy[eb])]
                        if kt == 0:
                            avw.append((sv, m_evac_prev))
                        prog.do("tensor", lambda e, vsl=vsl, eb=eb, kt=kt:
                                e.matmul(po0, vsl, e01[eb][:, 0:512],
                                         start=(kt == 0), stop=(kt == 31)),
                                waits=avw)
                        prog.do("tensor", lambda e, vsl=vsl, eb=eb, kt=kt:
                                e.matmul(po1, vsl, e01[eb][:, 512:1024],
                                         start=(kt == 0), stop=(kt == 31)),
                                incs=[(st, 1)])
                        mk_av[eb] = st.n

                    mk_e_rdy = {}
                    for gkt in range(130):
                        if gkt < 128:
                            qv, kt = divmod(gkt, 32)
                            if kt in (8, 10, 16, 18, 24, 26) and pending:
                                pending.pop(0)()
                            kp = gkt % 2
                            kp3 = gkt % 3
                            Sps = S01[kp]
                            emit_warm()
                            for e_i in range(2):
                                w0 = []
                                if e_i == 0:
                                    w0 = [(sa, mk_exp.get(kp, 0)),
                                          (sa, m_bias_a), (sv, m_vgt_v)]
                                tp = (64 * p + 32 * e_i, 0)
                                prog.do("tensor", lambda e, Sps=Sps, e_i=e_i,
                                        kt=kt, p=p, qv=qv, tp=tp: e.matmul(
                                            Sps[:, 512 * e_i:512 * (e_i + 1)],
                                            kg[64 * p + 32 * e_i:
                                               64 * p + 32 * (e_i + 1),
                                               128 * kt:128 * (kt + 1)],
                                            qg[64 * p + 32 * e_i:
                                               64 * p + 32 * (e_i + 1),
                                               512 * qv:512 * (qv + 1)],
                                            start=True, stop=True,
                                            tile_position=tp),
                                        waits=w0,
                                        incs=[(st, 1)] if e_i == 1 else [])
                            mS = st.n
                            prog.do("scalar", lambda e, kp3=kp3, Sps=Sps, p=p:
                                    e.activation(e01[kp3], Sps, AF.Exp,
                                                 bias=bias_gp[:, p:p + 1]),
                                    waits=[(st, mS), (st, mk_av.get(kp3, 0))],
                                    )
                            mk_exp[kp] = sa.n
                            mk_e_rdy[kp3] = sa.n
                        if gkt < 2:
                            continue
                        t_ = gkt - 2
                        tqv, tk = divmod(t_, 32)
                        emit_warm()
                        emit_av(tk, t_ % 3)
                        if tk != 31:
                            continue
                        m_av = st.n
                        grp_lastuse[g] = st.n
                        prev_c7 = marks.get("c7", 0)
                        qv = tqv

                        def mk_c1(m_av=m_av, g=g, p=p, qv=qv, prev_c7=prev_c7):
                            prog.do("vector", lambda e: e.tensor_copy(
                                o0s, po0),
                                waits=[(st, m_av), (sv, prev_c7)]
                                + marks.pop("tapwar", []))
                            prog.do("vector", lambda e: e.tensor_copy(
                                o1s, po1))
                            marks["evac"] = sv.n
                            ma = sv.n
                            if "o0" in taps and g == 0 and p == 0 and qv == 0:
                                mk1 = sdma.issue(lambda e: e.dma_start(
                                    taps["o0"], o0s), waits=[(sv, ma)])
                                mk2 = sdma.issue(lambda e: e.dma_start(
                                    taps["o1"], o1s), waits=[(sv, ma)])
                                marks["tapwar"] = [mk1, mk2]
                            prog.do("vector", lambda e: e.reciprocal(
                                rows[:, 512:1024], o1s[64:65, :]))
                            prog.do("vector", lambda e: e.tensor_copy(
                                rows[:, 2048:2560], o0s[64:65, :]))
                            prog.do("vector", lambda e: e.scalar_tensor_tensor(
                                rows[:, 1024:1536], rows[:, 2048:2560], lamA,
                                rows[:, 512:1024], OP.mult, OP.mult))
                            prog.do("vector", lambda e: e.tensor_copy(
                                rowsb[:, 0:512], rows[:, 1024:1536]))
                            prog.do("vector", lambda e: e.reciprocal(
                                rows[:, 0:512], o0s[64:65, :]))
                            marks["c1"] = sv.n

                        def mk_c2():
                            prog.do("tensor", lambda e: e.matmul(
                                sp64, ones64f, rowsb[:, 0:512],
                                start=True, stop=True),
                                waits=[(sv, marks["c1"]),
                                       (sv, marks.get("c7", 0)),
                                       (sv, marks.get("stat_v", 0))],
                                incs=[(st, 1)])
                            marks["c2"] = st.n

                        def mk_c3():
                            prog.do("vector", lambda e: e.tensor_mul(
                                t1s, o1s[0:64, :], sp64),
                                waits=[(st, marks["c2"])])
                            prog.do("vector", lambda e: e.tensor_sub(
                                cs, o0s[0:64, :], t1s))
                            prog.do("vector", lambda e: e.tensor_mul(csq, cs, cs),
                                    )
                            marks["c3"] = sv.n

                        def mk_c4():
                            prog.do("tensor", lambda e: e.matmul(
                                sp1, ones64c, csq, start=True, stop=True),
                                waits=[(sv, marks["c3"])], incs=[(st, 1)])
                            marks["c4"] = st.n

                        def mk_c5():
                            prog.do("vector", lambda e: e.tensor_mul(
                                rows[:, 1536:2048], sp1, rows[:, 0:512]),
                                waits=[(st, marks["c4"])])
                            prog.do("vector", lambda e: e.tensor_mul(
                                rows[:, 1536:2048], rows[:, 1536:2048],
                                rows[:, 0:512]))
                            mv5 = sv.n
                            prog.do("scalar", lambda e: e.activation(
                                rows[:, 1536:2048], rows[:, 1536:2048], AF.Ln,
                                bias=epsb[0:1, :], scale=1.0 / 64),
                                waits=[(sv, mv5)])
                            prog.do("scalar", lambda e: e.activation(
                                rows[:, 1536:2048], rows[:, 1536:2048], AF.Exp,
                                scale=-0.5))
                            ma5 = sa.n
                            prog.do("vector", lambda e: e.tensor_mul(
                                rows[:, 1536:2048], rows[:, 1536:2048],
                                rows[:, 0:512]),
                                waits=[(sa, ma5)])
                            prog.do("vector", lambda e: e.tensor_copy(
                                rowsb[:, 512:1024], rows[:, 1536:2048]))
                            marks["c5"] = sv.n

                        def mk_c6():
                            prog.do("tensor", lambda e: e.matmul(
                                sp64, ones64f, rowsb[:, 512:1024],
                                start=True, stop=True),
                                waits=[(sv, marks["c5"]), (sv, marks["c3"])],
                                incs=[(st, 1)])
                            marks["c6"] = st.n

                        def mk_c7(g=g, p=p, qv=qv):
                            dst = attn_cat[64 * p:64 * (p + 1),
                                           OWN * g + 512 * qv:
                                           OWN * g + 512 * (qv + 1)]
                            prog.do("vector", lambda e, dst=dst: e.tensor_mul(
                                dst, cs, sp64),
                                waits=[(st, marks["c6"])])
                            marks["c7"] = sv.n
                            cat_qv[qv] = sv.n

                        mk_c1()
                        m_evac_prev = marks["evac"]
                        pending.extend([mk_c2, mk_c3, mk_c4, mk_c5, mk_c6,
                                        mk_c7])
            m_exp_l = max(mk_exp.values())

            wo_a = []
            wo_dma = []
            for u, (qc, ot) in enumerate((qc, ot) for qc in range(4)
                                         for ot in range(8)):
                    emit_warm()
                    if u == 4:
                        while len(pending) > 2:
                            pending.pop(0)()
                    if u == 8:
                        while pending:
                            pending.pop(0)()
                        m_cat_v = sv.n
                    pj = ppj[u % 2]
                    war = [(sv, wo_a[-2])] if len(wo_a) >= 2 else []
                    for ic in range(8):
                        prog.do("tensor", lambda e, pj=pj, ic=ic, ot=ot, qc=qc:
                                e.matmul(
                                    pj,
                                    wob_all[:, 1024 * ic + 128 * ot:
                                            1024 * ic + 128 * (ot + 1)],
                                    attn_cat[:, OWN * ic + 512 * qc:
                                             OWN * ic + 512 * (qc + 1)],
                                    start=(ic == 0), stop=(ic == 7)),
                                waits=([(sv, cat_qv[qc]), (sa, m_exp_l)]
                                       + war) if ic == 0 else [],
                                incs=[(st, 1)] if ic == 7 else [])
                    mt = st.n
                    wslots = [wosb[:, 0:512], wosb[:, 512:1024],
                              wqb[0].bitcast(F32), wqb[1].bitcast(F32)]
                    wdst = wslots[u % 4]
                    aw = [(st, mt)]
                    if len(wo_dma) >= 4:
                        aw.append(wo_dma[-4])
                    prog.do("vector", lambda e, d=wdst, pj=pj: e.tensor_copy(
                        d, pj), waits=aw)
                    wo_a.append(sv.n)
                    ma = sv.n
                    mk = sdma.issue(lambda e, d=wdst, ot=ot, qc=qc: e.dma_start(
                        scr_d[128 * ot:128 * (ot + 1), 512 * qc:512 * (qc + 1)],
                        d), waits=[(sv, ma)])
                    wo_dma.append(mk)
            if "attn_cat" in taps:
                wdma.issue(lambda e: e.dma_start(taps["attn_cat"], attn_cat),
                           waits=[(sv, m_cat_v)])
            if "kg" in taps:
                wdma.issue(lambda e: e.dma_start(taps["kg"], kg),
                           waits=[(sv, m_cat_v)])
            if "qg" in taps:
                wdma.issue(lambda e: e.dma_start(taps["qg"], qg),
                           waits=[(sv, m_cat_v)])
            if "vgt" in taps:
                wdma.issue(lambda e: e.dma_start(taps["vgt"], vgt),
                           waits=[(sv, m_cat_v)])
            if "attn_scr" in taps:
                sdma.issue(lambda e: e.dma_start(taps["attn_scr"], scr_d),
                           waits=sdma.fin())

            w12_pre = {}
            for ot in range(2):
                par = ot
                w1sl = w12b[:, 2048 * par:2048 * par + 1024]
                w2sl = w12b[:, 2048 * par + 1024:2048 * (par + 1)]
                w12_pre[ot] = (
                    wdma.issue(lambda e, d=w1sl, ot=ot: e.dma_start(d, w1_d[ot])),
                    wdma.issue(lambda e, d=w2sl, ot=ot: e.dma_start(d, w2_d[ot])))

    scA.close()
    bar_marks = ([(st, st.n), (sv, sv.n), (sa, sa.n), (sg, sg.n)]
                 + wdma.fin() + xdma.fin() + sdma.fin() + odma.fin())
    for _e in ("sync", "scalar", "vector", "gpsimd"):
        prog.do(_e, None, waits=bar_marks)
    scr_fin = sdma.fin()

    with (
        nc.sbuf_tensor("a1", [128, 32 * TCH], BF16) as d1_,
        nc.sbuf_tensor("hbT", [128, 3 * 4096], BF16) as hb_top_,
        nc.sbuf_tensor("xldp", [128, 2 * 1024], F32) as xld_,
        nc.sbuf_tensor("identf", [128, 128], F32) as idf_,
        nc.psum_tensor("pM0", [128, 1024], F32) as e1_,
        nc.psum_tensor("pM1", [128, 1024], F32) as e2_,
        nc.psum_tensor("pM2", [128, 1024], F32) as e3_,
        nc.psum_tensor("pM3", [128, 1024], F32) as e4_,
    ):
        a1 = d1_.ap()
        hb = hb_top_.ap()
        xld = xld_.ap()
        identf = idf_.ap()
        pM = [e1_.ap(), e2_.ap(), e3_.ap(), e4_.ap()]
        ptrF = [pM[3][:, 0:128], pM[3][:, 512:640]]
        prog.do("gpsimd", lambda e: e.memset(identf, 0.0))
        prog.do("gpsimd", lambda e: e.affine_select(
            out=identf, in_=identf, compare_op=OP.not_equal, fill=1.0, base=0,
            pattern=[[-1, 128]], channel_multiplier=1))
        m_idf = sg.n

        w12_t, w12_a, w12_v = [], [], []
        h_t, h_a = [], []
        ring_t = []
        w3_t, w3_a, f3_v, f3_t, pc_v = [], [], [], [], []
        xadd_v = []
        out_done = []
        tap_gate = []

        for ch in range(NCH):
            coff = TCH * ch
            chunk_gate = []
            if ch > 0:
                chunk_gate = ([(st, w3_t[-1]), (sv, pc_v[-1]),
                               (sa, w3_a[-1])] + list(tap_gate))

            if "xnc" in taps and ch == 0:
                for icq in range(8):
                    tap_gate.append(wdma.issue(
                        lambda e, icq=icq: e.dma_start(
                            taps["xnc"][:, TCH * icq:TCH * (icq + 1)],
                            xn_own[:, OWN * icq + coff:
                                   OWN * icq + coff + TCH]), waits=[]))

            if True:
                with nc.sbuf_tensor(f"t0b{ch}", [128, 2 * 1024], F32) as t0b_:
                    t0b = t0b_.ap()
                    for ot in range(32):
                        par = ot % 2
                        w1sl = w12b[:, 2048 * par:2048 * par + 1024]
                        w2sl = w12b[:, 2048 * par + 1024:2048 * (par + 1)]
                        if ch == 0 and ot < 2:
                            mk1, mk2 = w12_pre[ot]
                        else:
                            war = ([(st, w12_t[-2])] if len(w12_t) >= 2
                                   else [])
                            mk1 = wdma.issue(lambda e, d=w1sl, ot=ot:
                                             e.dma_start(d, w1_d[ot]),
                                             waits=war)
                            mk2 = wdma.issue(lambda e, d=w2sl, ot=ot:
                                             e.dma_start(d, w2_d[ot]),
                                             waits=war)
                        pa = pM[0] if par == 0 else pM[2]
                        pb2 = pM[1] if par == 0 else pM[3]
                        war_a = [(sa, w12_a[-2])] if len(w12_a) >= 2 else []
                        war_v = [(sv, w12_v[-2])] if len(w12_v) >= 2 else []
                        w1w0, w1w1 = [], []
                        if ch == 0 and ot == 0:
                            w1w0 = [(sv, wo_a[-2])]
                            w1w1 = [(sv, wo_a[-1])]
                        elif ch > 0 and ot == 0:
                            w1w0 = [(sa, w3_a[-1])] + list(tap_gate)
                        elif ch > 0 and ot == 1:
                            w1w0 = [(sv, pc_v[-1])]
                        for ic in range(8):
                            for hc in range(2):
                                prog.do("tensor", lambda e, pa=pa, w=w1sl,
                                        ic=ic, hc=hc, coff=coff: e.matmul(
                                            pa[:, 512 * hc:512 * (hc + 1)],
                                            w[:, 128 * ic:128 * (ic + 1)],
                                            xn_own[:, OWN * ic + coff + 512 * hc:
                                                   OWN * ic + coff + 512 * (hc + 1)],
                                            start=(ic == 0), stop=(ic == 7)),
                                        waits=([mk1] + war_a + w1w0)
                                        if (ic == 0 and hc == 0) else
                                        (w1w1 if (ic == 0 and hc == 1) else []))
                        for ic in range(8):
                            for hc in range(2):
                                lastw = (ic == 7 and hc == 1)
                                prog.do("tensor", lambda e, pb2=pb2, w=w2sl,
                                        ic=ic, hc=hc, coff=coff: e.matmul(
                                            pb2[:, 512 * hc:512 * (hc + 1)],
                                            w[:, 128 * ic:128 * (ic + 1)],
                                            xn_own[:, OWN * ic + coff + 512 * hc:
                                                   OWN * ic + coff + 512 * (hc + 1)],
                                            start=(ic == 0), stop=(ic == 7)),
                                        waits=([mk2] + war_v)
                                        if (ic == 0 and hc == 0) else [],
                                        incs=[(st, 1)] if lastw else [])
                        w12_t.append(st.n)
                        mt = st.n
                        t0 = t0b[:, 1024 * par:1024 * (par + 1)]
                        prog.do("scalar", lambda e, t0=t0, pa=pa, ot=ot:
                                e.activation(t0, pa, AF.Silu,
                                             scale=s1s[:, ot:ot + 1]),
                                waits=[(st, mt)] + war_v)
                        w12_a.append(sa.n)
                        ma = sa.n
                        prog.do("vector", lambda e, pb2=pb2, t0=t0, ot=ot:
                                e.scalar_tensor_tensor(
                                    a1[:, TCH * ot:TCH * (ot + 1)], pb2,
                                    s2s[:, ot:ot + 1], t0, OP.mult, OP.mult),
                                waits=[(sa, ma)])
                        w12_v.append(sv.n)
                    m_a1_v = sv.n
                a1_tap_mk = []
                if "a1" in taps and ch == 0:
                    a1_tap_mk.append(wdma.issue(
                        lambda e: e.dma_start(taps["a1"], a1),
                        waits=[(sv, m_a1_v)]))

            with (
                nc.sbuf_tensor(f"a2{ch}", [128, 32 * TCH], BF16) as i1_,
            ):
                a2 = i1_.ap()

                def hstage(hd, srcap, dstap, ssc, src_ready, extra0=()):
                    for ot in range(32):
                        par = ot % 2
                        sl3 = len(ring_t) % 3
                        wsl = hb[:, 4096 * sl3:4096 * (sl3 + 1)]
                        war = [(st, ring_t[-3])] if len(ring_t) >= 3 else []
                        mk = wdma.issue(lambda e, d=wsl, ot=ot, hd=hd:
                                        e.dma_start(d, hd[ot]), waits=war)
                        pp = pM[par]
                        war_a = [(sa, h_a[-2])] if len(h_a) >= 2 else []
                        if not h_a:
                            war_a = [(sv, m_a1_v)]
                        if ot == 0:
                            war_a = war_a + list(extra0)
                        for ic in range(32):
                            for hc in range(2):
                                lastw = (ic == 31 and hc == 1)
                                prog.do("tensor", lambda e, pp=pp, w=wsl,
                                        ic=ic, hc=hc, srcap=srcap: e.matmul(
                                            pp[:, 512 * hc:512 * (hc + 1)],
                                            w[:, 128 * ic:128 * (ic + 1)],
                                            srcap[:, TCH * ic + 512 * hc:
                                                  TCH * ic + 512 * (hc + 1)],
                                            start=(ic == 0), stop=(ic == 31)),
                                        waits=([mk, src_ready] + war_a)
                                        if (ic == 0 and hc == 0) else [],
                                        incs=[(st, 1)] if lastw else [])
                        h_t.append(st.n)
                        ring_t.append(st.n)
                        mt = st.n
                        prog.do("scalar", lambda e, pp=pp, ot=ot, dstap=dstap,
                                ssc=ssc: e.activation(
                                    dstap[:, TCH * ot:TCH * (ot + 1)], pp,
                                    AF.Gelu, scale=ssc[:, ot:ot + 1]),
                                waits=[(st, mt)])
                        h_a.append(sa.n)
                    return sa.n

                m_a2_a = hstage(h1_d, a1, a2, h1ss, (sv, m_a1_v),
                                extra0=((list(out_done[-1])
                                         + [(sv, pc_v[-1])])
                                        if out_done else ()))
                if "a2" in taps and ch == 0:
                    tap_gate.append(wdma.issue(
                        lambda e, a2=a2: e.dma_start(taps["a2"], a2),
                        waits=[(sa, m_a2_a)]))
                m_a3_a = hstage(h2_d, a2, a1, h2ss, (sa, m_a2_a),
                                extra0=a1_tap_mk)

            with (
                nc.sbuf_tensor(f"f3s{ch}", [128, 2 * 1024], F32) as j2_,
                nc.sbuf_tensor(f"asld{ch}", [128, 2 * 1024], F32) as j3_,
                nc.sbuf_tensor(f"outT{ch}", [128, 8 * 1024], F32) as j5_,
            ):
                f3sb, asld, outT = (j2_.ap(), j3_.ap(), j5_.ap())
                row_mk = []
                for ot3 in range(8):
                    par = ot3 % 2
                    xwar = ([(sv, xadd_v[-2])] if len(xadd_v) >= 2 else [])
                    if ot3 < 2 and out_done:
                        xwar = xwar + list(out_done[-1])
                    xmk = []
                    for tj in range(8):
                        row0 = coff + 128 * tj
                        xmk.append(xdma.issue(
                            lambda e, tj=tj, row0=row0, par=par, ot3=ot3:
                            e.dma_start(
                                xld[:, 1024 * par + 128 * tj:
                                    1024 * par + 128 * (tj + 1)],
                                x_d[row0:row0 + 128,
                                    128 * ot3:128 * (ot3 + 1)]), waits=xwar))
                    sl3 = len(ring_t) % 3
                    wsl = hb[:, 4096 * sl3:4096 * (sl3 + 1)]
                    war = [(st, ring_t[-3])] if len(ring_t) >= 3 else []
                    mk = wdma.issue(lambda e, d=wsl, ot3=ot3: e.dma_start(
                        d, w3_d[ot3]), waits=war)
                    mka = sdma.issue(lambda e, ot3=ot3, par=par, coff=coff,
                                     asld=asld: e.dma_start(
                                         asld[:, 1024 * par:1024 * (par + 1)],
                                         scr_d[128 * ot3:128 * (ot3 + 1),
                                               coff:coff + TCH]),
                                     waits=scr_fin + (chunk_gate +
                                      [(sa, m_a3_a)] if ot3 < 2 else []) +
                                     ([(sv, f3_v[-2])] if len(f3_v) >= 2
                                      else [(sa, m_a3_a)]))
                    pp = pM[par]
                    war_a = [(sa, w3_a[-2])] if len(w3_a) >= 2 else []
                    for ic in range(32):
                        for hc in range(2):
                            lastw = (ic == 31 and hc == 1)
                            prog.do("tensor", lambda e, pp=pp, w=wsl, ic=ic,
                                    hc=hc: e.matmul(
                                        pp[:, 512 * hc:512 * (hc + 1)],
                                        w[:, 128 * ic:128 * (ic + 1)],
                                        a1[:, TCH * ic + 512 * hc:
                                            TCH * ic + 512 * (hc + 1)],
                                        start=(ic == 0), stop=(ic == 31)),
                                    waits=([mk, (sa, m_a3_a)] + war_a)
                                    if (ic == 0 and hc == 0) else [],
                                    incs=[(st, 1)] if lastw else [])
                    w3_t.append(st.n)
                    ring_t.append(st.n)
                    mt = st.n
                    f3 = f3sb[:, 1024 * par:1024 * (par + 1)]
                    fw = [(st, mt)]
                    if len(f3_t) >= 2:
                        fw.append((st, f3_t[-2]))
                    if ot3 < 2 and chunk_gate:
                        fw += chunk_gate
                    prog.do("scalar", lambda e, f3=f3, pp=pp, ot3=ot3:
                            e.activation(f3, pp, AF.Copy,
                                         scale=w3ss[:, ot3:ot3 + 1]),
                            waits=fw)
                    w3_a.append(sa.n)
                    ma = sa.n
                    prog.do("vector", lambda e, f3=f3, par=par, asld=asld:
                            e.tensor_add(
                        f3, f3, asld[:, 1024 * par:1024 * (par + 1)]),
                        waits=[(sa, ma), mka, (sv, m_a1_v)])
                    f3_v.append(sv.n)
                    mfv = sv.n
                    for tj in range(8):
                        jp = tj % 2
                        war2 = [(sv, pc_v[-2])] if len(pc_v) >= 2 else []
                        if not pc_v:
                            war2 = [(sg, m_idf)]
                        prog.do("tensor", lambda e, jp=jp, f3=f3, tj=tj:
                                e.transpose(ptrF[jp],
                                            f3[:, 128 * tj:128 * (tj + 1)],
                                            identf),
                                waits=[(sv, mfv)] + war2, incs=[(st, 1)])
                        mt2 = st.n
                        dst = outT[:, TCH * tj + 128 * ot3:
                                   TCH * tj + 128 * (ot3 + 1)]
                        xsl = xld[:, 1024 * par + 128 * tj:
                                  1024 * par + 128 * (tj + 1)]
                        prog.do("vector", lambda e, d=dst, jp=jp, xsl=xsl:
                                e.tensor_add(d, ptrF[jp], xsl),
                                waits=[(st, mt2), xmk[tj]])
                        pc_v.append(sv.n)
                        if ot3 == 7:
                            row_mk.append(sv.n)
                    xadd_v.append(sv.n)
                    f3_t.append(st.n)
                m_outT_v = sv.n
                for tj in range(8):
                    odma.issue(lambda e, tj=tj, coff=coff, outT=outT:
                               e.dma_start(
                        out_d[coff + 128 * tj:coff + 128 * (tj + 1), :],
                        outT[:, TCH * tj:TCH * (tj + 1)]),
                        waits=[(sv, row_mk[tj])])
                if "outT" in taps and ch == 0:
                    tap_gate.append(wdma.issue(
                        lambda e, outT=outT: e.dma_start(taps["outT"], outT),
                        waits=[(sv, m_outT_v)]))
                out_done.append(odma.fin())

        prog.do("sync", None, waits=odma.fin() + sdma.fin())
        prog.do("gpsimd", None, waits=wdma.fin())

    with nc.Block() as block:
        prog.replay(block)
    stack.close()
    return nc


def host_layout(inputs):
    W = lambda a: np.ascontiguousarray(np.asarray(a), dtype=np.float32)
    o = {}
    o["wq4"] = W(np.asarray(inputs["L0_attn_Wq"]).reshape(8, 128, 8, 128)
                 .transpose(0, 3, 2, 1).reshape(8, 128, 1024))
    o["wk4"] = W(np.asarray(inputs["L0_attn_Wk"]).reshape(8, 128, 8, 128)
                 .transpose(0, 3, 2, 1).reshape(8, 128, 1024))
    o["wv4"] = W(np.asarray(inputs["L0_attn_Wv"]).reshape(8, 128, 8, 128)
                 .transpose(0, 3, 2, 1).reshape(8, 128, 1024))
    o["wo4"] = W(np.asarray(inputs["L0_attn_Wo"]).reshape(8, 128, 8, 128)
                 .transpose(2, 3, 0, 1).reshape(8, 128, 1024))
    o["w14"] = W(np.asarray(inputs["L0_ffn_w1_w"]).reshape(32, 128, 8, 128)
                 .transpose(0, 3, 2, 1).reshape(32, 128, 1024))
    o["w24"] = W(np.asarray(inputs["L0_ffn_w2_w"]).reshape(32, 128, 8, 128)
                 .transpose(0, 3, 2, 1).reshape(32, 128, 1024))
    o["h14"] = W(np.asarray(inputs["L0_ffn_h1_w"]).reshape(32, 128, 32, 128)
                 .transpose(0, 3, 2, 1).reshape(32, 128, 4096))
    o["h24"] = W(np.asarray(inputs["L0_ffn_h2_w"]).reshape(32, 128, 32, 128)
                 .transpose(0, 3, 2, 1).reshape(32, 128, 4096))
    o["w34"] = W(np.asarray(inputs["L0_ffn_w3_w"]).reshape(8, 128, 32, 128)
                 .transpose(0, 3, 2, 1).reshape(8, 128, 4096))
    o["s1c"] = W(np.asarray(inputs["L0_ffn_w1_s"]).reshape(32, 128).T)
    o["s2c"] = W(np.asarray(inputs["L0_ffn_w2_s"]).reshape(32, 128).T)
    o["h1sc"] = W(np.asarray(inputs["L0_ffn_h1_s"]).reshape(32, 128).T)
    o["h2sc"] = W(np.asarray(inputs["L0_ffn_h2_s"]).reshape(32, 128).T)
    o["w3sc"] = W(np.asarray(inputs["L0_ffn_w3_s"]).reshape(8, 128).T)
    g = np.asarray(inputs["L0_attn_g"])
    o["g128"] = W(np.concatenate([g, g]).reshape(128, 1))
    o["lamvec"] = W(np.concatenate([
        np.asarray(inputs["L0_attn_lambda_q1"]),
        np.asarray(inputs["L0_attn_lambda_k1"]),
        np.asarray(inputs["L0_attn_lambda_q2"]),
        np.asarray(inputs["L0_attn_lambda_k2"])]).reshape(1, 128))
    return o


def kernel(**inputs):
    shared = host_layout(inputs)
    x = np.ascontiguousarray(np.asarray(inputs["L0_attn_x"]), dtype=np.float32)
    nc = build()
    in_maps = []
    for c in range(8):
        b, h = c // 2, c % 2
        xb = x[b]
        xc = np.ascontiguousarray(np.concatenate(
            [xb[h * OWN:(h + 1) * OWN], xb[(1 - h) * OWN:(2 - h) * OWN]]))
        m = dict(shared)
        m["x"] = xc
        in_maps.append(m)
    res = run_bass_kernel_spmd(nc, in_maps, core_ids=list(range(8)))
    out = np.empty((4, SEQ, DM), dtype=np.float32)
    for c in range(8):
        b, h = c // 2, c % 2
        out[b, h * OWN:(h + 1) * OWN] = res.results[c]["out"]
    return out


# version 29
# speedup vs baseline: 1.0696x; 1.0696x over previous
import math
import contextlib
import numpy as np
import concourse.bass as bass
import concourse.mybir as mybir
from concourse.bass_utils import run_bass_kernel_spmd

F32 = mybir.dt.float32
BF16 = mybir.dt.bfloat16
AF = mybir.ActivationFunctionType
OP = mybir.AluOpType
AX = mybir.AxisListType.X

SEQ, OWN, DM = 4096, 2048, 1024
NG = 8
LAMBDA_INIT = 0.8 - 0.6 * math.exp(-0.3 * 12)
ONE_M_LI = 1.0 - LAMBDA_INIT
QK_SCALE = 32.0 ** -0.25
EPS = 1e-5
TCH = 1024
NCH = OWN // TCH

ENGINES = ("sync", "tensor", "scalar", "vector", "gpsimd")


class Sem:
    __slots__ = ("h", "n")

    def __init__(self, h):
        self.h = h
        self.n = 0


class Prog:

    CHAINED = ("vector", "scalar", "gpsimd")

    def __init__(self, chain_sems=None):
        self.q = {e: [] for e in ENGINES}
        self.last_wait = {e: {} for e in ENGINES}
        self.chain = chain_sems or {}

    def do(self, eng, fn, waits=(), incs=(), dma=False):
        cs = self.chain.get(eng)
        waits = list(waits)
        incs = list(incs)
        if cs is not None and fn is not None:
            waits.append((cs, cs.n))
            if not dma:
                incs.append((cs, 1))
        lw = self.last_wait[eng]
        ws = []
        for s, v in waits:
            if v <= 0:
                continue
            k = id(s)
            if lw.get(k, -1) >= v:
                continue
            lw[k] = v
            ws.append((s, v))
        self.q[eng].append((ws, fn, tuple(incs)))
        for s, k in incs:
            s.n += k

    def barrier(self, marks):
        for e in ENGINES:
            self.do(e, None, waits=list(marks))

    def replay(self, block):
        def mk(name):
            def body(eng):
                for ws, fn, incs in self.q[name]:
                    for s, v in ws:
                        eng.wait_ge(s.h, v)
                    if fn is None:
                        if incs:
                            ins = eng.nop()
                            for s, k in incs:
                                ins.then_inc(s.h, k)
                        continue
                    ins = fn(eng)
                    for s, k in incs:
                        ins.then_inc(s.h, k)
            return body
        block.sync(mk("sync"))
        block.tensor(mk("tensor"))
        block.scalar(mk("scalar"))
        block.vector(mk("vector"))
        block.gpsimd(mk("gpsimd"))


class Stream:

    def __init__(self, prog, eng, sems):
        self.prog = prog
        self.eng = eng
        self.sems = sems
        self.idx = 0

    def issue(self, fn, waits=()):
        s = self.sems[self.idx % 2]
        self.idx += 1
        w = list(waits) + [(s, s.n)]
        self.prog.do(self.eng, fn, waits=w, incs=[(s, 16)], dma=True)
        return (s, s.n)

    def fin(self):
        return [(s, s.n) for s in self.sems]


def build(tap=()):
    nc = bass.Bass("TRN2", target_bir_lowering=False)

    x_d = nc.dram_tensor("x", [SEQ, DM], F32, kind="ExternalInput").ap()
    wq_d = nc.dram_tensor("wq4", [8, 128, 1024], F32, kind="ExternalInput").ap()
    wk_d = nc.dram_tensor("wk4", [8, 128, 1024], F32, kind="ExternalInput").ap()
    wv_d = nc.dram_tensor("wv4", [8, 128, 1024], F32, kind="ExternalInput").ap()
    wo_d = nc.dram_tensor("wo4", [8, 128, 1024], F32, kind="ExternalInput").ap()
    w1_d = nc.dram_tensor("w14", [32, 128, 1024], F32, kind="ExternalInput").ap()
    w2_d = nc.dram_tensor("w24", [32, 128, 1024], F32, kind="ExternalInput").ap()
    h1_d = nc.dram_tensor("h14", [32, 128, 4096], F32, kind="ExternalInput").ap()
    h2_d = nc.dram_tensor("h24", [32, 128, 4096], F32, kind="ExternalInput").ap()
    w3_d = nc.dram_tensor("w34", [8, 128, 4096], F32, kind="ExternalInput").ap()
    s1_d = nc.dram_tensor("s1c", [128, 32], F32, kind="ExternalInput").ap()
    s2_d = nc.dram_tensor("s2c", [128, 32], F32, kind="ExternalInput").ap()
    h1s_d = nc.dram_tensor("h1sc", [128, 32], F32, kind="ExternalInput").ap()
    h2s_d = nc.dram_tensor("h2sc", [128, 32], F32, kind="ExternalInput").ap()
    w3s_d = nc.dram_tensor("w3sc", [128, 8], F32, kind="ExternalInput").ap()
    g_d = nc.dram_tensor("g128", [128, 1], F32, kind="ExternalInput").ap()
    lv_d = nc.dram_tensor("lamvec", [1, 128], F32, kind="ExternalInput").ap()
    out_d = nc.dram_tensor("out", [OWN, DM], F32, kind="ExternalOutput").ap()
    scr_d = nc.dram_tensor("attn_scr", [DM, OWN], F32, kind="Internal").ap()
    taps = {}
    for t in tap:
        shp = {"xn_own": [128, 16384], "xn_oth": [128, 16384],
               "attn_cat": [128, 16384], "attn_scr": [DM, OWN],
               "a1": [128, 32 * TCH], "kg": [128, 4096],
               "qg": [128, 2048], "vgt": [128, 32 * 130],
               "bias": [128, 2], "o0": [65, 512], "o1": [65, 512],
               "xnc": [128, 8 * TCH], "a2": [128, 32 * TCH],
               "outT": [128, 8 * TCH]}[t]
        taps[t] = nc.dram_tensor("tap_" + t, shp, F32, kind="ExternalOutput").ap()

    A = nc.alloc_sbuf_tensor
    identb = A("identb", [128, 128], BF16).ap()
    s1s = A("s1s", [128, 32], F32).ap()
    s2s = A("s2s", [128, 32], F32).ap()
    h1ss = A("h1ss", [128, 32], F32).ap()
    h2ss = A("h2ss", [128, 32], F32).ap()
    w3ss = A("w3ss", [128, 8], F32).ap()
    g128 = A("g128s", [128, 1], F32).ap()
    g128s = A("g128sc", [128, 1], F32).ap()
    lvs = A("lvs", [1, 128], F32).ap()
    epsb = A("epsb", [128, 1], F32).ap()
    ones64f = A("ones64f", [1, 64], BF16).ap()
    ones64c = A("ones64c", [64, 1], BF16).ap()
    ones128f = A("ones128f", [1, 128], F32).ap()
    maskp = A("maskp", [128, 1], BF16).ap()
    smll = A("smll", [1, 64], F32).ap()
    lamA = A("lamA", [1, 1], F32).ap()
    bias_gp = A("bias_gp", [128, 2], F32).ap()
    xn_own = A("xn_own", [128, 8 * OWN], BF16).ap()
    w12b = A("w12b", [128, 4 * 1024], BF16).ap()

    stack = contextlib.ExitStack()
    sem = {n: Sem(stack.enter_context(nc.semaphore(n))) for n in
           ("t", "a", "v", "g", "fin", "cv", "ca", "cg",
            "wd0", "wd1", "xd0", "xd1", "sw0", "sw1", "ow0", "ow1")}
    prog = Prog(chain_sems={"vector": sem["cv"], "scalar": sem["ca"],
                            "gpsimd": sem["cg"]})
    sem["v"], sem["a"], sem["g"] = sem["cv"], sem["ca"], sem["cg"]
    st, sa, sv, sg = sem["t"], sem["a"], sem["v"], sem["g"]
    wdma = Stream(prog, "gpsimd", [sem["wd0"], sem["wd1"]])
    xdma = Stream(prog, "sync", [sem["xd0"], sem["xd1"]])
    sdma = Stream(prog, "sync", [sem["sw0"], sem["sw1"]])
    odma = Stream(prog, "sync", [sem["ow0"], sem["ow1"]])

    for ap in (identb,):
        prog.do("gpsimd", lambda e, ap=ap: e.memset(ap, 0.0))
        prog.do("gpsimd", lambda e, ap=ap: e.affine_select(
            out=ap, in_=ap, compare_op=OP.not_equal, fill=1.0, base=0,
            pattern=[[-1, 128]], channel_multiplier=1))
    for ap, val in ((epsb, EPS), (ones64f, 1.0), (ones64c, 1.0),
                    (ones128f, 1.0), (maskp, 1.0)):
        prog.do("gpsimd", lambda e, ap=ap, val=val: e.memset(ap, val))
    prog.do("gpsimd", lambda e: e.nop())
    m_init = sg.n
    for dst, src in ((s1s, s1_d), (s2s, s2_d), (h1ss, h1s_d), (h2ss, h2s_d),
                     (w3ss, w3s_d), (g128, g_d), (lvs, lv_d)):
        wdma.issue(lambda e, d=dst, s=src: e.dma_start(d, s))
    prog.do("gpsimd", None, waits=wdma.fin())
    prog.do("gpsimd", lambda e: e.nop())
    m_consts = sg.n
    def emit_lam_g128():
        prog.do("scalar",
                lambda e: e.activation(g128s, g128, AF.Copy, scale=ONE_M_LI),
                waits=[(sg, m_consts)])
        mg = sa.n
        prog.do("vector", lambda e: e.tensor_mul(smll[:, 0:32], lvs[:, 0:32],
                                                 lvs[:, 32:64]),
                waits=[(sg, m_consts)])
        prog.do("vector", lambda e: e.tensor_mul(smll[:, 32:64], lvs[:, 64:96],
                                                 lvs[:, 96:128]))
        prog.do("vector", lambda e: e.reduce_sum(smll[:, 0:1], smll[:, 0:32],
                                                 axis=AX))
        prog.do("vector", lambda e: e.reduce_sum(smll[:, 1:2], smll[:, 32:64],
                                                 axis=AX))
        m1 = sv.n
        prog.do("scalar", lambda e: e.activation(smll[:, 2:3], smll[:, 0:1],
                                                 AF.Exp), waits=[(sv, m1)])
        prog.do("scalar", lambda e: e.activation(smll[:, 3:4], smll[:, 1:2],
                                                 AF.Exp))
        m2 = sa.n
        prog.do("vector", lambda e: e.tensor_sub(smll[:, 4:5], smll[:, 2:3],
                                                 smll[:, 3:4]), waits=[(sa, m2)])
        prog.do("vector", lambda e: e.tensor_single_scalar(
            lamA, smll[:, 4:5], LAMBDA_INIT, OP.add))
        return mg

    def emit_ln(bufs, row0, dest_writer, ln_v_marks, tr_a_marks, extra_tr_waits,
                trp, extra_dma_waits=()):
        xb, xc, sq, xnt, rr = bufs
        war = [(sv, ln_v_marks[-2])] if len(ln_v_marks) >= 2 else []
        war += list(extra_dma_waits)
        mk_x = xdma.issue(lambda e, xb=xb, row0=row0: e.dma_start(
            xb, x_d[row0:row0 + 128, :]), waits=war)
        prog.do("vector", lambda e, xb=xb, r=rr[0]: e.reduce_sum(r, xb, axis=AX),
                waits=[mk_x])
        prog.do("vector", lambda e, r=rr[1], s=rr[0]: e.tensor_single_scalar(
            r, s, -1.0 / DM, OP.mult))
        prog.do("vector", lambda e, xc=xc, xb=xb, r=rr[1]: e.tensor_scalar(
            xc, xb, r, None, OP.add))
        prog.do("vector", lambda e, sq=sq, xc=xc: e.tensor_mul(sq, xc, xc))
        prog.do("vector", lambda e, r=rr[2], sq=sq: e.reduce_sum(r, sq, axis=AX),
                )
        ln_v_marks.append(sv.n)
        mv = sv.n
        prog.do("scalar", lambda e, r=rr[2]: e.activation(
            r, r, AF.Ln, bias=epsb, scale=1.0 / DM),
            waits=[(sv, mv), (sg, m_init)])
        prog.do("scalar", lambda e, d=rr[3], s=rr[2]: e.activation(
            d, s, AF.Exp, scale=-0.5))
        ma = sa.n
        prog.do("vector", lambda e, d=xnt, xc=xc, r=rr[3]: e.tensor_scalar(
            d, xc, r, None, OP.mult), waits=[(sa, ma)])
        mn = sv.n
        ln_v_marks[-1] = sv.n
        for c in range(8):
            jp = c % 2
            war2 = [(sa, tr_a_marks[-2])] if len(tr_a_marks) >= 2 else []
            prog.do("tensor", lambda e, pt=trp[jp], xnt=xnt, c=c: e.transpose(
                pt, xnt[:, 128 * c:128 * (c + 1)], identb),
                waits=[(sv, mn), (sg, m_init)] + war2 + list(extra_tr_waits),
                incs=[(st, 1)])
            mt = st.n
            dst = dest_writer(c)
            prog.do("scalar", lambda e, d=dst, pt=trp[jp]: e.activation(
                d, pt, AF.Copy), waits=[(st, mt)])
            tr_a_marks.append(sa.n)

    if True:
        scA = stack.enter_context(contextlib.ExitStack())
        _sb = lambda n, s, d: scA.enter_context(nc.sbuf_tensor(n, s, d)).ap()
        xn_oth = _sb("xn_oth", [128, 8 * OWN], BF16)
        attn_cat = _sb("attn_cat", [128, 8 * OWN], BF16)
        wob_all = _sb("wob_all", [128, 8 * 1024], BF16)
        rows = _sb("rows", [1, 5 * 512], F32)
        rowsb = _sb("rowsb", [1, 2 * 512], BF16)
        wqb = [_sb(f"wqb{i}", [128, 1024], BF16) for i in range(2)]
        wkb = [_sb(f"wkb{i}", [128, 1024], BF16) for i in range(2)]
        wvb = [_sb(f"wvb{i}", [128, 1024], BF16) for i in range(2)]
        wotmp = [_sb(f"wotmp{i}", [128, 1024], BF16) for i in range(2)]
        psA = scA.enter_context(nc.psum_tensor("psA", [128, 1024], F32)).ap()
        psB = scA.enter_context(nc.psum_tensor("psB", [128, 1024], F32)).ap()
        psO = scA.enter_context(nc.psum_tensor("psO", [128, 1024], F32)).ap()
        psS = scA.enter_context(nc.psum_tensor("psS", [128, 1024], F32)).ap()
        S01 = [psA, psB]
        po0 = psO[0:65, 0:512]
        po1 = psO[0:65, 512:1024]
        pb = psO[0:128, 0:1]
        sp64 = psS[0:64, 0:512]
        sp1 = psS[0:1, 0:512]
        pwarm = psS[:, 512:1024]
        tr_psum = [psB[:, 0:64].bitcast(BF16), psB[:, 512:576].bitcast(BF16)]
        ppj = [psA[:, 0:512], psA[:, 512:1024]]

        grp_fetch = {}
        grp_lastuse = {}

        def fetch_group(g):
            war = []
            if (g - 2) in grp_lastuse:
                war = [(st, grp_lastuse[g - 2])]
            par = g % 2
            return (wdma.issue(lambda e, g=g: e.dma_start(wqb[par], wq_d[g]), war),
                    wdma.issue(lambda e, g=g: e.dma_start(wkb[par], wk_d[g]), war),
                    wdma.issue(lambda e, g=g: e.dma_start(wvb[par], wv_d[g]), war),
                    wdma.issue(lambda e, g=g: e.dma_start(wotmp[par], wo_d[g]), war))

        grp_fetch[0] = fetch_group(0)

        ln_v, tr_a = [], []
        sc0 = contextlib.ExitStack()
        with sc0:
            p0x = sc0.enter_context(nc.sbuf_tensor("ph0x", [128, 2048], F32)).ap()
            p0c = sc0.enter_context(nc.sbuf_tensor("ph0c", [128, 2048], F32)).ap()
            p0s = sc0.enter_context(nc.sbuf_tensor("ph0s", [128, 2048], F32)).ap()
            p0n = sc0.enter_context(nc.sbuf_tensor("ph0n", [128, 2048], BF16)).ap()
            p0r = sc0.enter_context(nc.sbuf_tensor("ph0r", [128, 8], F32)).ap()
            xn_tile_a = []
            for tt in range(32):
                par = tt % 2
                bufs = (p0x[:, 1024 * par:1024 * (par + 1)],
                        p0c[:, 1024 * par:1024 * (par + 1)],
                        p0s[:, 1024 * par:1024 * (par + 1)],
                        p0n[:, 1024 * par:1024 * (par + 1)],
                        [p0r[:, 4 * par + i:4 * par + i + 1] for i in range(4)])
                half = xn_own if tt < 16 else xn_oth
                tl = tt % 16

                def dw(c, half=half, tl=tl):
                    return half[:, OWN * c + 128 * tl:OWN * c + 128 * (tl + 1)]
                emit_ln(bufs, 128 * tt, dw, ln_v, tr_a, [], tr_psum)
                xn_tile_a.append(sa.n)
        m_xn_a = sa.n
        m_ph0_v = sv.n
        m_g128s = emit_lam_g128()
        if "xn_own" in taps:
            wdma.issue(lambda e: e.dma_start(taps["xn_own"], xn_own),
                       waits=[(sa, m_xn_a)])
        if "xn_oth" in taps:
            wdma.issue(lambda e: e.dma_start(taps["xn_oth"], xn_oth),
                       waits=[(sa, m_xn_a)])

        scG = scA.enter_context(contextlib.ExitStack())
        if True:
            kg = scG.enter_context(nc.sbuf_tensor("kg", [128, 4096], BF16)).ap()
            qg = scG.enter_context(nc.sbuf_tensor("qg", [128, 2048], BF16)).ap()
            vg = scG.enter_context(nc.sbuf_tensor("vg", [128, 4096], BF16)).ap()
            vgt = scG.enter_context(nc.sbuf_tensor("vgt", [128, 32 * 130], BF16)).ap()
            ksq = scG.enter_context(nc.sbuf_tensor("ksq", [128, 4096], BF16)).ap()
            e01 = [scG.enter_context(nc.sbuf_tensor(f"e01{i}", [128, 1024], BF16)).ap()
                   for i in range(3)]
            o0s = scG.enter_context(nc.sbuf_tensor("o0s", [65, 512], F32)).ap()
            o1s = scG.enter_context(nc.sbuf_tensor("o1s", [65, 512], F32)).ap()
            cs = scG.enter_context(nc.sbuf_tensor("cs", [64, 512], F32)).ap()
            t1s = scG.enter_context(nc.sbuf_tensor("t1s", [64, 512], F32)).ap()
            csq = scG.enter_context(nc.sbuf_tensor("csq", [64, 512], BF16)).ap()
            wosb = scG.enter_context(nc.sbuf_tensor("wosb", [128, 1024], F32)).ap()

            prog.do("gpsimd", lambda e: e.memset(vgt, 1.0),
                    waits=[(sa, m_xn_a), (sv, m_ph0_v)])
            m_vgt_ones = sg.n

            pending = []
            marks = {}
            cat_qv = {}
            mk_exp = {}
            mk_av = {}
            m_evac_prev = 0
            pj_a = []
            vt_v = []

            for g in range(NG):
                if g + 1 < NG:
                    grp_fetch[g + 1] = fetch_group(g + 1)
                par = g % 2
                mk_q, mk_k, mk_v, mk_o = grp_fetch[g]

                prog.do("vector", lambda e, g=g, w=wotmp[par]: e.tensor_scalar(
                    wob_all[:, 1024 * g:1024 * (g + 1)], w, g128s, None, OP.mult),
                    waits=[mk_o, (sa, m_g128s)])

                if g > 0:
                    prog.do("tensor", None, waits=[(st, grp_lastuse[g - 1])])

                def proj(wb, mk_w, nchunks, scale, dst):
                    for tc in range(nchunks):
                        pj = ppj[tc % 2]
                        src = xn_own if (nchunks == 4 or tc < 4) else xn_oth
                        toff = 512 * (tc if (nchunks == 4 or tc < 4) else tc - 4)
                        half_off = 0 if (nchunks == 4 or tc < 4) else 16
                        need_tile = half_off + (toff // 512) * 4 + 3
                        m_xn_need = xn_tile_a[need_tile]
                        war = [(sa, pj_a[-2])] if len(pj_a) >= 2 else []
                        w0 = []
                        if g > 0 and tc == 0:
                            w0 = [(sa, mk_exp.get(0, 0)), (sa, mk_exp.get(1, 0))]
                        for ic in range(8):
                            prog.do("tensor", lambda e, pj=pj, wb=wb, ic=ic,
                                    src=src, toff=toff: e.matmul(
                                        pj, wb[:, 128 * ic:128 * (ic + 1)],
                                        src[:, OWN * ic + toff:OWN * ic + toff + 512],
                                        start=(ic == 0), stop=(ic == 7)),
                                    waits=([mk_w, (sa, m_xn_need)] + war + w0)
                                    if ic == 0 else [],
                                    incs=[(st, 1)] if ic == 7 else [])
                        mt = st.n
                        dsl = dst[:, 512 * tc:512 * (tc + 1)]
                        if scale is None:
                            prog.do("scalar", lambda e, d=dsl, pj=pj:
                                    e.activation(d, pj, AF.Copy),
                                    waits=[(st, mt)])
                        else:
                            prog.do("scalar", lambda e, d=dsl, pj=pj, sc=scale:
                                    e.activation(d, pj, AF.Copy, scale=sc),
                                    waits=[(st, mt)])
                        pj_a.append(sa.n)
                    return sa.n

                m_k_a = proj(wkb[par], mk_k, 8, QK_SCALE, kg)
                m_v_a = proj(wvb[par], mk_v, 8, None, vg)
                m_q_a = proj(wqb[par], mk_q, 4, QK_SCALE, qg)

                for kt in range(32):
                    jp = kt % 2
                    war2 = [(sv, vt_v[-2])] if len(vt_v) >= 2 else []
                    prog.do("tensor", lambda e, pt=tr_psum[jp], kt=kt:
                            e.transpose(pt, vg[:, 128 * kt:128 * (kt + 1)],
                                        identb),
                            waits=[(sa, m_v_a), (sg, m_vgt_ones)] + war2,
                            incs=[(st, 1)])
                    mt = st.n
                    dst = vgt[:, 130 * kt:130 * kt + 130].rearrange(
                        "p (v d) -> p v d", v=2, d=65)[:, :, 0:64]
                    srcv = tr_psum[jp].rearrange("p (v d) -> p v d", v=2, d=64)
                    del jp
                    prog.do("vector", lambda e, d=dst, s=srcv: e.tensor_copy(d, s),
                            waits=[(st, mt)])
                    vt_v.append(sv.n)
                m_vgt_v = sv.n
                m_vt_t = st.n

                prog.do("vector", lambda e: e.tensor_mul(ksq, kg, kg),
                        waits=[(sa, m_k_a)])
                prog.do("vector", lambda e: e.tensor_mul(
                    vg[:, 0:2048], qg, qg),
                    waits=[(sa, m_q_a), (st, m_vt_t)])
                m_sq_v = sv.n

                ms_v = []
                scans = [(ksq, 0, 8, 8, 10, 0), (vg, 0, 4, 9, 11, 0),
                         (ksq, 0, 8, 13, 15, 1), (vg, 0, 4, 14, 16, 1)]
                steps = []
                for si, (srcap, base, nch, oc, tc_, p_) in enumerate(scans):
                    for c8 in range(nch):
                        steps.append((si, c8))
                order = []
                c8pos = [0] * 4
                while len(order) < len(steps):
                    for si in range(4):
                        nch = scans[si][2]
                        if c8pos[si] < nch:
                            order.append((si, c8pos[si]))
                            c8pos[si] += 1
                gi = 0
                for si, c8 in order:
                    srcap, base, nch, oc, tc_, p_ = scans[si]
                    pp = (psS[0:1, 0:512] if gi % 2 == 0
                          else psS[0:1, 512:1024])
                    gi += 1
                    war = [(sv, ms_v[-2])] if len(ms_v) >= 2 else []
                    prog.do("tensor", lambda e, pp=pp, c8=c8, p=p_,
                            base=base, srcap=srcap: e.matmul(
                                pp, maskp[64 * p:64 * (p + 1), :],
                                srcap[64 * p:64 * (p + 1),
                                      base + 512 * c8:base + 512 * (c8 + 1)],
                                start=True, stop=True),
                            waits=[(sv, m_sq_v),
                                   (sv, marks.get("c7", 0))] + war,
                            incs=[(st, 1)])
                    mt = st.n
                    if c8 == 0:
                        prog.do("vector", lambda e, pp=pp, oc=oc:
                                e.reduce_max(smll[:, oc:oc + 1], pp, axis=AX),
                                waits=[(st, mt)])
                    else:
                        prog.do("vector", lambda e, pp=pp, tc_=tc_:
                                e.reduce_max(smll[:, tc_:tc_ + 1], pp, axis=AX),
                                waits=[(st, mt)])
                        prog.do("vector", lambda e, oc=oc, tc_=tc_:
                                e.tensor_max(smll[:, oc:oc + 1],
                                             smll[:, oc:oc + 1],
                                             smll[:, tc_:tc_ + 1]))
                    ms_v.append(sv.n)
                prog.do("vector", lambda e: e.tensor_add(
                    smll[:, 11:12], smll[:, 8:9], smll[:, 9:10]))
                prog.do("vector", lambda e: e.tensor_single_scalar(
                    smll[:, 12:13], smll[:, 11:12], -0.5, OP.mult))
                prog.do("vector", lambda e: e.tensor_add(
                    smll[:, 17:18], smll[:, 13:14], smll[:, 14:15]))
                prog.do("vector", lambda e: e.tensor_single_scalar(
                    smll[:, 18:19], smll[:, 17:18], -0.5, OP.mult))
                m_mh_v = sv.n
                marks["stat_v"] = sv.n
                bias_a = {}
                for p_ in range(2):
                    col = 12 if p_ == 0 else 18
                    prog.do("tensor", lambda e, col=col: e.matmul(
                        pb, ones128f, smll[:, col:col + 1],
                        start=True, stop=True),
                        waits=[(sv, m_mh_v), (sv, m_evac_prev), (sg, m_init),
                               (sa, bias_a.get(0, 0))],
                        incs=[(st, 1)])
                    mt = st.n
                    prog.do("scalar", lambda e, p=p_: e.activation(
                        bias_gp[:, p:p + 1], pb, AF.Copy),
                        waits=[(st, mt)] + marks.pop("biastap", []))
                    bias_a[p_] = sa.n
                if "bias" in taps and g == 0:
                    mkb = sdma.issue(lambda e: e.dma_start(
                        taps["bias"], bias_gp), waits=[(sa, bias_a[1])])
                    marks["biastap"] = [mkb]

                def emit_warm():
                    prog.do("tensor", lambda e: e.matmul(
                        pwarm[:, 0:448], identb, xn_own[:, 0:448],
                        start=True, stop=True),
                        waits=[(sv, marks.get("stat_v", 0)), (sa, m_xn_a),
                               (sg, m_init)])

                for p in range(2):
                    m_bias_a = bias_a[1]

                    def emit_av(kt, eb, p=p):
                        vsl = vgt[:, 130 * kt + 65 * p:
                                  130 * kt + 65 * (p + 1)]
                        avw = [(sa, mk_e_rdy[eb])]
                        if kt == 0:
                            avw.append((sv, m_evac_prev))
                        prog.do("tensor", lambda e, vsl=vsl, eb=eb, kt=kt:
                                e.matmul(po0, vsl, e01[eb][:, 0:512],
                                         start=(kt == 0), stop=(kt == 31)),
                                waits=avw)
                        prog.do("tensor", lambda e, vsl=vsl, eb=eb, kt=kt:
                                e.matmul(po1, vsl, e01[eb][:, 512:1024],
                                         start=(kt == 0), stop=(kt == 31)),
                                incs=[(st, 1)])
                        mk_av[eb] = st.n

                    mk_e_rdy = {}
                    for gkt in range(130):
                        if gkt < 128:
                            qv, kt = divmod(gkt, 32)
                            if kt in (8, 10, 16, 18, 24, 26) and pending:
                                pending.pop(0)()
                            kp = gkt % 2
                            kp3 = gkt % 3
                            Sps = S01[kp]
                            emit_warm()
                            for e_i in range(2):
                                w0 = []
                                if e_i == 0:
                                    w0 = [(sa, mk_exp.get(kp, 0)),
                                          (sa, m_bias_a), (sv, m_vgt_v)]
                                tp = (64 * p + 32 * e_i, 0)
                                prog.do("tensor", lambda e, Sps=Sps, e_i=e_i,
                                        kt=kt, p=p, qv=qv, tp=tp: e.matmul(
                                            Sps[:, 512 * e_i:512 * (e_i + 1)],
                                            kg[64 * p + 32 * e_i:
                                               64 * p + 32 * (e_i + 1),
                                               128 * kt:128 * (kt + 1)],
                                            qg[64 * p + 32 * e_i:
                                               64 * p + 32 * (e_i + 1),
                                               512 * qv:512 * (qv + 1)],
                                            start=True, stop=True,
                                            tile_position=tp),
                                        waits=w0,
                                        incs=[(st, 1)] if e_i == 1 else [])
                            mS = st.n
                            prog.do("scalar", lambda e, kp3=kp3, Sps=Sps, p=p:
                                    e.activation(e01[kp3], Sps, AF.Exp,
                                                 bias=bias_gp[:, p:p + 1]),
                                    waits=[(st, mS), (st, mk_av.get(kp3, 0))],
                                    )
                            mk_exp[kp] = sa.n
                            mk_e_rdy[kp3] = sa.n
                        if gkt < 2:
                            continue
                        t_ = gkt - 2
                        tqv, tk = divmod(t_, 32)
                        emit_warm()
                        emit_av(tk, t_ % 3)
                        if tk != 31:
                            continue
                        m_av = st.n
                        grp_lastuse[g] = st.n
                        prev_c7 = marks.get("c7", 0)
                        qv = tqv

                        def mk_c1(m_av=m_av, g=g, p=p, qv=qv, prev_c7=prev_c7):
                            prog.do("vector", lambda e: e.tensor_copy(
                                o0s, po0),
                                waits=[(st, m_av), (sv, prev_c7)]
                                + marks.pop("tapwar", []))
                            prog.do("vector", lambda e: e.tensor_copy(
                                o1s, po1))
                            marks["evac"] = sv.n
                            ma = sv.n
                            if "o0" in taps and g == 0 and p == 0 and qv == 0:
                                mk1 = sdma.issue(lambda e: e.dma_start(
                                    taps["o0"], o0s), waits=[(sv, ma)])
                                mk2 = sdma.issue(lambda e: e.dma_start(
                                    taps["o1"], o1s), waits=[(sv, ma)])
                                marks["tapwar"] = [mk1, mk2]
                            prog.do("vector", lambda e: e.reciprocal(
                                rows[:, 512:1024], o1s[64:65, :]))
                            prog.do("vector", lambda e: e.tensor_copy(
                                rows[:, 2048:2560], o0s[64:65, :]))
                            prog.do("vector", lambda e: e.scalar_tensor_tensor(
                                rows[:, 1024:1536], rows[:, 2048:2560], lamA,
                                rows[:, 512:1024], OP.mult, OP.mult))
                            prog.do("vector", lambda e: e.tensor_copy(
                                rowsb[:, 0:512], rows[:, 1024:1536]))
                            prog.do("vector", lambda e: e.reciprocal(
                                rows[:, 0:512], o0s[64:65, :]))
                            marks["c1"] = sv.n

                        def mk_c2():
                            prog.do("tensor", lambda e: e.matmul(
                                sp64, ones64f, rowsb[:, 0:512],
                                start=True, stop=True),
                                waits=[(sv, marks["c1"]),
                                       (sv, marks.get("c7", 0)),
                                       (sv, marks.get("stat_v", 0))],
                                incs=[(st, 1)])
                            marks["c2"] = st.n

                        def mk_c3():
                            prog.do("vector", lambda e: e.tensor_mul(
                                t1s, o1s[0:64, :], sp64),
                                waits=[(st, marks["c2"])])
                            prog.do("vector", lambda e: e.tensor_sub(
                                cs, o0s[0:64, :], t1s))
                            prog.do("vector", lambda e: e.tensor_mul(csq, cs, cs),
                                    )
                            marks["c3"] = sv.n

                        def mk_c4():
                            prog.do("tensor", lambda e: e.matmul(
                                sp1, ones64c, csq, start=True, stop=True),
                                waits=[(sv, marks["c3"])], incs=[(st, 1)])
                            marks["c4"] = st.n

                        def mk_c5():
                            prog.do("vector", lambda e: e.tensor_mul(
                                rows[:, 1536:2048], sp1, rows[:, 0:512]),
                                waits=[(st, marks["c4"])])
                            prog.do("vector", lambda e: e.tensor_mul(
                                rows[:, 1536:2048], rows[:, 1536:2048],
                                rows[:, 0:512]))
                            mv5 = sv.n
                            prog.do("scalar", lambda e: e.activation(
                                rows[:, 1536:2048], rows[:, 1536:2048], AF.Ln,
                                bias=epsb[0:1, :], scale=1.0 / 64),
                                waits=[(sv, mv5)])
                            prog.do("scalar", lambda e: e.activation(
                                rows[:, 1536:2048], rows[:, 1536:2048], AF.Exp,
                                scale=-0.5))
                            ma5 = sa.n
                            prog.do("vector", lambda e: e.tensor_mul(
                                rows[:, 1536:2048], rows[:, 1536:2048],
                                rows[:, 0:512]),
                                waits=[(sa, ma5)])
                            prog.do("vector", lambda e: e.tensor_copy(
                                rowsb[:, 512:1024], rows[:, 1536:2048]))
                            marks["c5"] = sv.n

                        def mk_c6():
                            prog.do("tensor", lambda e: e.matmul(
                                sp64, ones64f, rowsb[:, 512:1024],
                                start=True, stop=True),
                                waits=[(sv, marks["c5"]), (sv, marks["c3"])],
                                incs=[(st, 1)])
                            marks["c6"] = st.n

                        def mk_c7(g=g, p=p, qv=qv):
                            dst = attn_cat[64 * p:64 * (p + 1),
                                           OWN * g + 512 * qv:
                                           OWN * g + 512 * (qv + 1)]
                            prog.do("vector", lambda e, dst=dst: e.tensor_mul(
                                dst, cs, sp64),
                                waits=[(st, marks["c6"])])
                            marks["c7"] = sv.n
                            cat_qv[qv] = sv.n

                        mk_c1()
                        m_evac_prev = marks["evac"]
                        pending.extend([mk_c2, mk_c3, mk_c4, mk_c5, mk_c6,
                                        mk_c7])
            m_exp_l = max(mk_exp.values())

            wo_a = []
            wo_dma = []
            for u, (qc, ot) in enumerate((qc, ot) for qc in range(4)
                                         for ot in range(8)):
                    emit_warm()
                    if u == 4:
                        while len(pending) > 2:
                            pending.pop(0)()
                    if u == 8:
                        while pending:
                            pending.pop(0)()
                        m_cat_v = sv.n
                    pj = ppj[u % 2]
                    war = [(sv, wo_a[-2])] if len(wo_a) >= 2 else []
                    for ic in range(8):
                        prog.do("tensor", lambda e, pj=pj, ic=ic, ot=ot, qc=qc:
                                e.matmul(
                                    pj,
                                    wob_all[:, 1024 * ic + 128 * ot:
                                            1024 * ic + 128 * (ot + 1)],
                                    attn_cat[:, OWN * ic + 512 * qc:
                                             OWN * ic + 512 * (qc + 1)],
                                    start=(ic == 0), stop=(ic == 7)),
                                waits=([(sv, cat_qv[qc]), (sa, m_exp_l)]
                                       + war) if ic == 0 else [],
                                incs=[(st, 1)] if ic == 7 else [])
                    mt = st.n
                    wslots = [wosb[:, 0:512], wosb[:, 512:1024],
                              wqb[0].bitcast(F32), wqb[1].bitcast(F32)]
                    wdst = wslots[u % 4]
                    aw = [(st, mt)]
                    if len(wo_dma) >= 4:
                        aw.append(wo_dma[-4])
                    prog.do("vector", lambda e, d=wdst, pj=pj: e.tensor_copy(
                        d, pj), waits=aw)
                    wo_a.append(sv.n)
                    ma = sv.n
                    mk = sdma.issue(lambda e, d=wdst, ot=ot, qc=qc: e.dma_start(
                        scr_d[128 * ot:128 * (ot + 1), 512 * qc:512 * (qc + 1)],
                        d), waits=[(sv, ma)])
                    wo_dma.append(mk)
            if "attn_cat" in taps:
                wdma.issue(lambda e: e.dma_start(taps["attn_cat"], attn_cat),
                           waits=[(sv, m_cat_v)])
            if "kg" in taps:
                wdma.issue(lambda e: e.dma_start(taps["kg"], kg),
                           waits=[(sv, m_cat_v)])
            if "qg" in taps:
                wdma.issue(lambda e: e.dma_start(taps["qg"], qg),
                           waits=[(sv, m_cat_v)])
            if "vgt" in taps:
                wdma.issue(lambda e: e.dma_start(taps["vgt"], vgt),
                           waits=[(sv, m_cat_v)])
            if "attn_scr" in taps:
                sdma.issue(lambda e: e.dma_start(taps["attn_scr"], scr_d),
                           waits=sdma.fin())

            w12_pre = {}
            for ot in range(2):
                par = ot
                w1sl = w12b[:, 2048 * par:2048 * par + 1024]
                w2sl = w12b[:, 2048 * par + 1024:2048 * (par + 1)]
                w12_pre[ot] = (
                    wdma.issue(lambda e, d=w1sl, ot=ot: e.dma_start(d, w1_d[ot])),
                    wdma.issue(lambda e, d=w2sl, ot=ot: e.dma_start(d, w2_d[ot])))

    scA.close()
    bar_marks = ([(st, st.n), (sv, sv.n), (sa, sa.n), (sg, sg.n)]
                 + wdma.fin() + xdma.fin() + sdma.fin() + odma.fin())
    for _e in ("sync", "scalar", "vector", "gpsimd"):
        prog.do(_e, None, waits=bar_marks)
    scr_fin = sdma.fin()

    with (
        nc.sbuf_tensor("a1", [128, 32 * TCH], BF16) as d1_,
        nc.sbuf_tensor("hbT", [128, 3 * 4096], BF16) as hb_top_,
        nc.sbuf_tensor("xldp", [128, 2 * 1024], F32) as xld_,
        nc.sbuf_tensor("identf", [128, 128], F32) as idf_,
        nc.psum_tensor("pM0", [128, 1024], F32) as e1_,
        nc.psum_tensor("pM1", [128, 1024], F32) as e2_,
        nc.psum_tensor("pM2", [128, 1024], F32) as e3_,
        nc.psum_tensor("pM3", [128, 1024], F32) as e4_,
    ):
        a1 = d1_.ap()
        hb = hb_top_.ap()
        xld = xld_.ap()
        identf = idf_.ap()
        pM = [e1_.ap(), e2_.ap(), e3_.ap(), e4_.ap()]
        ptrF = [pM[3][:, 0:128], pM[3][:, 512:640]]
        prog.do("gpsimd", lambda e: e.memset(identf, 0.0))
        prog.do("gpsimd", lambda e: e.affine_select(
            out=identf, in_=identf, compare_op=OP.not_equal, fill=1.0, base=0,
            pattern=[[-1, 128]], channel_multiplier=1))
        m_idf = sg.n

        w12_t, w12_a, w12_v = [], [], []
        h_t, h_a = [], []
        ring_t = []
        w3_t, w3_a, f3_v, f3_t, pc_v = [], [], [], [], []
        xadd_v = []
        out_done = []
        tap_gate = []

        for ch in range(NCH):
            coff = TCH * ch
            chunk_gate = []
            if ch > 0:
                chunk_gate = ([(st, w3_t[-1]), (sv, pc_v[-1]),
                               (sa, w3_a[-1])] + list(tap_gate))

            if "xnc" in taps and ch == 0:
                for icq in range(8):
                    tap_gate.append(wdma.issue(
                        lambda e, icq=icq: e.dma_start(
                            taps["xnc"][:, TCH * icq:TCH * (icq + 1)],
                            xn_own[:, OWN * icq + coff:
                                   OWN * icq + coff + TCH]), waits=[]))

            if True:
                with nc.sbuf_tensor(f"t0b{ch}", [128, 2 * 1024], F32) as t0b_:
                    t0b = t0b_.ap()
                    for ot in range(32):
                        par = ot % 2
                        w1sl = w12b[:, 2048 * par:2048 * par + 1024]
                        w2sl = w12b[:, 2048 * par + 1024:2048 * (par + 1)]
                        if ch == 0 and ot < 2:
                            mk1, mk2 = w12_pre[ot]
                        else:
                            war = ([(st, w12_t[-2])] if len(w12_t) >= 2
                                   else [])
                            mk1 = wdma.issue(lambda e, d=w1sl, ot=ot:
                                             e.dma_start(d, w1_d[ot]),
                                             waits=war)
                            mk2 = wdma.issue(lambda e, d=w2sl, ot=ot:
                                             e.dma_start(d, w2_d[ot]),
                                             waits=war)
                        pa = pM[0] if par == 0 else pM[2]
                        pb2 = pM[1] if par == 0 else pM[3]
                        war_a = [(sa, w12_a[-2])] if len(w12_a) >= 2 else []
                        war_v = [(sv, w12_v[-2])] if len(w12_v) >= 2 else []
                        w1w0, w1w1 = [], []
                        if ch == 0 and ot == 0:
                            w1w0 = [(sv, wo_a[-2])]
                            w1w1 = [(sv, wo_a[-1])]
                        elif ch > 0 and ot == 0:
                            w1w0 = [(sa, w3_a[-1])] + list(tap_gate)
                        elif ch > 0 and ot == 1:
                            w1w0 = [(sv, pc_v[-1])]
                        for ic in range(8):
                            for hc in range(2):
                                prog.do("tensor", lambda e, pa=pa, w=w1sl,
                                        ic=ic, hc=hc, coff=coff: e.matmul(
                                            pa[:, 512 * hc:512 * (hc + 1)],
                                            w[:, 128 * ic:128 * (ic + 1)],
                                            xn_own[:, OWN * ic + coff + 512 * hc:
                                                   OWN * ic + coff + 512 * (hc + 1)],
                                            start=(ic == 0), stop=(ic == 7)),
                                        waits=([mk1] + war_a + w1w0)
                                        if (ic == 0 and hc == 0) else
                                        (w1w1 if (ic == 0 and hc == 1) else []))
                        for ic in range(8):
                            for hc in range(2):
                                lastw = (ic == 7 and hc == 1)
                                prog.do("tensor", lambda e, pb2=pb2, w=w2sl,
                                        ic=ic, hc=hc, coff=coff: e.matmul(
                                            pb2[:, 512 * hc:512 * (hc + 1)],
                                            w[:, 128 * ic:128 * (ic + 1)],
                                            xn_own[:, OWN * ic + coff + 512 * hc:
                                                   OWN * ic + coff + 512 * (hc + 1)],
                                            start=(ic == 0), stop=(ic == 7)),
                                        waits=([mk2] + war_v)
                                        if (ic == 0 and hc == 0) else [],
                                        incs=[(st, 1)] if lastw else [])
                        w12_t.append(st.n)
                        mt = st.n
                        t0 = t0b[:, 1024 * par:1024 * (par + 1)]
                        prog.do("scalar", lambda e, t0=t0, pa=pa, ot=ot:
                                e.activation(t0, pa, AF.Silu,
                                             scale=s1s[:, ot:ot + 1]),
                                waits=[(st, mt)] + war_v)
                        w12_a.append(sa.n)
                        ma = sa.n
                        prog.do("vector", lambda e, pb2=pb2, t0=t0, ot=ot:
                                e.scalar_tensor_tensor(
                                    a1[:, TCH * ot:TCH * (ot + 1)], pb2,
                                    s2s[:, ot:ot + 1], t0, OP.mult, OP.mult),
                                waits=[(sa, ma)])
                        w12_v.append(sv.n)
                    m_a1_v = sv.n
                a1_tap_mk = []
                if "a1" in taps and ch == 0:
                    a1_tap_mk.append(wdma.issue(
                        lambda e: e.dma_start(taps["a1"], a1),
                        waits=[(sv, m_a1_v)]))

            with (
                nc.sbuf_tensor(f"a2{ch}", [128, 32 * TCH], BF16) as i1_,
            ):
                a2 = i1_.ap()

                def hstage(hd, srcap, dstap, ssc, src_ready, extra0=()):
                    for ot in range(32):
                        par = ot % 2
                        sl3 = len(ring_t) % 3
                        wsl = hb[:, 4096 * sl3:4096 * (sl3 + 1)]
                        war = [(st, ring_t[-3])] if len(ring_t) >= 3 else []
                        mk = wdma.issue(lambda e, d=wsl, ot=ot, hd=hd:
                                        e.dma_start(d, hd[ot]), waits=war)
                        pp = pM[par]
                        war_a = [(sa, h_a[-2])] if len(h_a) >= 2 else []
                        if not h_a:
                            war_a = [(sv, m_a1_v)]
                        if ot == 0:
                            war_a = war_a + list(extra0)
                        for ic in range(32):
                            for hc in range(2):
                                lastw = (ic == 31 and hc == 1)
                                prog.do("tensor", lambda e, pp=pp, w=wsl,
                                        ic=ic, hc=hc, srcap=srcap: e.matmul(
                                            pp[:, 512 * hc:512 * (hc + 1)],
                                            w[:, 128 * ic:128 * (ic + 1)],
                                            srcap[:, TCH * ic + 512 * hc:
                                                  TCH * ic + 512 * (hc + 1)],
                                            start=(ic == 0), stop=(ic == 31)),
                                        waits=([mk, src_ready] + war_a)
                                        if (ic == 0 and hc == 0) else [],
                                        incs=[(st, 1)] if lastw else [])
                        h_t.append(st.n)
                        ring_t.append(st.n)
                        mt = st.n
                        prog.do("scalar", lambda e, pp=pp, ot=ot, dstap=dstap,
                                ssc=ssc: e.activation(
                                    dstap[:, TCH * ot:TCH * (ot + 1)], pp,
                                    AF.Gelu, scale=ssc[:, ot:ot + 1]),
                                waits=[(st, mt)])
                        h_a.append(sa.n)
                    return sa.n

                m_a2_a = hstage(h1_d, a1, a2, h1ss, (sv, m_a1_v),
                                extra0=((list(out_done[-1])
                                         + [(sv, pc_v[-1])])
                                        if out_done else ()))
                if "a2" in taps and ch == 0:
                    tap_gate.append(wdma.issue(
                        lambda e, a2=a2: e.dma_start(taps["a2"], a2),
                        waits=[(sa, m_a2_a)]))
                m_a3_a = hstage(h2_d, a2, a1, h2ss, (sa, m_a2_a),
                                extra0=a1_tap_mk)

            with (
                nc.sbuf_tensor(f"f3s{ch}", [128, 2 * 1024], F32) as j2_,
                nc.sbuf_tensor(f"asld{ch}", [128, 2 * 1024], F32) as j3_,
                nc.sbuf_tensor(f"outT{ch}", [128, 8 * 1024], F32) as j5_,
            ):
                f3sb, asld, outT = (j2_.ap(), j3_.ap(), j5_.ap())
                row_mk = []
                for ot3 in range(8):
                    par = ot3 % 2
                    xwar = ([(sv, xadd_v[-2])] if len(xadd_v) >= 2 else [])
                    if ot3 < 2 and out_done:
                        xwar = xwar + list(out_done[-1])
                    xmk = []
                    for tj in range(8):
                        row0 = coff + 128 * tj
                        xmk.append(xdma.issue(
                            lambda e, tj=tj, row0=row0, par=par, ot3=ot3:
                            e.dma_start(
                                xld[:, 1024 * par + 128 * tj:
                                    1024 * par + 128 * (tj + 1)],
                                x_d[row0:row0 + 128,
                                    128 * ot3:128 * (ot3 + 1)]), waits=xwar))
                    sl3 = len(ring_t) % 3
                    wsl = hb[:, 4096 * sl3:4096 * (sl3 + 1)]
                    war = [(st, ring_t[-3])] if len(ring_t) >= 3 else []
                    mk = wdma.issue(lambda e, d=wsl, ot3=ot3: e.dma_start(
                        d, w3_d[ot3]), waits=war)
                    mka = sdma.issue(lambda e, ot3=ot3, par=par, coff=coff,
                                     asld=asld: e.dma_start(
                                         asld[:, 1024 * par:1024 * (par + 1)],
                                         scr_d[128 * ot3:128 * (ot3 + 1),
                                               coff:coff + TCH]),
                                     waits=scr_fin + (chunk_gate +
                                      [(sa, m_a3_a)] if ot3 < 2 else []) +
                                     ([(sv, f3_v[-2])] if len(f3_v) >= 2
                                      else [(sa, m_a3_a)]))
                    pp = pM[par]
                    war_a = [(sa, w3_a[-2])] if len(w3_a) >= 2 else []
                    for ic in range(32):
                        for hc in range(2):
                            lastw = (ic == 31 and hc == 1)
                            prog.do("tensor", lambda e, pp=pp, w=wsl, ic=ic,
                                    hc=hc: e.matmul(
                                        pp[:, 512 * hc:512 * (hc + 1)],
                                        w[:, 128 * ic:128 * (ic + 1)],
                                        a1[:, TCH * ic + 512 * hc:
                                            TCH * ic + 512 * (hc + 1)],
                                        start=(ic == 0), stop=(ic == 31)),
                                    waits=([mk, (sa, m_a3_a)] + war_a)
                                    if (ic == 0 and hc == 0) else [],
                                    incs=[(st, 1)] if lastw else [])
                    w3_t.append(st.n)
                    ring_t.append(st.n)
                    mt = st.n
                    f3 = f3sb[:, 1024 * par:1024 * (par + 1)]
                    fw = [(st, mt)]
                    if len(f3_t) >= 2:
                        fw.append((st, f3_t[-2]))
                    if ot3 < 2 and chunk_gate:
                        fw += chunk_gate
                    prog.do("scalar", lambda e, f3=f3, pp=pp, ot3=ot3:
                            e.activation(f3, pp, AF.Copy,
                                         scale=w3ss[:, ot3:ot3 + 1]),
                            waits=fw)
                    w3_a.append(sa.n)
                    ma = sa.n
                    prog.do("vector", lambda e, f3=f3, par=par, asld=asld:
                            e.tensor_add(
                        f3, f3, asld[:, 1024 * par:1024 * (par + 1)]),
                        waits=[(sa, ma), mka, (sv, m_a1_v)])
                    f3_v.append(sv.n)
                    mfv = sv.n
                    for tj in range(8):
                        jp = tj % 2
                        war2 = [(sv, pc_v[-2])] if len(pc_v) >= 2 else []
                        if not pc_v:
                            war2 = [(sg, m_idf)]
                        prog.do("tensor", lambda e, jp=jp, f3=f3, tj=tj:
                                e.transpose(ptrF[jp],
                                            f3[:, 128 * tj:128 * (tj + 1)],
                                            identf),
                                waits=[(sv, mfv)] + war2, incs=[(st, 1)])
                        mt2 = st.n
                        dst = outT[:, TCH * tj + 128 * ot3:
                                   TCH * tj + 128 * (ot3 + 1)]
                        xsl = xld[:, 1024 * par + 128 * tj:
                                  1024 * par + 128 * (tj + 1)]
                        prog.do("vector", lambda e, d=dst, jp=jp, xsl=xsl:
                                e.tensor_add(d, ptrF[jp], xsl),
                                waits=[(st, mt2), xmk[tj]])
                        pc_v.append(sv.n)
                        if ot3 == 7:
                            row_mk.append(sv.n)
                    xadd_v.append(sv.n)
                    f3_t.append(st.n)
                m_outT_v = sv.n
                for tj in range(8):
                    odma.issue(lambda e, tj=tj, coff=coff, outT=outT:
                               e.dma_start(
                        out_d[coff + 128 * tj:coff + 128 * (tj + 1), :],
                        outT[:, TCH * tj:TCH * (tj + 1)]),
                        waits=[(sv, row_mk[tj])])
                if "outT" in taps and ch == 0:
                    tap_gate.append(wdma.issue(
                        lambda e, outT=outT: e.dma_start(taps["outT"], outT),
                        waits=[(sv, m_outT_v)]))
                out_done.append(odma.fin())

        prog.do("sync", None, waits=odma.fin() + sdma.fin())
        prog.do("gpsimd", None, waits=wdma.fin())

    with nc.Block() as block:
        prog.replay(block)
    stack.close()
    return nc


def host_layout(inputs):
    W = lambda a: np.ascontiguousarray(np.asarray(a), dtype=np.float32)
    o = {}
    o["wq4"] = W(np.asarray(inputs["L0_attn_Wq"]).reshape(8, 128, 8, 128)
                 .transpose(0, 3, 2, 1).reshape(8, 128, 1024))
    o["wk4"] = W(np.asarray(inputs["L0_attn_Wk"]).reshape(8, 128, 8, 128)
                 .transpose(0, 3, 2, 1).reshape(8, 128, 1024))
    o["wv4"] = W(np.asarray(inputs["L0_attn_Wv"]).reshape(8, 128, 8, 128)
                 .transpose(0, 3, 2, 1).reshape(8, 128, 1024))
    o["wo4"] = W(np.asarray(inputs["L0_attn_Wo"]).reshape(8, 128, 8, 128)
                 .transpose(2, 3, 0, 1).reshape(8, 128, 1024))
    o["w14"] = W(np.asarray(inputs["L0_ffn_w1_w"]).reshape(32, 128, 8, 128)
                 .transpose(0, 3, 2, 1).reshape(32, 128, 1024))
    o["w24"] = W(np.asarray(inputs["L0_ffn_w2_w"]).reshape(32, 128, 8, 128)
                 .transpose(0, 3, 2, 1).reshape(32, 128, 1024))
    o["h14"] = W(np.asarray(inputs["L0_ffn_h1_w"]).reshape(32, 128, 32, 128)
                 .transpose(0, 3, 2, 1).reshape(32, 128, 4096))
    o["h24"] = W(np.asarray(inputs["L0_ffn_h2_w"]).reshape(32, 128, 32, 128)
                 .transpose(0, 3, 2, 1).reshape(32, 128, 4096))
    o["w34"] = W(np.asarray(inputs["L0_ffn_w3_w"]).reshape(8, 128, 32, 128)
                 .transpose(0, 3, 2, 1).reshape(8, 128, 4096))
    o["s1c"] = W(np.asarray(inputs["L0_ffn_w1_s"]).reshape(32, 128).T)
    o["s2c"] = W(np.asarray(inputs["L0_ffn_w2_s"]).reshape(32, 128).T)
    o["h1sc"] = W(np.asarray(inputs["L0_ffn_h1_s"]).reshape(32, 128).T)
    o["h2sc"] = W(np.asarray(inputs["L0_ffn_h2_s"]).reshape(32, 128).T)
    o["w3sc"] = W(np.asarray(inputs["L0_ffn_w3_s"]).reshape(8, 128).T)
    g = np.asarray(inputs["L0_attn_g"])
    o["g128"] = W(np.concatenate([g, g]).reshape(128, 1))
    o["lamvec"] = W(np.concatenate([
        np.asarray(inputs["L0_attn_lambda_q1"]),
        np.asarray(inputs["L0_attn_lambda_k1"]),
        np.asarray(inputs["L0_attn_lambda_q2"]),
        np.asarray(inputs["L0_attn_lambda_k2"])]).reshape(1, 128))
    return o


def kernel(**inputs):
    shared = host_layout(inputs)
    x = np.ascontiguousarray(np.asarray(inputs["L0_attn_x"]), dtype=np.float32)
    nc = build()
    in_maps = []
    for c in range(8):
        b, h = c // 2, c % 2
        xb = x[b]
        xc = np.ascontiguousarray(np.concatenate(
            [xb[h * OWN:(h + 1) * OWN], xb[(1 - h) * OWN:(2 - h) * OWN]]))
        m = dict(shared)
        m["x"] = xc
        in_maps.append(m)
    res = run_bass_kernel_spmd(nc, in_maps, core_ids=list(range(8)))
    out = np.empty((4, SEQ, DM), dtype=np.float32)
    for c in range(8):
        b, h = c // 2, c % 2
        out[b, h * OWN:(h + 1) * OWN] = res.results[c]["out"]
    return out


# version 31
# speedup vs baseline: 1.0777x; 1.0076x over previous
import math
import contextlib
import numpy as np
import concourse.bass as bass
import concourse.mybir as mybir
from concourse.bass_utils import run_bass_kernel_spmd

F32 = mybir.dt.float32
BF16 = mybir.dt.bfloat16
AF = mybir.ActivationFunctionType
OP = mybir.AluOpType
AX = mybir.AxisListType.X

SEQ, OWN, DM = 4096, 2048, 1024
NG = 8
LAMBDA_INIT = 0.8 - 0.6 * math.exp(-0.3 * 12)
ONE_M_LI = 1.0 - LAMBDA_INIT
QK_SCALE = 32.0 ** -0.25
EPS = 1e-5
TCH = 1024
NCH = OWN // TCH

ENGINES = ("sync", "tensor", "scalar", "vector", "gpsimd")


class Sem:
    __slots__ = ("h", "n")

    def __init__(self, h):
        self.h = h
        self.n = 0


class Prog:

    CHAINED = ("vector", "scalar", "gpsimd")

    def __init__(self, chain_sems=None):
        self.q = {e: [] for e in ENGINES}
        self.last_wait = {e: {} for e in ENGINES}
        self.chain = chain_sems or {}

    def do(self, eng, fn, waits=(), incs=(), dma=False):
        cs = self.chain.get(eng)
        waits = list(waits)
        incs = list(incs)
        if cs is not None and fn is not None:
            waits.append((cs, cs.n))
            if not dma:
                incs.append((cs, 1))
        lw = self.last_wait[eng]
        ws = []
        for s, v in waits:
            if v <= 0:
                continue
            k = id(s)
            if lw.get(k, -1) >= v:
                continue
            lw[k] = v
            ws.append((s, v))
        self.q[eng].append((ws, fn, tuple(incs)))
        for s, k in incs:
            s.n += k

    def barrier(self, marks):
        for e in ENGINES:
            self.do(e, None, waits=list(marks))

    def replay(self, block):
        def mk(name):
            def body(eng):
                for ws, fn, incs in self.q[name]:
                    for s, v in ws:
                        eng.wait_ge(s.h, v)
                    if fn is None:
                        if incs:
                            ins = eng.nop()
                            for s, k in incs:
                                ins.then_inc(s.h, k)
                        continue
                    ins = fn(eng)
                    for s, k in incs:
                        ins.then_inc(s.h, k)
            return body
        block.sync(mk("sync"))
        block.tensor(mk("tensor"))
        block.scalar(mk("scalar"))
        block.vector(mk("vector"))
        block.gpsimd(mk("gpsimd"))


class Stream:

    def __init__(self, prog, eng, sems):
        self.prog = prog
        self.eng = eng
        self.sems = sems
        self.idx = 0

    def issue(self, fn, waits=()):
        s = self.sems[self.idx % 2]
        self.idx += 1
        w = list(waits) + [(s, s.n)]
        self.prog.do(self.eng, fn, waits=w, incs=[(s, 16)], dma=True)
        return (s, s.n)

    def fin(self):
        return [(s, s.n) for s in self.sems]


def build(tap=()):
    nc = bass.Bass("TRN2", target_bir_lowering=False)

    x_d = nc.dram_tensor("x", [SEQ, DM], F32, kind="ExternalInput").ap()
    wq_d = nc.dram_tensor("wq4", [8, 128, 1024], F32, kind="ExternalInput").ap()
    wk_d = nc.dram_tensor("wk4", [8, 128, 1024], F32, kind="ExternalInput").ap()
    wv_d = nc.dram_tensor("wv4", [8, 128, 1024], F32, kind="ExternalInput").ap()
    wo_d = nc.dram_tensor("wo4", [8, 128, 1024], F32, kind="ExternalInput").ap()
    w1_d = nc.dram_tensor("w14", [32, 128, 1024], F32, kind="ExternalInput").ap()
    w2_d = nc.dram_tensor("w24", [32, 128, 1024], F32, kind="ExternalInput").ap()
    h1_d = nc.dram_tensor("h14", [32, 128, 4096], F32, kind="ExternalInput").ap()
    h2_d = nc.dram_tensor("h24", [32, 128, 4096], F32, kind="ExternalInput").ap()
    w3_d = nc.dram_tensor("w34", [8, 128, 4096], F32, kind="ExternalInput").ap()
    s1_d = nc.dram_tensor("s1c", [128, 32], F32, kind="ExternalInput").ap()
    s2_d = nc.dram_tensor("s2c", [128, 32], F32, kind="ExternalInput").ap()
    h1s_d = nc.dram_tensor("h1sc", [128, 32], F32, kind="ExternalInput").ap()
    h2s_d = nc.dram_tensor("h2sc", [128, 32], F32, kind="ExternalInput").ap()
    w3s_d = nc.dram_tensor("w3sc", [128, 8], F32, kind="ExternalInput").ap()
    g_d = nc.dram_tensor("g128", [128, 1], F32, kind="ExternalInput").ap()
    lv_d = nc.dram_tensor("lamvec", [1, 128], F32, kind="ExternalInput").ap()
    out_d = nc.dram_tensor("out", [OWN, DM], F32, kind="ExternalOutput").ap()
    scr_d = nc.dram_tensor("attn_scr", [DM, OWN], F32, kind="Internal").ap()
    taps = {}
    for t in tap:
        shp = {"xn_own": [128, 16384], "xn_oth": [128, 16384],
               "attn_cat": [128, 16384], "attn_scr": [DM, OWN],
               "a1": [128, 32 * TCH], "kg": [128, 4096],
               "qg": [128, 2048], "vgt": [128, 32 * 130],
               "bias": [128, 2], "o0": [65, 512], "o1": [65, 512],
               "xnc": [128, 8 * TCH], "a2": [128, 32 * TCH],
               "outT": [128, 8 * TCH]}[t]
        taps[t] = nc.dram_tensor("tap_" + t, shp, F32, kind="ExternalOutput").ap()

    A = nc.alloc_sbuf_tensor
    identb = A("identb", [128, 128], BF16).ap()
    s1s = A("s1s", [128, 32], F32).ap()
    s2s = A("s2s", [128, 32], F32).ap()
    h1ss = A("h1ss", [128, 32], F32).ap()
    h2ss = A("h2ss", [128, 32], F32).ap()
    w3ss = A("w3ss", [128, 8], F32).ap()
    g128 = A("g128s", [128, 1], F32).ap()
    g128s = A("g128sc", [128, 1], F32).ap()
    lvs = A("lvs", [1, 128], F32).ap()
    epsb = A("epsb", [128, 1], F32).ap()
    ones64f = A("ones64f", [1, 64], BF16).ap()
    ones64c = A("ones64c", [64, 1], BF16).ap()
    ones128f = A("ones128f", [1, 128], F32).ap()
    maskp = A("maskp", [128, 1], BF16).ap()
    smll = A("smll", [1, 64], F32).ap()
    lamA = A("lamA", [1, 1], F32).ap()
    bias_gp = A("bias_gp", [128, 2], F32).ap()
    xn_own = A("xn_own", [128, 8 * OWN], BF16).ap()
    w12b = A("w12b", [128, 4 * 1024], BF16).ap()

    stack = contextlib.ExitStack()
    sem = {n: Sem(stack.enter_context(nc.semaphore(n))) for n in
           ("t", "a", "v", "g", "fin", "cv", "ca", "cg",
            "wd0", "wd1", "xd0", "xd1", "sw0", "sw1", "ow0", "ow1")}
    prog = Prog(chain_sems={"vector": sem["cv"], "scalar": sem["ca"],
                            "gpsimd": sem["cg"]})
    sem["v"], sem["a"], sem["g"] = sem["cv"], sem["ca"], sem["cg"]
    st, sa, sv, sg = sem["t"], sem["a"], sem["v"], sem["g"]
    wdma = Stream(prog, "gpsimd", [sem["wd0"], sem["wd1"]])
    xdma = Stream(prog, "sync", [sem["xd0"], sem["xd1"]])
    sdma = Stream(prog, "sync", [sem["sw0"], sem["sw1"]])
    odma = Stream(prog, "sync", [sem["ow0"], sem["ow1"]])

    for ap in (identb,):
        prog.do("gpsimd", lambda e, ap=ap: e.memset(ap, 0.0))
        prog.do("gpsimd", lambda e, ap=ap: e.affine_select(
            out=ap, in_=ap, compare_op=OP.not_equal, fill=1.0, base=0,
            pattern=[[-1, 128]], channel_multiplier=1))
    for ap, val in ((epsb, EPS), (ones64f, 1.0), (ones64c, 1.0),
                    (ones128f, 1.0), (maskp, 1.0)):
        prog.do("gpsimd", lambda e, ap=ap, val=val: e.memset(ap, val))
    prog.do("gpsimd", lambda e: e.nop())
    m_init = sg.n
    for dst, src in ((s1s, s1_d), (s2s, s2_d), (h1ss, h1s_d), (h2ss, h2s_d),
                     (w3ss, w3s_d), (g128, g_d), (lvs, lv_d)):
        wdma.issue(lambda e, d=dst, s=src: e.dma_start(d, s))
    prog.do("gpsimd", None, waits=wdma.fin())
    prog.do("gpsimd", lambda e: e.nop())
    m_consts = sg.n
    def emit_lam_g128():
        prog.do("scalar",
                lambda e: e.activation(g128s, g128, AF.Copy, scale=ONE_M_LI),
                waits=[(sg, m_consts)])
        mg = sa.n
        prog.do("vector", lambda e: e.tensor_mul(smll[:, 0:32], lvs[:, 0:32],
                                                 lvs[:, 32:64]),
                waits=[(sg, m_consts)])
        prog.do("vector", lambda e: e.tensor_mul(smll[:, 32:64], lvs[:, 64:96],
                                                 lvs[:, 96:128]))
        prog.do("vector", lambda e: e.reduce_sum(smll[:, 0:1], smll[:, 0:32],
                                                 axis=AX))
        prog.do("vector", lambda e: e.reduce_sum(smll[:, 1:2], smll[:, 32:64],
                                                 axis=AX))
        m1 = sv.n
        prog.do("scalar", lambda e: e.activation(smll[:, 2:3], smll[:, 0:1],
                                                 AF.Exp), waits=[(sv, m1)])
        prog.do("scalar", lambda e: e.activation(smll[:, 3:4], smll[:, 1:2],
                                                 AF.Exp))
        m2 = sa.n
        prog.do("vector", lambda e: e.tensor_sub(smll[:, 4:5], smll[:, 2:3],
                                                 smll[:, 3:4]), waits=[(sa, m2)])
        prog.do("vector", lambda e: e.tensor_single_scalar(
            lamA, smll[:, 4:5], LAMBDA_INIT, OP.add))
        return mg

    def emit_ln(bufs, row0, dest_writer, ln_v_marks, tr_a_marks, extra_tr_waits,
                trp, extra_dma_waits=()):
        xb, xc, sq, xnt, rr = bufs
        war = [(sv, ln_v_marks[-2])] if len(ln_v_marks) >= 2 else []
        war += list(extra_dma_waits)
        mk_x = xdma.issue(lambda e, xb=xb, row0=row0: e.dma_start(
            xb, x_d[row0:row0 + 128, :]), waits=war)
        prog.do("vector", lambda e, xb=xb, r=rr[0]: e.reduce_sum(r, xb, axis=AX),
                waits=[mk_x])
        prog.do("vector", lambda e, r=rr[1], s=rr[0]: e.tensor_single_scalar(
            r, s, -1.0 / DM, OP.mult))
        prog.do("vector", lambda e, xc=xc, xb=xb, r=rr[1]: e.tensor_scalar(
            xc, xb, r, None, OP.add))
        prog.do("vector", lambda e, sq=sq, xc=xc: e.tensor_mul(sq, xc, xc))
        prog.do("vector", lambda e, r=rr[2], sq=sq: e.reduce_sum(r, sq, axis=AX),
                )
        ln_v_marks.append(sv.n)
        mv = sv.n
        prog.do("scalar", lambda e, r=rr[2]: e.activation(
            r, r, AF.Ln, bias=epsb, scale=1.0 / DM),
            waits=[(sv, mv), (sg, m_init)])
        prog.do("scalar", lambda e, d=rr[3], s=rr[2]: e.activation(
            d, s, AF.Exp, scale=-0.5))
        ma = sa.n
        prog.do("vector", lambda e, d=xnt, xc=xc, r=rr[3]: e.tensor_scalar(
            d, xc, r, None, OP.mult), waits=[(sa, ma)])
        mn = sv.n
        ln_v_marks[-1] = sv.n
        for c in range(8):
            jp = c % 2
            war2 = [(sa, tr_a_marks[-2])] if len(tr_a_marks) >= 2 else []
            prog.do("tensor", lambda e, pt=trp[jp], xnt=xnt, c=c: e.transpose(
                pt, xnt[:, 128 * c:128 * (c + 1)], identb),
                waits=[(sv, mn), (sg, m_init)] + war2 + list(extra_tr_waits),
                incs=[(st, 1)])
            mt = st.n
            dst = dest_writer(c)
            prog.do("scalar", lambda e, d=dst, pt=trp[jp]: e.activation(
                d, pt, AF.Copy), waits=[(st, mt)])
            tr_a_marks.append(sa.n)

    if True:
        scA = stack.enter_context(contextlib.ExitStack())
        _sb = lambda n, s, d: scA.enter_context(nc.sbuf_tensor(n, s, d)).ap()
        xn_oth = _sb("xn_oth", [128, 8 * OWN], BF16)
        attn_cat = _sb("attn_cat", [128, 8 * OWN], BF16)
        wob_all = _sb("wob_all", [128, 8 * 1024], BF16)
        rows = _sb("rows", [1, 5 * 512], F32)
        rowsb = _sb("rowsb", [1, 2 * 512], BF16)
        wqb = [_sb(f"wqb{i}", [128, 1024], BF16) for i in range(2)]
        wkb = [_sb(f"wkb{i}", [128, 1024], BF16) for i in range(2)]
        wvb = [_sb(f"wvb{i}", [128, 1024], BF16) for i in range(2)]
        wotmp = [_sb(f"wotmp{i}", [128, 1024], BF16) for i in range(2)]
        psA = scA.enter_context(nc.psum_tensor("psA", [128, 1024], F32)).ap()
        psB = scA.enter_context(nc.psum_tensor("psB", [128, 1024], F32)).ap()
        psO = scA.enter_context(nc.psum_tensor("psO", [128, 1024], F32)).ap()
        psS = scA.enter_context(nc.psum_tensor("psS", [128, 1024], F32)).ap()
        S01 = [psA, psB]
        po0 = psO[0:65, 0:512]
        po1 = psO[0:65, 512:1024]
        pb = psO[0:128, 0:1]
        sp64 = psS[0:64, 0:512]
        sp1 = psS[0:1, 0:512]
        pwarm = psS[:, 512:1024]
        tr_psum = [psB[:, 0:64].bitcast(BF16), psB[:, 512:576].bitcast(BF16)]
        ppj = [psA[:, 0:512], psA[:, 512:1024]]

        grp_fetch = {}
        grp_lastuse = {}

        def fetch_group(g):
            war = []
            if (g - 2) in grp_lastuse:
                war = [(st, grp_lastuse[g - 2])]
            par = g % 2
            return (wdma.issue(lambda e, g=g: e.dma_start(wqb[par], wq_d[g]), war),
                    wdma.issue(lambda e, g=g: e.dma_start(wkb[par], wk_d[g]), war),
                    wdma.issue(lambda e, g=g: e.dma_start(wvb[par], wv_d[g]), war),
                    wdma.issue(lambda e, g=g: e.dma_start(wotmp[par], wo_d[g]), war))

        grp_fetch[0] = fetch_group(0)

        ln_v, tr_a = [], []
        sc0 = contextlib.ExitStack()
        with sc0:
            p0x = sc0.enter_context(nc.sbuf_tensor("ph0x", [128, 2048], F32)).ap()
            p0c = sc0.enter_context(nc.sbuf_tensor("ph0c", [128, 2048], F32)).ap()
            p0s = sc0.enter_context(nc.sbuf_tensor("ph0s", [128, 2048], F32)).ap()
            p0n = sc0.enter_context(nc.sbuf_tensor("ph0n", [128, 2048], BF16)).ap()
            p0r = sc0.enter_context(nc.sbuf_tensor("ph0r", [128, 8], F32)).ap()
            xn_tile_a = []
            for tt in range(32):
                par = tt % 2
                bufs = (p0x[:, 1024 * par:1024 * (par + 1)],
                        p0c[:, 1024 * par:1024 * (par + 1)],
                        p0s[:, 1024 * par:1024 * (par + 1)],
                        p0n[:, 1024 * par:1024 * (par + 1)],
                        [p0r[:, 4 * par + i:4 * par + i + 1] for i in range(4)])
                half = xn_own if tt < 16 else xn_oth
                tl = tt % 16

                def dw(c, half=half, tl=tl):
                    return half[:, OWN * c + 128 * tl:OWN * c + 128 * (tl + 1)]
                emit_ln(bufs, 128 * tt, dw, ln_v, tr_a, [], tr_psum)
                xn_tile_a.append(sa.n)
        m_xn_a = sa.n
        m_ph0_v = sv.n
        m_g128s = emit_lam_g128()
        if "xn_own" in taps:
            wdma.issue(lambda e: e.dma_start(taps["xn_own"], xn_own),
                       waits=[(sa, m_xn_a)])
        if "xn_oth" in taps:
            wdma.issue(lambda e: e.dma_start(taps["xn_oth"], xn_oth),
                       waits=[(sa, m_xn_a)])

        scG = scA.enter_context(contextlib.ExitStack())
        if True:
            kg = scG.enter_context(nc.sbuf_tensor("kg", [128, 4096], BF16)).ap()
            qg = scG.enter_context(nc.sbuf_tensor("qg", [128, 2048], BF16)).ap()
            vg = scG.enter_context(nc.sbuf_tensor("vg", [128, 4096], BF16)).ap()
            vgt = scG.enter_context(nc.sbuf_tensor("vgt", [128, 32 * 130], BF16)).ap()
            ksq = scG.enter_context(nc.sbuf_tensor("ksq", [128, 4096], BF16)).ap()
            e01 = [scG.enter_context(nc.sbuf_tensor(f"e01{i}", [128, 1024], BF16)).ap()
                   for i in range(3)]
            o0s = scG.enter_context(nc.sbuf_tensor("o0s", [65, 512], F32)).ap()
            o1s = scG.enter_context(nc.sbuf_tensor("o1s", [65, 512], F32)).ap()
            cs = scG.enter_context(nc.sbuf_tensor("cs", [64, 512], F32)).ap()
            t1s = scG.enter_context(nc.sbuf_tensor("t1s", [64, 512], F32)).ap()
            csq = scG.enter_context(nc.sbuf_tensor("csq", [64, 512], BF16)).ap()
            wosb = scG.enter_context(nc.sbuf_tensor("wosb", [128, 1024], F32)).ap()

            prog.do("gpsimd", lambda e: e.memset(vgt, 1.0),
                    waits=[(sa, m_xn_a), (sv, m_ph0_v)])
            m_vgt_ones = sg.n

            pending = []
            marks = {}
            cat_qv = {}
            mk_exp = {}
            mk_av = {}
            m_evac_prev = 0
            pj_a = []
            vt_v = []

            for g in range(NG):
                if g + 1 < NG:
                    grp_fetch[g + 1] = fetch_group(g + 1)
                par = g % 2
                mk_q, mk_k, mk_v, mk_o = grp_fetch[g]

                prog.do("vector", lambda e, g=g, w=wotmp[par]: e.tensor_scalar(
                    wob_all[:, 1024 * g:1024 * (g + 1)], w, g128s, None, OP.mult),
                    waits=[mk_o, (sa, m_g128s)])

                if g > 0:
                    prog.do("tensor", None, waits=[(st, grp_lastuse[g - 1])])

                def proj(wb, mk_w, nchunks, scale, dst):
                    for tc in range(nchunks):
                        pj = ppj[tc % 2]
                        src = xn_own if (nchunks == 4 or tc < 4) else xn_oth
                        toff = 512 * (tc if (nchunks == 4 or tc < 4) else tc - 4)
                        half_off = 0 if (nchunks == 4 or tc < 4) else 16
                        need_tile = half_off + (toff // 512) * 4 + 3
                        m_xn_need = xn_tile_a[need_tile]
                        war = [(sa, pj_a[-2])] if len(pj_a) >= 2 else []
                        w0 = []
                        if g > 0 and tc == 0:
                            w0 = [(sa, mk_exp.get(0, 0)), (sa, mk_exp.get(1, 0))]
                        for ic in range(8):
                            prog.do("tensor", lambda e, pj=pj, wb=wb, ic=ic,
                                    src=src, toff=toff: e.matmul(
                                        pj, wb[:, 128 * ic:128 * (ic + 1)],
                                        src[:, OWN * ic + toff:OWN * ic + toff + 512],
                                        start=(ic == 0), stop=(ic == 7)),
                                    waits=([mk_w, (sa, m_xn_need)] + war + w0)
                                    if ic == 0 else [],
                                    incs=[(st, 1)] if ic == 7 else [])
                        mt = st.n
                        dsl = dst[:, 512 * tc:512 * (tc + 1)]
                        if scale is None:
                            prog.do("scalar", lambda e, d=dsl, pj=pj:
                                    e.activation(d, pj, AF.Copy),
                                    waits=[(st, mt)])
                        else:
                            prog.do("scalar", lambda e, d=dsl, pj=pj, sc=scale:
                                    e.activation(d, pj, AF.Copy, scale=sc),
                                    waits=[(st, mt)])
                        pj_a.append(sa.n)
                    return sa.n

                m_k_a = proj(wkb[par], mk_k, 8, QK_SCALE, kg)
                m_v_a = proj(wvb[par], mk_v, 8, None, vg)
                m_q_a = proj(wqb[par], mk_q, 4, QK_SCALE, qg)

                for kt in range(32):
                    jp = kt % 2
                    war2 = [(sv, vt_v[-2])] if len(vt_v) >= 2 else []
                    prog.do("tensor", lambda e, pt=tr_psum[jp], kt=kt:
                            e.transpose(pt, vg[:, 128 * kt:128 * (kt + 1)],
                                        identb),
                            waits=[(sa, m_v_a), (sg, m_vgt_ones)] + war2,
                            incs=[(st, 1)])
                    mt = st.n
                    dst = vgt[:, 130 * kt:130 * kt + 130].rearrange(
                        "p (v d) -> p v d", v=2, d=65)[:, :, 0:64]
                    srcv = tr_psum[jp].rearrange("p (v d) -> p v d", v=2, d=64)
                    del jp
                    prog.do("vector", lambda e, d=dst, s=srcv: e.tensor_copy(d, s),
                            waits=[(st, mt)])
                    vt_v.append(sv.n)
                m_vgt_v = sv.n
                m_vt_t = st.n

                prog.do("vector", lambda e: e.tensor_mul(ksq, kg, kg),
                        waits=[(sa, m_k_a)])
                prog.do("vector", lambda e: e.tensor_mul(
                    vg[:, 0:2048], qg, qg),
                    waits=[(sa, m_q_a), (st, m_vt_t)])
                m_sq_v = sv.n

                ms_v = []
                scans = [(ksq, 0, 8, 8, 10, 0), (vg, 0, 4, 9, 11, 0),
                         (ksq, 0, 8, 13, 15, 1), (vg, 0, 4, 14, 16, 1)]
                steps = []
                for si, (srcap, base, nch, oc, tc_, p_) in enumerate(scans):
                    for c8 in range(nch):
                        steps.append((si, c8))
                order = []
                c8pos = [0] * 4
                while len(order) < len(steps):
                    for si in range(4):
                        nch = scans[si][2]
                        if c8pos[si] < nch:
                            order.append((si, c8pos[si]))
                            c8pos[si] += 1
                gi = 0
                for si, c8 in order:
                    srcap, base, nch, oc, tc_, p_ = scans[si]
                    pp = (psS[0:1, 0:512] if gi % 2 == 0
                          else psS[0:1, 512:1024])
                    gi += 1
                    war = [(sv, ms_v[-2])] if len(ms_v) >= 2 else []
                    prog.do("tensor", lambda e, pp=pp, c8=c8, p=p_,
                            base=base, srcap=srcap: e.matmul(
                                pp, maskp[64 * p:64 * (p + 1), :],
                                srcap[64 * p:64 * (p + 1),
                                      base + 512 * c8:base + 512 * (c8 + 1)],
                                start=True, stop=True),
                            waits=[(sv, m_sq_v),
                                   (sv, marks.get("c7", 0))] + war,
                            incs=[(st, 1)])
                    mt = st.n
                    if c8 == 0:
                        prog.do("vector", lambda e, pp=pp, oc=oc:
                                e.reduce_max(smll[:, oc:oc + 1], pp, axis=AX),
                                waits=[(st, mt)])
                    else:
                        prog.do("vector", lambda e, pp=pp, tc_=tc_:
                                e.reduce_max(smll[:, tc_:tc_ + 1], pp, axis=AX),
                                waits=[(st, mt)])
                        prog.do("vector", lambda e, oc=oc, tc_=tc_:
                                e.tensor_max(smll[:, oc:oc + 1],
                                             smll[:, oc:oc + 1],
                                             smll[:, tc_:tc_ + 1]))
                    ms_v.append(sv.n)
                prog.do("vector", lambda e: e.tensor_add(
                    smll[:, 11:12], smll[:, 8:9], smll[:, 9:10]))
                prog.do("vector", lambda e: e.tensor_single_scalar(
                    smll[:, 12:13], smll[:, 11:12], -0.5, OP.mult))
                prog.do("vector", lambda e: e.tensor_add(
                    smll[:, 17:18], smll[:, 13:14], smll[:, 14:15]))
                prog.do("vector", lambda e: e.tensor_single_scalar(
                    smll[:, 18:19], smll[:, 17:18], -0.5, OP.mult))
                m_mh_v = sv.n
                marks["stat_v"] = sv.n
                bias_a = {}
                for p_ in range(2):
                    col = 12 if p_ == 0 else 18
                    prog.do("tensor", lambda e, col=col: e.matmul(
                        pb, ones128f, smll[:, col:col + 1],
                        start=True, stop=True),
                        waits=[(sv, m_mh_v), (sv, m_evac_prev), (sg, m_init),
                               (sa, bias_a.get(0, 0))],
                        incs=[(st, 1)])
                    mt = st.n
                    prog.do("scalar", lambda e, p=p_: e.activation(
                        bias_gp[:, p:p + 1], pb, AF.Copy),
                        waits=[(st, mt)] + marks.pop("biastap", []))
                    bias_a[p_] = sa.n
                if "bias" in taps and g == 0:
                    mkb = sdma.issue(lambda e: e.dma_start(
                        taps["bias"], bias_gp), waits=[(sa, bias_a[1])])
                    marks["biastap"] = [mkb]

                def emit_warm():
                    prog.do("tensor", lambda e: e.matmul(
                        pwarm[:, 0:432], identb, xn_own[:, 0:432],
                        start=True, stop=True),
                        waits=[(sv, marks.get("stat_v", 0)), (sa, m_xn_a),
                               (sg, m_init)])

                for p in range(2):
                    m_bias_a = bias_a[1]

                    def emit_av(kt, eb, p=p):
                        vsl = vgt[:, 130 * kt + 65 * p:
                                  130 * kt + 65 * (p + 1)]
                        avw = [(sa, mk_e_rdy[eb])]
                        if kt == 0:
                            avw.append((sv, m_evac_prev))
                        prog.do("tensor", lambda e, vsl=vsl, eb=eb, kt=kt:
                                e.matmul(po0, vsl, e01[eb][:, 0:512],
                                         start=(kt == 0), stop=(kt == 31)),
                                waits=avw)
                        prog.do("tensor", lambda e, vsl=vsl, eb=eb, kt=kt:
                                e.matmul(po1, vsl, e01[eb][:, 512:1024],
                                         start=(kt == 0), stop=(kt == 31)),
                                incs=[(st, 1)])
                        mk_av[eb] = st.n

                    mk_e_rdy = {}
                    for gkt in range(130):
                        if gkt < 128:
                            qv, kt = divmod(gkt, 32)
                            if kt in (8, 10, 16, 18, 24, 26) and pending:
                                pending.pop(0)()
                            kp = gkt % 2
                            kp3 = gkt % 3
                            Sps = S01[kp]
                            emit_warm()
                            for e_i in range(2):
                                w0 = []
                                if e_i == 0:
                                    w0 = [(sa, mk_exp.get(kp, 0)),
                                          (sa, m_bias_a), (sv, m_vgt_v)]
                                tp = (64 * p + 32 * e_i, 0)
                                prog.do("tensor", lambda e, Sps=Sps, e_i=e_i,
                                        kt=kt, p=p, qv=qv, tp=tp: e.matmul(
                                            Sps[:, 512 * e_i:512 * (e_i + 1)],
                                            kg[64 * p + 32 * e_i:
                                               64 * p + 32 * (e_i + 1),
                                               128 * kt:128 * (kt + 1)],
                                            qg[64 * p + 32 * e_i:
                                               64 * p + 32 * (e_i + 1),
                                               512 * qv:512 * (qv + 1)],
                                            start=True, stop=True,
                                            tile_position=tp),
                                        waits=w0,
                                        incs=[(st, 1)] if e_i == 1 else [])
                            mS = st.n
                            prog.do("scalar", lambda e, kp3=kp3, Sps=Sps, p=p:
                                    e.activation(e01[kp3], Sps, AF.Exp,
                                                 bias=bias_gp[:, p:p + 1]),
                                    waits=[(st, mS), (st, mk_av.get(kp3, 0))],
                                    )
                            mk_exp[kp] = sa.n
                            mk_e_rdy[kp3] = sa.n
                        if gkt < 2:
                            continue
                        t_ = gkt - 2
                        tqv, tk = divmod(t_, 32)
                        emit_warm()
                        emit_av(tk, t_ % 3)
                        if tk != 31:
                            continue
                        m_av = st.n
                        grp_lastuse[g] = st.n
                        prev_c7 = marks.get("c7", 0)
                        qv = tqv

                        def mk_c1(m_av=m_av, g=g, p=p, qv=qv, prev_c7=prev_c7):
                            prog.do("vector", lambda e: e.tensor_copy(
                                o0s, po0),
                                waits=[(st, m_av), (sv, prev_c7)]
                                + marks.pop("tapwar", []))
                            prog.do("vector", lambda e: e.tensor_copy(
                                o1s, po1))
                            marks["evac"] = sv.n
                            ma = sv.n
                            if "o0" in taps and g == 0 and p == 0 and qv == 0:
                                mk1 = sdma.issue(lambda e: e.dma_start(
                                    taps["o0"], o0s), waits=[(sv, ma)])
                                mk2 = sdma.issue(lambda e: e.dma_start(
                                    taps["o1"], o1s), waits=[(sv, ma)])
                                marks["tapwar"] = [mk1, mk2]
                            prog.do("vector", lambda e: e.reciprocal(
                                rows[:, 512:1024], o1s[64:65, :]))
                            prog.do("vector", lambda e: e.tensor_copy(
                                rows[:, 2048:2560], o0s[64:65, :]))
                            prog.do("vector", lambda e: e.scalar_tensor_tensor(
                                rows[:, 1024:1536], rows[:, 2048:2560], lamA,
                                rows[:, 512:1024], OP.mult, OP.mult))
                            prog.do("vector", lambda e: e.tensor_copy(
                                rowsb[:, 0:512], rows[:, 1024:1536]))
                            prog.do("vector", lambda e: e.reciprocal(
                                rows[:, 0:512], o0s[64:65, :]))
                            marks["c1"] = sv.n

                        def mk_c2():
                            prog.do("tensor", lambda e: e.matmul(
                                sp64, ones64f, rowsb[:, 0:512],
                                start=True, stop=True),
                                waits=[(sv, marks["c1"]),
                                       (sv, marks.get("c7", 0)),
                                       (sv, marks.get("stat_v", 0))],
                                incs=[(st, 1)])
                            marks["c2"] = st.n

                        def mk_c3():
                            prog.do("vector", lambda e: e.tensor_mul(
                                t1s, o1s[0:64, :], sp64),
                                waits=[(st, marks["c2"])])
                            prog.do("vector", lambda e: e.tensor_sub(
                                cs, o0s[0:64, :], t1s))
                            prog.do("vector", lambda e: e.tensor_mul(csq, cs, cs),
                                    )
                            marks["c3"] = sv.n

                        def mk_c4():
                            prog.do("tensor", lambda e: e.matmul(
                                sp1, ones64c, csq, start=True, stop=True),
                                waits=[(sv, marks["c3"])], incs=[(st, 1)])
                            marks["c4"] = st.n

                        def mk_c5():
                            prog.do("vector", lambda e: e.tensor_mul(
                                rows[:, 1536:2048], sp1, rows[:, 0:512]),
                                waits=[(st, marks["c4"])])
                            prog.do("vector", lambda e: e.tensor_mul(
                                rows[:, 1536:2048], rows[:, 1536:2048],
                                rows[:, 0:512]))
                            mv5 = sv.n
                            prog.do("scalar", lambda e: e.activation(
                                rows[:, 1536:2048], rows[:, 1536:2048], AF.Ln,
                                bias=epsb[0:1, :], scale=1.0 / 64),
                                waits=[(sv, mv5)])
                            prog.do("scalar", lambda e: e.activation(
                                rows[:, 1536:2048], rows[:, 1536:2048], AF.Exp,
                                scale=-0.5))
                            ma5 = sa.n
                            prog.do("vector", lambda e: e.tensor_mul(
                                rows[:, 1536:2048], rows[:, 1536:2048],
                                rows[:, 0:512]),
                                waits=[(sa, ma5)])
                            prog.do("vector", lambda e: e.tensor_copy(
                                rowsb[:, 512:1024], rows[:, 1536:2048]))
                            marks["c5"] = sv.n

                        def mk_c6():
                            prog.do("tensor", lambda e: e.matmul(
                                sp64, ones64f, rowsb[:, 512:1024],
                                start=True, stop=True),
                                waits=[(sv, marks["c5"]), (sv, marks["c3"])],
                                incs=[(st, 1)])
                            marks["c6"] = st.n

                        def mk_c7(g=g, p=p, qv=qv):
                            dst = attn_cat[64 * p:64 * (p + 1),
                                           OWN * g + 512 * qv:
                                           OWN * g + 512 * (qv + 1)]
                            prog.do("vector", lambda e, dst=dst: e.tensor_mul(
                                dst, cs, sp64),
                                waits=[(st, marks["c6"])])
                            marks["c7"] = sv.n
                            cat_qv[qv] = sv.n

                        mk_c1()
                        m_evac_prev = marks["evac"]
                        pending.extend([mk_c2, mk_c3, mk_c4, mk_c5, mk_c6,
                                        mk_c7])
            m_exp_l = max(mk_exp.values())

            wo_a = []
            wo_dma = []
            for u, (qc, ot) in enumerate((qc, ot) for qc in range(4)
                                         for ot in range(8)):
                    emit_warm()
                    if u == 4:
                        while len(pending) > 2:
                            pending.pop(0)()
                    if u == 8:
                        while pending:
                            pending.pop(0)()
                        m_cat_v = sv.n
                    pj = ppj[u % 2]
                    war = [(sv, wo_a[-2])] if len(wo_a) >= 2 else []
                    for ic in range(8):
                        prog.do("tensor", lambda e, pj=pj, ic=ic, ot=ot, qc=qc:
                                e.matmul(
                                    pj,
                                    wob_all[:, 1024 * ic + 128 * ot:
                                            1024 * ic + 128 * (ot + 1)],
                                    attn_cat[:, OWN * ic + 512 * qc:
                                             OWN * ic + 512 * (qc + 1)],
                                    start=(ic == 0), stop=(ic == 7)),
                                waits=([(sv, cat_qv[qc]), (sa, m_exp_l)]
                                       + war) if ic == 0 else [],
                                incs=[(st, 1)] if ic == 7 else [])
                    mt = st.n
                    wslots = [wosb[:, 0:512], wosb[:, 512:1024],
                              wqb[0].bitcast(F32), wqb[1].bitcast(F32)]
                    wdst = wslots[u % 4]
                    aw = [(st, mt)]
                    if len(wo_dma) >= 4:
                        aw.append(wo_dma[-4])
                    prog.do("vector", lambda e, d=wdst, pj=pj: e.tensor_copy(
                        d, pj), waits=aw)
                    wo_a.append(sv.n)
                    ma = sv.n
                    mk = sdma.issue(lambda e, d=wdst, ot=ot, qc=qc: e.dma_start(
                        scr_d[128 * ot:128 * (ot + 1), 512 * qc:512 * (qc + 1)],
                        d), waits=[(sv, ma)])
                    wo_dma.append(mk)
            if "attn_cat" in taps:
                wdma.issue(lambda e: e.dma_start(taps["attn_cat"], attn_cat),
                           waits=[(sv, m_cat_v)])
            if "kg" in taps:
                wdma.issue(lambda e: e.dma_start(taps["kg"], kg),
                           waits=[(sv, m_cat_v)])
            if "qg" in taps:
                wdma.issue(lambda e: e.dma_start(taps["qg"], qg),
                           waits=[(sv, m_cat_v)])
            if "vgt" in taps:
                wdma.issue(lambda e: e.dma_start(taps["vgt"], vgt),
                           waits=[(sv, m_cat_v)])
            if "attn_scr" in taps:
                sdma.issue(lambda e: e.dma_start(taps["attn_scr"], scr_d),
                           waits=sdma.fin())

            w12_pre = {}
            for ot in range(2):
                par = ot
                w1sl = w12b[:, 2048 * par:2048 * par + 1024]
                w2sl = w12b[:, 2048 * par + 1024:2048 * (par + 1)]
                w12_pre[ot] = (
                    wdma.issue(lambda e, d=w1sl, ot=ot: e.dma_start(d, w1_d[ot])),
                    wdma.issue(lambda e, d=w2sl, ot=ot: e.dma_start(d, w2_d[ot])))

    scA.close()
    bar_marks = ([(st, st.n), (sv, sv.n), (sa, sa.n), (sg, sg.n)]
                 + wdma.fin() + xdma.fin() + sdma.fin() + odma.fin())
    for _e in ("sync", "scalar", "vector", "gpsimd"):
        prog.do(_e, None, waits=bar_marks)
    scr_fin = sdma.fin()

    with (
        nc.sbuf_tensor("a1", [128, 32 * TCH], BF16) as d1_,
        nc.sbuf_tensor("hbT", [128, 3 * 4096], BF16) as hb_top_,
        nc.sbuf_tensor("xldp", [128, 2 * 1024], F32) as xld_,
        nc.sbuf_tensor("identf", [128, 128], F32) as idf_,
        nc.psum_tensor("pM0", [128, 1024], F32) as e1_,
        nc.psum_tensor("pM1", [128, 1024], F32) as e2_,
        nc.psum_tensor("pM2", [128, 1024], F32) as e3_,
        nc.psum_tensor("pM3", [128, 1024], F32) as e4_,
    ):
        a1 = d1_.ap()
        hb = hb_top_.ap()
        xld = xld_.ap()
        identf = idf_.ap()
        pM = [e1_.ap(), e2_.ap(), e3_.ap(), e4_.ap()]
        ptrF = [pM[3][:, 0:128], pM[3][:, 512:640]]
        prog.do("gpsimd", lambda e: e.memset(identf, 0.0))
        prog.do("gpsimd", lambda e: e.affine_select(
            out=identf, in_=identf, compare_op=OP.not_equal, fill=1.0, base=0,
            pattern=[[-1, 128]], channel_multiplier=1))
        m_idf = sg.n

        w12_t, w12_a, w12_v = [], [], []
        h_t, h_a = [], []
        ring_t = []
        w3_t, w3_a, f3_v, f3_t, pc_v = [], [], [], [], []
        xadd_v = []
        out_done = []
        tap_gate = []

        for ch in range(NCH):
            coff = TCH * ch
            chunk_gate = []
            if ch > 0:
                chunk_gate = ([(st, w3_t[-1]), (sv, pc_v[-1]),
                               (sa, w3_a[-1])] + list(tap_gate))

            if "xnc" in taps and ch == 0:
                for icq in range(8):
                    tap_gate.append(wdma.issue(
                        lambda e, icq=icq: e.dma_start(
                            taps["xnc"][:, TCH * icq:TCH * (icq + 1)],
                            xn_own[:, OWN * icq + coff:
                                   OWN * icq + coff + TCH]), waits=[]))

            if True:
                with nc.sbuf_tensor(f"t0b{ch}", [128, 2 * 1024], F32) as t0b_:
                    t0b = t0b_.ap()
                    for ot in range(32):
                        par = ot % 2
                        w1sl = w12b[:, 2048 * par:2048 * par + 1024]
                        w2sl = w12b[:, 2048 * par + 1024:2048 * (par + 1)]
                        if ch == 0 and ot < 2:
                            mk1, mk2 = w12_pre[ot]
                        else:
                            war = ([(st, w12_t[-2])] if len(w12_t) >= 2
                                   else [])
                            mk1 = wdma.issue(lambda e, d=w1sl, ot=ot:
                                             e.dma_start(d, w1_d[ot]),
                                             waits=war)
                            mk2 = wdma.issue(lambda e, d=w2sl, ot=ot:
                                             e.dma_start(d, w2_d[ot]),
                                             waits=war)
                        pa = pM[0] if par == 0 else pM[2]
                        pb2 = pM[1] if par == 0 else pM[3]
                        war_a = [(sa, w12_a[-2])] if len(w12_a) >= 2 else []
                        war_v = [(sv, w12_v[-2])] if len(w12_v) >= 2 else []
                        w1w0, w1w1 = [], []
                        if ch == 0 and ot == 0:
                            w1w0 = [(sv, wo_a[-2])]
                            w1w1 = [(sv, wo_a[-1])]
                        elif ch > 0 and ot == 0:
                            w1w0 = [(sa, w3_a[-1])] + list(tap_gate)
                        elif ch > 0 and ot == 1:
                            w1w0 = [(sv, pc_v[-1])]
                        for ic in range(8):
                            for hc in range(2):
                                prog.do("tensor", lambda e, pa=pa, w=w1sl,
                                        ic=ic, hc=hc, coff=coff: e.matmul(
                                            pa[:, 512 * hc:512 * (hc + 1)],
                                            w[:, 128 * ic:128 * (ic + 1)],
                                            xn_own[:, OWN * ic + coff + 512 * hc:
                                                   OWN * ic + coff + 512 * (hc + 1)],
                                            start=(ic == 0), stop=(ic == 7)),
                                        waits=([mk1] + war_a + w1w0)
                                        if (ic == 0 and hc == 0) else
                                        (w1w1 if (ic == 0 and hc == 1) else []))
                        for ic in range(8):
                            for hc in range(2):
                                lastw = (ic == 7 and hc == 1)
                                prog.do("tensor", lambda e, pb2=pb2, w=w2sl,
                                        ic=ic, hc=hc, coff=coff: e.matmul(
                                            pb2[:, 512 * hc:512 * (hc + 1)],
                                            w[:, 128 * ic:128 * (ic + 1)],
                                            xn_own[:, OWN * ic + coff + 512 * hc:
                                                   OWN * ic + coff + 512 * (hc + 1)],
                                            start=(ic == 0), stop=(ic == 7)),
                                        waits=([mk2] + war_v)
                                        if (ic == 0 and hc == 0) else [],
                                        incs=[(st, 1)] if lastw else [])
                        w12_t.append(st.n)
                        mt = st.n
                        t0 = t0b[:, 1024 * par:1024 * (par + 1)]
                        prog.do("scalar", lambda e, t0=t0, pa=pa, ot=ot:
                                e.activation(t0, pa, AF.Silu,
                                             scale=s1s[:, ot:ot + 1]),
                                waits=[(st, mt)] + war_v)
                        w12_a.append(sa.n)
                        ma = sa.n
                        prog.do("vector", lambda e, pb2=pb2, t0=t0, ot=ot:
                                e.scalar_tensor_tensor(
                                    a1[:, TCH * ot:TCH * (ot + 1)], pb2,
                                    s2s[:, ot:ot + 1], t0, OP.mult, OP.mult),
                                waits=[(sa, ma)])
                        w12_v.append(sv.n)
                    m_a1_v = sv.n
                a1_tap_mk = []
                if "a1" in taps and ch == 0:
                    a1_tap_mk.append(wdma.issue(
                        lambda e: e.dma_start(taps["a1"], a1),
                        waits=[(sv, m_a1_v)]))

            with (
                nc.sbuf_tensor(f"a2{ch}", [128, 32 * TCH], BF16) as i1_,
            ):
                a2 = i1_.ap()

                def hstage(hd, srcap, dstap, ssc, src_ready, extra0=()):
                    for ot in range(32):
                        par = ot % 2
                        sl3 = len(ring_t) % 3
                        wsl = hb[:, 4096 * sl3:4096 * (sl3 + 1)]
                        war = [(st, ring_t[-3])] if len(ring_t) >= 3 else []
                        mk = wdma.issue(lambda e, d=wsl, ot=ot, hd=hd:
                                        e.dma_start(d, hd[ot]), waits=war)
                        pp = pM[par]
                        war_a = [(sa, h_a[-2])] if len(h_a) >= 2 else []
                        if not h_a:
                            war_a = [(sv, m_a1_v)]
                        if ot == 0:
                            war_a = war_a + list(extra0)
                        for ic in range(32):
                            for hc in range(2):
                                lastw = (ic == 31 and hc == 1)
                                prog.do("tensor", lambda e, pp=pp, w=wsl,
                                        ic=ic, hc=hc, srcap=srcap: e.matmul(
                                            pp[:, 512 * hc:512 * (hc + 1)],
                                            w[:, 128 * ic:128 * (ic + 1)],
                                            srcap[:, TCH * ic + 512 * hc:
                                                  TCH * ic + 512 * (hc + 1)],
                                            start=(ic == 0), stop=(ic == 31)),
                                        waits=([mk, src_ready] + war_a)
                                        if (ic == 0 and hc == 0) else [],
                                        incs=[(st, 1)] if lastw else [])
                        h_t.append(st.n)
                        ring_t.append(st.n)
                        mt = st.n
                        prog.do("scalar", lambda e, pp=pp, ot=ot, dstap=dstap,
                                ssc=ssc: e.activation(
                                    dstap[:, TCH * ot:TCH * (ot + 1)], pp,
                                    AF.Gelu, scale=ssc[:, ot:ot + 1]),
                                waits=[(st, mt)])
                        h_a.append(sa.n)
                    return sa.n

                m_a2_a = hstage(h1_d, a1, a2, h1ss, (sv, m_a1_v),
                                extra0=((list(out_done[-1])
                                         + [(sv, pc_v[-1])])
                                        if out_done else ()))
                if "a2" in taps and ch == 0:
                    tap_gate.append(wdma.issue(
                        lambda e, a2=a2: e.dma_start(taps["a2"], a2),
                        waits=[(sa, m_a2_a)]))
                m_a3_a = hstage(h2_d, a2, a1, h2ss, (sa, m_a2_a),
                                extra0=a1_tap_mk)

            with (
                nc.sbuf_tensor(f"f3s{ch}", [128, 2 * 1024], F32) as j2_,
                nc.sbuf_tensor(f"asld{ch}", [128, 2 * 1024], F32) as j3_,
                nc.sbuf_tensor(f"outT{ch}", [128, 8 * 1024], F32) as j5_,
            ):
                f3sb, asld, outT = (j2_.ap(), j3_.ap(), j5_.ap())
                row_mk = []
                for ot3 in range(8):
                    par = ot3 % 2
                    xwar = ([(sv, xadd_v[-2])] if len(xadd_v) >= 2 else [])
                    if ot3 < 2 and out_done:
                        xwar = xwar + list(out_done[-1])
                    xmk = []
                    for tj in range(8):
                        row0 = coff + 128 * tj
                        xmk.append(xdma.issue(
                            lambda e, tj=tj, row0=row0, par=par, ot3=ot3:
                            e.dma_start(
                                xld[:, 1024 * par + 128 * tj:
                                    1024 * par + 128 * (tj + 1)],
                                x_d[row0:row0 + 128,
                                    128 * ot3:128 * (ot3 + 1)]), waits=xwar))
                    sl3 = len(ring_t) % 3
                    wsl = hb[:, 4096 * sl3:4096 * (sl3 + 1)]
                    war = [(st, ring_t[-3])] if len(ring_t) >= 3 else []
                    mk = wdma.issue(lambda e, d=wsl, ot3=ot3: e.dma_start(
                        d, w3_d[ot3]), waits=war)
                    mka = sdma.issue(lambda e, ot3=ot3, par=par, coff=coff,
                                     asld=asld: e.dma_start(
                                         asld[:, 1024 * par:1024 * (par + 1)],
                                         scr_d[128 * ot3:128 * (ot3 + 1),
                                               coff:coff + TCH]),
                                     waits=scr_fin + (chunk_gate +
                                      [(sa, m_a3_a)] if ot3 < 2 else []) +
                                     ([(sv, f3_v[-2])] if len(f3_v) >= 2
                                      else [(sa, m_a3_a)]))
                    pp = pM[par]
                    war_a = [(sa, w3_a[-2])] if len(w3_a) >= 2 else []
                    for ic in range(32):
                        for hc in range(2):
                            lastw = (ic == 31 and hc == 1)
                            prog.do("tensor", lambda e, pp=pp, w=wsl, ic=ic,
                                    hc=hc: e.matmul(
                                        pp[:, 512 * hc:512 * (hc + 1)],
                                        w[:, 128 * ic:128 * (ic + 1)],
                                        a1[:, TCH * ic + 512 * hc:
                                            TCH * ic + 512 * (hc + 1)],
                                        start=(ic == 0), stop=(ic == 31)),
                                    waits=([mk, (sa, m_a3_a)] + war_a)
                                    if (ic == 0 and hc == 0) else [],
                                    incs=[(st, 1)] if lastw else [])
                    w3_t.append(st.n)
                    ring_t.append(st.n)
                    mt = st.n
                    f3 = f3sb[:, 1024 * par:1024 * (par + 1)]
                    fw = [(st, mt)]
                    if len(f3_t) >= 2:
                        fw.append((st, f3_t[-2]))
                    if ot3 < 2 and chunk_gate:
                        fw += chunk_gate
                    prog.do("scalar", lambda e, f3=f3, pp=pp, ot3=ot3:
                            e.activation(f3, pp, AF.Copy,
                                         scale=w3ss[:, ot3:ot3 + 1]),
                            waits=fw)
                    w3_a.append(sa.n)
                    ma = sa.n
                    prog.do("vector", lambda e, f3=f3, par=par, asld=asld:
                            e.tensor_add(
                        f3, f3, asld[:, 1024 * par:1024 * (par + 1)]),
                        waits=[(sa, ma), mka, (sv, m_a1_v)])
                    f3_v.append(sv.n)
                    mfv = sv.n
                    for tj in range(8):
                        jp = tj % 2
                        war2 = [(sv, pc_v[-2])] if len(pc_v) >= 2 else []
                        if not pc_v:
                            war2 = [(sg, m_idf)]
                        prog.do("tensor", lambda e, jp=jp, f3=f3, tj=tj:
                                e.transpose(ptrF[jp],
                                            f3[:, 128 * tj:128 * (tj + 1)],
                                            identf),
                                waits=[(sv, mfv)] + war2, incs=[(st, 1)])
                        mt2 = st.n
                        dst = outT[:, TCH * tj + 128 * ot3:
                                   TCH * tj + 128 * (ot3 + 1)]
                        xsl = xld[:, 1024 * par + 128 * tj:
                                  1024 * par + 128 * (tj + 1)]
                        prog.do("vector", lambda e, d=dst, jp=jp, xsl=xsl:
                                e.tensor_add(d, ptrF[jp], xsl),
                                waits=[(st, mt2), xmk[tj]])
                        pc_v.append(sv.n)
                        if ot3 == 7:
                            row_mk.append(sv.n)
                    xadd_v.append(sv.n)
                    f3_t.append(st.n)
                m_outT_v = sv.n
                for tj in range(8):
                    odma.issue(lambda e, tj=tj, coff=coff, outT=outT:
                               e.dma_start(
                        out_d[coff + 128 * tj:coff + 128 * (tj + 1), :],
                        outT[:, TCH * tj:TCH * (tj + 1)]),
                        waits=[(sv, row_mk[tj])])
                if "outT" in taps and ch == 0:
                    tap_gate.append(wdma.issue(
                        lambda e, outT=outT: e.dma_start(taps["outT"], outT),
                        waits=[(sv, m_outT_v)]))
                out_done.append(odma.fin())

        prog.do("sync", None, waits=odma.fin() + sdma.fin())
        prog.do("gpsimd", None, waits=wdma.fin())

    with nc.Block() as block:
        prog.replay(block)
    stack.close()
    return nc


def host_layout(inputs):
    W = lambda a: np.ascontiguousarray(np.asarray(a), dtype=np.float32)
    o = {}
    o["wq4"] = W(np.asarray(inputs["L0_attn_Wq"]).reshape(8, 128, 8, 128)
                 .transpose(0, 3, 2, 1).reshape(8, 128, 1024))
    o["wk4"] = W(np.asarray(inputs["L0_attn_Wk"]).reshape(8, 128, 8, 128)
                 .transpose(0, 3, 2, 1).reshape(8, 128, 1024))
    o["wv4"] = W(np.asarray(inputs["L0_attn_Wv"]).reshape(8, 128, 8, 128)
                 .transpose(0, 3, 2, 1).reshape(8, 128, 1024))
    o["wo4"] = W(np.asarray(inputs["L0_attn_Wo"]).reshape(8, 128, 8, 128)
                 .transpose(2, 3, 0, 1).reshape(8, 128, 1024))
    o["w14"] = W(np.asarray(inputs["L0_ffn_w1_w"]).reshape(32, 128, 8, 128)
                 .transpose(0, 3, 2, 1).reshape(32, 128, 1024))
    o["w24"] = W(np.asarray(inputs["L0_ffn_w2_w"]).reshape(32, 128, 8, 128)
                 .transpose(0, 3, 2, 1).reshape(32, 128, 1024))
    o["h14"] = W(np.asarray(inputs["L0_ffn_h1_w"]).reshape(32, 128, 32, 128)
                 .transpose(0, 3, 2, 1).reshape(32, 128, 4096))
    o["h24"] = W(np.asarray(inputs["L0_ffn_h2_w"]).reshape(32, 128, 32, 128)
                 .transpose(0, 3, 2, 1).reshape(32, 128, 4096))
    o["w34"] = W(np.asarray(inputs["L0_ffn_w3_w"]).reshape(8, 128, 32, 128)
                 .transpose(0, 3, 2, 1).reshape(8, 128, 4096))
    o["s1c"] = W(np.asarray(inputs["L0_ffn_w1_s"]).reshape(32, 128).T)
    o["s2c"] = W(np.asarray(inputs["L0_ffn_w2_s"]).reshape(32, 128).T)
    o["h1sc"] = W(np.asarray(inputs["L0_ffn_h1_s"]).reshape(32, 128).T)
    o["h2sc"] = W(np.asarray(inputs["L0_ffn_h2_s"]).reshape(32, 128).T)
    o["w3sc"] = W(np.asarray(inputs["L0_ffn_w3_s"]).reshape(8, 128).T)
    g = np.asarray(inputs["L0_attn_g"])
    o["g128"] = W(np.concatenate([g, g]).reshape(128, 1))
    o["lamvec"] = W(np.concatenate([
        np.asarray(inputs["L0_attn_lambda_q1"]),
        np.asarray(inputs["L0_attn_lambda_k1"]),
        np.asarray(inputs["L0_attn_lambda_q2"]),
        np.asarray(inputs["L0_attn_lambda_k2"])]).reshape(1, 128))
    return o


def kernel(**inputs):
    shared = host_layout(inputs)
    x = np.ascontiguousarray(np.asarray(inputs["L0_attn_x"]), dtype=np.float32)
    nc = build()
    in_maps = []
    for c in range(8):
        b, h = c // 2, c % 2
        xb = x[b]
        xc = np.ascontiguousarray(np.concatenate(
            [xb[h * OWN:(h + 1) * OWN], xb[(1 - h) * OWN:(2 - h) * OWN]]))
        m = dict(shared)
        m["x"] = xc
        in_maps.append(m)
    res = run_bass_kernel_spmd(nc, in_maps, core_ids=list(range(8)))
    out = np.empty((4, SEQ, DM), dtype=np.float32)
    for c in range(8):
        b, h = c // 2, c % 2
        out[b, h * OWN:(h + 1) * OWN] = res.results[c]["out"]
    return out


# version 32
# speedup vs baseline: 1.0788x; 1.0010x over previous
import math
import contextlib
import numpy as np
import concourse.bass as bass
import concourse.mybir as mybir
from concourse.bass_utils import run_bass_kernel_spmd

F32 = mybir.dt.float32
BF16 = mybir.dt.bfloat16
AF = mybir.ActivationFunctionType
OP = mybir.AluOpType
AX = mybir.AxisListType.X

SEQ, OWN, DM = 4096, 2048, 1024
NG = 8
LAMBDA_INIT = 0.8 - 0.6 * math.exp(-0.3 * 12)
ONE_M_LI = 1.0 - LAMBDA_INIT
QK_SCALE = 32.0 ** -0.25
EPS = 1e-5
TCH = 1024
NCH = OWN // TCH

ENGINES = ("sync", "tensor", "scalar", "vector", "gpsimd")


class Sem:
    __slots__ = ("h", "n")

    def __init__(self, h):
        self.h = h
        self.n = 0


class Prog:

    CHAINED = ("vector", "scalar", "gpsimd")

    def __init__(self, chain_sems=None):
        self.q = {e: [] for e in ENGINES}
        self.last_wait = {e: {} for e in ENGINES}
        self.chain = chain_sems or {}

    def do(self, eng, fn, waits=(), incs=(), dma=False):
        cs = self.chain.get(eng)
        waits = list(waits)
        incs = list(incs)
        if cs is not None and fn is not None:
            waits.append((cs, cs.n))
            if not dma:
                incs.append((cs, 1))
        lw = self.last_wait[eng]
        ws = []
        for s, v in waits:
            if v <= 0:
                continue
            k = id(s)
            if lw.get(k, -1) >= v:
                continue
            lw[k] = v
            ws.append((s, v))
        self.q[eng].append((ws, fn, tuple(incs)))
        for s, k in incs:
            s.n += k

    def barrier(self, marks):
        for e in ENGINES:
            self.do(e, None, waits=list(marks))

    def replay(self, block):
        def mk(name):
            def body(eng):
                for ws, fn, incs in self.q[name]:
                    for s, v in ws:
                        eng.wait_ge(s.h, v)
                    if fn is None:
                        if incs:
                            ins = eng.nop()
                            for s, k in incs:
                                ins.then_inc(s.h, k)
                        continue
                    ins = fn(eng)
                    for s, k in incs:
                        ins.then_inc(s.h, k)
            return body
        block.sync(mk("sync"))
        block.tensor(mk("tensor"))
        block.scalar(mk("scalar"))
        block.vector(mk("vector"))
        block.gpsimd(mk("gpsimd"))


class Stream:

    def __init__(self, prog, eng, sems):
        self.prog = prog
        self.eng = eng
        self.sems = sems
        self.idx = 0

    def issue(self, fn, waits=()):
        s = self.sems[self.idx % 2]
        self.idx += 1
        w = list(waits) + [(s, s.n)]
        self.prog.do(self.eng, fn, waits=w, incs=[(s, 16)], dma=True)
        return (s, s.n)

    def fin(self):
        return [(s, s.n) for s in self.sems]


def build(tap=()):
    nc = bass.Bass("TRN2", target_bir_lowering=False)

    x_d = nc.dram_tensor("x", [SEQ, DM], F32, kind="ExternalInput").ap()
    wq_d = nc.dram_tensor("wq4", [8, 128, 1024], F32, kind="ExternalInput").ap()
    wk_d = nc.dram_tensor("wk4", [8, 128, 1024], F32, kind="ExternalInput").ap()
    wv_d = nc.dram_tensor("wv4", [8, 128, 1024], F32, kind="ExternalInput").ap()
    wo_d = nc.dram_tensor("wo4", [8, 128, 1024], F32, kind="ExternalInput").ap()
    w1_d = nc.dram_tensor("w14", [32, 128, 1024], F32, kind="ExternalInput").ap()
    w2_d = nc.dram_tensor("w24", [32, 128, 1024], F32, kind="ExternalInput").ap()
    h1_d = nc.dram_tensor("h14", [32, 128, 4096], F32, kind="ExternalInput").ap()
    h2_d = nc.dram_tensor("h24", [32, 128, 4096], F32, kind="ExternalInput").ap()
    w3_d = nc.dram_tensor("w34", [8, 128, 4096], F32, kind="ExternalInput").ap()
    s1_d = nc.dram_tensor("s1c", [128, 32], F32, kind="ExternalInput").ap()
    s2_d = nc.dram_tensor("s2c", [128, 32], F32, kind="ExternalInput").ap()
    h1s_d = nc.dram_tensor("h1sc", [128, 32], F32, kind="ExternalInput").ap()
    h2s_d = nc.dram_tensor("h2sc", [128, 32], F32, kind="ExternalInput").ap()
    w3s_d = nc.dram_tensor("w3sc", [128, 8], F32, kind="ExternalInput").ap()
    g_d = nc.dram_tensor("g128", [128, 1], F32, kind="ExternalInput").ap()
    lv_d = nc.dram_tensor("lamvec", [1, 128], F32, kind="ExternalInput").ap()
    out_d = nc.dram_tensor("out", [OWN, DM], F32, kind="ExternalOutput").ap()
    scr_d = nc.dram_tensor("attn_scr", [DM, OWN], F32, kind="Internal").ap()
    taps = {}
    for t in tap:
        shp = {"xn_own": [128, 16384], "xn_oth": [128, 16384],
               "attn_cat": [128, 16384], "attn_scr": [DM, OWN],
               "a1": [128, 32 * TCH], "kg": [128, 4096],
               "qg": [128, 2048], "vgt": [128, 32 * 130],
               "bias": [128, 2], "o0": [65, 512], "o1": [65, 512],
               "xnc": [128, 8 * TCH], "a2": [128, 32 * TCH],
               "outT": [128, 8 * TCH]}[t]
        taps[t] = nc.dram_tensor("tap_" + t, shp, F32, kind="ExternalOutput").ap()

    A = nc.alloc_sbuf_tensor
    identb = A("identb", [128, 128], BF16).ap()
    s1s = A("s1s", [128, 32], F32).ap()
    s2s = A("s2s", [128, 32], F32).ap()
    h1ss = A("h1ss", [128, 32], F32).ap()
    h2ss = A("h2ss", [128, 32], F32).ap()
    w3ss = A("w3ss", [128, 8], F32).ap()
    g128 = A("g128s", [128, 1], F32).ap()
    g128s = A("g128sc", [128, 1], F32).ap()
    lvs = A("lvs", [1, 128], F32).ap()
    epsb = A("epsb", [128, 1], F32).ap()
    ones64f = A("ones64f", [1, 64], BF16).ap()
    ones64c = A("ones64c", [64, 1], BF16).ap()
    ones128f = A("ones128f", [1, 128], F32).ap()
    maskp = A("maskp", [128, 1], BF16).ap()
    smll = A("smll", [1, 64], F32).ap()
    lamA = A("lamA", [1, 1], F32).ap()
    bias_gp = A("bias_gp", [128, 2], F32).ap()
    xn_own = A("xn_own", [128, 8 * OWN], BF16).ap()
    w12b = A("w12b", [128, 4 * 1024], BF16).ap()

    stack = contextlib.ExitStack()
    sem = {n: Sem(stack.enter_context(nc.semaphore(n))) for n in
           ("t", "a", "v", "g", "fin", "cv", "ca", "cg",
            "wd0", "wd1", "xd0", "xd1", "sw0", "sw1", "ow0", "ow1")}
    prog = Prog(chain_sems={"vector": sem["cv"], "scalar": sem["ca"],
                            "gpsimd": sem["cg"]})
    sem["v"], sem["a"], sem["g"] = sem["cv"], sem["ca"], sem["cg"]
    st, sa, sv, sg = sem["t"], sem["a"], sem["v"], sem["g"]
    wdma = Stream(prog, "gpsimd", [sem["wd0"], sem["wd1"]])
    xdma = Stream(prog, "sync", [sem["xd0"], sem["xd1"]])
    sdma = Stream(prog, "sync", [sem["sw0"], sem["sw1"]])
    odma = Stream(prog, "sync", [sem["ow0"], sem["ow1"]])

    for ap in (identb,):
        prog.do("gpsimd", lambda e, ap=ap: e.memset(ap, 0.0))
        prog.do("gpsimd", lambda e, ap=ap: e.affine_select(
            out=ap, in_=ap, compare_op=OP.not_equal, fill=1.0, base=0,
            pattern=[[-1, 128]], channel_multiplier=1))
    for ap, val in ((epsb, EPS), (ones64f, 1.0), (ones64c, 1.0),
                    (ones128f, 1.0), (maskp, 1.0)):
        prog.do("gpsimd", lambda e, ap=ap, val=val: e.memset(ap, val))
    prog.do("gpsimd", lambda e: e.nop())
    m_init = sg.n
    for dst, src in ((s1s, s1_d), (s2s, s2_d), (h1ss, h1s_d), (h2ss, h2s_d),
                     (w3ss, w3s_d), (g128, g_d), (lvs, lv_d)):
        wdma.issue(lambda e, d=dst, s=src: e.dma_start(d, s))
    prog.do("gpsimd", None, waits=wdma.fin())
    prog.do("gpsimd", lambda e: e.nop())
    m_consts = sg.n
    def emit_lam_g128():
        prog.do("scalar",
                lambda e: e.activation(g128s, g128, AF.Copy, scale=ONE_M_LI),
                waits=[(sg, m_consts)])
        mg = sa.n
        prog.do("vector", lambda e: e.tensor_mul(smll[:, 0:32], lvs[:, 0:32],
                                                 lvs[:, 32:64]),
                waits=[(sg, m_consts)])
        prog.do("vector", lambda e: e.tensor_mul(smll[:, 32:64], lvs[:, 64:96],
                                                 lvs[:, 96:128]))
        prog.do("vector", lambda e: e.reduce_sum(smll[:, 0:1], smll[:, 0:32],
                                                 axis=AX))
        prog.do("vector", lambda e: e.reduce_sum(smll[:, 1:2], smll[:, 32:64],
                                                 axis=AX))
        m1 = sv.n
        prog.do("scalar", lambda e: e.activation(smll[:, 2:3], smll[:, 0:1],
                                                 AF.Exp), waits=[(sv, m1)])
        prog.do("scalar", lambda e: e.activation(smll[:, 3:4], smll[:, 1:2],
                                                 AF.Exp))
        m2 = sa.n
        prog.do("vector", lambda e: e.tensor_sub(smll[:, 4:5], smll[:, 2:3],
                                                 smll[:, 3:4]), waits=[(sa, m2)])
        prog.do("vector", lambda e: e.tensor_single_scalar(
            lamA, smll[:, 4:5], LAMBDA_INIT, OP.add))
        return mg

    def emit_ln(bufs, row0, dest_writer, ln_v_marks, tr_a_marks, extra_tr_waits,
                trp, extra_dma_waits=()):
        xb, xc, sq, xnt, rr = bufs
        war = [(sv, ln_v_marks[-2])] if len(ln_v_marks) >= 2 else []
        war += list(extra_dma_waits)
        mk_x = xdma.issue(lambda e, xb=xb, row0=row0: e.dma_start(
            xb, x_d[row0:row0 + 128, :]), waits=war)
        prog.do("vector", lambda e, xb=xb, r=rr[0]: e.reduce_sum(r, xb, axis=AX),
                waits=[mk_x])
        prog.do("vector", lambda e, r=rr[1], s=rr[0]: e.tensor_single_scalar(
            r, s, -1.0 / DM, OP.mult))
        prog.do("vector", lambda e, xc=xc, xb=xb, r=rr[1]: e.tensor_scalar(
            xc, xb, r, None, OP.add))
        prog.do("vector", lambda e, sq=sq, xc=xc: e.tensor_mul(sq, xc, xc))
        prog.do("vector", lambda e, r=rr[2], sq=sq: e.reduce_sum(r, sq, axis=AX),
                )
        ln_v_marks.append(sv.n)
        mv = sv.n
        prog.do("scalar", lambda e, r=rr[2]: e.activation(
            r, r, AF.Ln, bias=epsb, scale=1.0 / DM),
            waits=[(sv, mv), (sg, m_init)])
        prog.do("scalar", lambda e, d=rr[3], s=rr[2]: e.activation(
            d, s, AF.Exp, scale=-0.5))
        ma = sa.n
        prog.do("vector", lambda e, d=xnt, xc=xc, r=rr[3]: e.tensor_scalar(
            d, xc, r, None, OP.mult), waits=[(sa, ma)])
        mn = sv.n
        ln_v_marks[-1] = sv.n
        for c in range(8):
            jp = c % 2
            war2 = [(sa, tr_a_marks[-2])] if len(tr_a_marks) >= 2 else []
            prog.do("tensor", lambda e, pt=trp[jp], xnt=xnt, c=c: e.transpose(
                pt, xnt[:, 128 * c:128 * (c + 1)], identb),
                waits=[(sv, mn), (sg, m_init)] + war2 + list(extra_tr_waits),
                incs=[(st, 1)])
            mt = st.n
            dst = dest_writer(c)
            prog.do("scalar", lambda e, d=dst, pt=trp[jp]: e.activation(
                d, pt, AF.Copy), waits=[(st, mt)])
            tr_a_marks.append(sa.n)

    if True:
        scA = stack.enter_context(contextlib.ExitStack())
        _sb = lambda n, s, d: scA.enter_context(nc.sbuf_tensor(n, s, d)).ap()
        xn_oth = _sb("xn_oth", [128, 8 * OWN], BF16)
        attn_cat = _sb("attn_cat", [128, 8 * OWN], BF16)
        wob_all = _sb("wob_all", [128, 8 * 1024], BF16)
        rows = _sb("rows", [1, 5 * 512], F32)
        rowsb = _sb("rowsb", [1, 2 * 512], BF16)
        wqb = [_sb(f"wqb{i}", [128, 1024], BF16) for i in range(2)]
        wkb = [_sb(f"wkb{i}", [128, 1024], BF16) for i in range(2)]
        wvb = [_sb(f"wvb{i}", [128, 1024], BF16) for i in range(2)]
        wotmp = [_sb(f"wotmp{i}", [128, 1024], BF16) for i in range(2)]
        psA = scA.enter_context(nc.psum_tensor("psA", [128, 1024], F32)).ap()
        psB = scA.enter_context(nc.psum_tensor("psB", [128, 1024], F32)).ap()
        psO = scA.enter_context(nc.psum_tensor("psO", [128, 1024], F32)).ap()
        psS = scA.enter_context(nc.psum_tensor("psS", [128, 1024], F32)).ap()
        S01 = [psA, psB]
        po0 = psO[0:65, 0:512]
        po1 = psO[0:65, 512:1024]
        pb = psO[0:128, 0:1]
        sp64 = psS[0:64, 0:512]
        sp1 = psS[0:1, 0:512]
        pwarm = psS[:, 512:1024]
        tr_psum = [psB[:, 0:64].bitcast(BF16), psB[:, 512:576].bitcast(BF16)]
        ppj = [psA[:, 0:512], psA[:, 512:1024]]

        grp_fetch = {}
        grp_lastuse = {}

        def fetch_group(g):
            war = []
            if (g - 2) in grp_lastuse:
                war = [(st, grp_lastuse[g - 2])]
            par = g % 2
            return (wdma.issue(lambda e, g=g: e.dma_start(wqb[par], wq_d[g]), war),
                    wdma.issue(lambda e, g=g: e.dma_start(wkb[par], wk_d[g]), war),
                    wdma.issue(lambda e, g=g: e.dma_start(wvb[par], wv_d[g]), war),
                    wdma.issue(lambda e, g=g: e.dma_start(wotmp[par], wo_d[g]), war))

        grp_fetch[0] = fetch_group(0)

        ln_v, tr_a = [], []
        sc0 = contextlib.ExitStack()
        with sc0:
            p0x = sc0.enter_context(nc.sbuf_tensor("ph0x", [128, 2048], F32)).ap()
            p0c = sc0.enter_context(nc.sbuf_tensor("ph0c", [128, 2048], F32)).ap()
            p0s = sc0.enter_context(nc.sbuf_tensor("ph0s", [128, 2048], F32)).ap()
            p0n = sc0.enter_context(nc.sbuf_tensor("ph0n", [128, 2048], BF16)).ap()
            p0r = sc0.enter_context(nc.sbuf_tensor("ph0r", [128, 8], F32)).ap()
            xn_tile_a = []
            for tt in range(32):
                par = tt % 2
                bufs = (p0x[:, 1024 * par:1024 * (par + 1)],
                        p0c[:, 1024 * par:1024 * (par + 1)],
                        p0s[:, 1024 * par:1024 * (par + 1)],
                        p0n[:, 1024 * par:1024 * (par + 1)],
                        [p0r[:, 4 * par + i:4 * par + i + 1] for i in range(4)])
                half = xn_own if tt < 16 else xn_oth
                tl = tt % 16

                def dw(c, half=half, tl=tl):
                    return half[:, OWN * c + 128 * tl:OWN * c + 128 * (tl + 1)]
                emit_ln(bufs, 128 * tt, dw, ln_v, tr_a, [], tr_psum)
                xn_tile_a.append(sa.n)
        m_xn_a = sa.n
        m_ph0_v = sv.n
        m_g128s = emit_lam_g128()
        if "xn_own" in taps:
            wdma.issue(lambda e: e.dma_start(taps["xn_own"], xn_own),
                       waits=[(sa, m_xn_a)])
        if "xn_oth" in taps:
            wdma.issue(lambda e: e.dma_start(taps["xn_oth"], xn_oth),
                       waits=[(sa, m_xn_a)])

        scG = scA.enter_context(contextlib.ExitStack())
        if True:
            kg = scG.enter_context(nc.sbuf_tensor("kg", [128, 4096], BF16)).ap()
            qg = scG.enter_context(nc.sbuf_tensor("qg", [128, 2048], BF16)).ap()
            vg = scG.enter_context(nc.sbuf_tensor("vg", [128, 4096], BF16)).ap()
            vgt = scG.enter_context(nc.sbuf_tensor("vgt", [128, 32 * 130], BF16)).ap()
            ksq = scG.enter_context(nc.sbuf_tensor("ksq", [128, 4096], BF16)).ap()
            e01 = [scG.enter_context(nc.sbuf_tensor(f"e01{i}", [128, 1024], BF16)).ap()
                   for i in range(3)]
            o0s = scG.enter_context(nc.sbuf_tensor("o0s", [65, 512], F32)).ap()
            o1s = scG.enter_context(nc.sbuf_tensor("o1s", [65, 512], F32)).ap()
            cs = scG.enter_context(nc.sbuf_tensor("cs", [64, 512], F32)).ap()
            t1s = scG.enter_context(nc.sbuf_tensor("t1s", [64, 512], F32)).ap()
            csq = scG.enter_context(nc.sbuf_tensor("csq", [64, 512], BF16)).ap()
            wosb = scG.enter_context(nc.sbuf_tensor("wosb", [128, 1024], F32)).ap()

            prog.do("gpsimd", lambda e: e.memset(vgt, 1.0),
                    waits=[(sa, m_xn_a), (sv, m_ph0_v)])
            m_vgt_ones = sg.n

            pending = []
            marks = {}
            cat_qv = {}
            mk_exp = {}
            mk_av = {}
            m_evac_prev = 0
            pj_a = []
            vt_v = []

            for g in range(NG):
                if g + 1 < NG:
                    grp_fetch[g + 1] = fetch_group(g + 1)
                par = g % 2
                mk_q, mk_k, mk_v, mk_o = grp_fetch[g]

                prog.do("vector", lambda e, g=g, w=wotmp[par]: e.tensor_scalar(
                    wob_all[:, 1024 * g:1024 * (g + 1)], w, g128s, None, OP.mult),
                    waits=[mk_o, (sa, m_g128s)])

                if g > 0:
                    prog.do("tensor", None, waits=[(st, grp_lastuse[g - 1])])
                    for _ in range(8):
                        emit_warm()

                def proj(wb, mk_w, nchunks, scale, dst):
                    for tc in range(nchunks):
                        pj = ppj[tc % 2]
                        src = xn_own if (nchunks == 4 or tc < 4) else xn_oth
                        toff = 512 * (tc if (nchunks == 4 or tc < 4) else tc - 4)
                        half_off = 0 if (nchunks == 4 or tc < 4) else 16
                        need_tile = half_off + (toff // 512) * 4 + 3
                        m_xn_need = xn_tile_a[need_tile]
                        war = [(sa, pj_a[-2])] if len(pj_a) >= 2 else []
                        w0 = []
                        if g > 0 and tc == 0:
                            w0 = [(sa, mk_exp.get(0, 0)), (sa, mk_exp.get(1, 0))]
                        for ic in range(8):
                            prog.do("tensor", lambda e, pj=pj, wb=wb, ic=ic,
                                    src=src, toff=toff: e.matmul(
                                        pj, wb[:, 128 * ic:128 * (ic + 1)],
                                        src[:, OWN * ic + toff:OWN * ic + toff + 512],
                                        start=(ic == 0), stop=(ic == 7)),
                                    waits=([mk_w, (sa, m_xn_need)] + war + w0)
                                    if ic == 0 else [],
                                    incs=[(st, 1)] if ic == 7 else [])
                        mt = st.n
                        dsl = dst[:, 512 * tc:512 * (tc + 1)]
                        if scale is None:
                            prog.do("scalar", lambda e, d=dsl, pj=pj:
                                    e.activation(d, pj, AF.Copy),
                                    waits=[(st, mt)])
                        else:
                            prog.do("scalar", lambda e, d=dsl, pj=pj, sc=scale:
                                    e.activation(d, pj, AF.Copy, scale=sc),
                                    waits=[(st, mt)])
                        pj_a.append(sa.n)
                    return sa.n

                m_k_a = proj(wkb[par], mk_k, 8, QK_SCALE, kg)
                m_v_a = proj(wvb[par], mk_v, 8, None, vg)
                m_q_a = proj(wqb[par], mk_q, 4, QK_SCALE, qg)

                for kt in range(32):
                    jp = kt % 2
                    war2 = [(sv, vt_v[-2])] if len(vt_v) >= 2 else []
                    prog.do("tensor", lambda e, pt=tr_psum[jp], kt=kt:
                            e.transpose(pt, vg[:, 128 * kt:128 * (kt + 1)],
                                        identb),
                            waits=[(sa, m_v_a), (sg, m_vgt_ones)] + war2,
                            incs=[(st, 1)])
                    mt = st.n
                    dst = vgt[:, 130 * kt:130 * kt + 130].rearrange(
                        "p (v d) -> p v d", v=2, d=65)[:, :, 0:64]
                    srcv = tr_psum[jp].rearrange("p (v d) -> p v d", v=2, d=64)
                    del jp
                    prog.do("vector", lambda e, d=dst, s=srcv: e.tensor_copy(d, s),
                            waits=[(st, mt)])
                    vt_v.append(sv.n)
                m_vgt_v = sv.n
                m_vt_t = st.n

                prog.do("vector", lambda e: e.tensor_mul(ksq, kg, kg),
                        waits=[(sa, m_k_a)])
                prog.do("vector", lambda e: e.tensor_mul(
                    vg[:, 0:2048], qg, qg),
                    waits=[(sa, m_q_a), (st, m_vt_t)])
                m_sq_v = sv.n

                ms_v = []
                scans = [(ksq, 0, 8, 8, 10, 0), (vg, 0, 4, 9, 11, 0),
                         (ksq, 0, 8, 13, 15, 1), (vg, 0, 4, 14, 16, 1)]
                steps = []
                for si, (srcap, base, nch, oc, tc_, p_) in enumerate(scans):
                    for c8 in range(nch):
                        steps.append((si, c8))
                order = []
                c8pos = [0] * 4
                while len(order) < len(steps):
                    for si in range(4):
                        nch = scans[si][2]
                        if c8pos[si] < nch:
                            order.append((si, c8pos[si]))
                            c8pos[si] += 1
                gi = 0
                for si, c8 in order:
                    srcap, base, nch, oc, tc_, p_ = scans[si]
                    pp = (psS[0:1, 0:512] if gi % 2 == 0
                          else psS[0:1, 512:1024])
                    gi += 1
                    war = [(sv, ms_v[-2])] if len(ms_v) >= 2 else []
                    prog.do("tensor", lambda e, pp=pp, c8=c8, p=p_,
                            base=base, srcap=srcap: e.matmul(
                                pp, maskp[64 * p:64 * (p + 1), :],
                                srcap[64 * p:64 * (p + 1),
                                      base + 512 * c8:base + 512 * (c8 + 1)],
                                start=True, stop=True),
                            waits=[(sv, m_sq_v),
                                   (sv, marks.get("c7", 0))] + war,
                            incs=[(st, 1)])
                    mt = st.n
                    if c8 == 0:
                        prog.do("vector", lambda e, pp=pp, oc=oc:
                                e.reduce_max(smll[:, oc:oc + 1], pp, axis=AX),
                                waits=[(st, mt)])
                    else:
                        prog.do("vector", lambda e, pp=pp, tc_=tc_:
                                e.reduce_max(smll[:, tc_:tc_ + 1], pp, axis=AX),
                                waits=[(st, mt)])
                        prog.do("vector", lambda e, oc=oc, tc_=tc_:
                                e.tensor_max(smll[:, oc:oc + 1],
                                             smll[:, oc:oc + 1],
                                             smll[:, tc_:tc_ + 1]))
                    ms_v.append(sv.n)
                prog.do("vector", lambda e: e.tensor_add(
                    smll[:, 11:12], smll[:, 8:9], smll[:, 9:10]))
                prog.do("vector", lambda e: e.tensor_single_scalar(
                    smll[:, 12:13], smll[:, 11:12], -0.5, OP.mult))
                prog.do("vector", lambda e: e.tensor_add(
                    smll[:, 17:18], smll[:, 13:14], smll[:, 14:15]))
                prog.do("vector", lambda e: e.tensor_single_scalar(
                    smll[:, 18:19], smll[:, 17:18], -0.5, OP.mult))
                m_mh_v = sv.n
                marks["stat_v"] = sv.n
                bias_a = {}
                for p_ in range(2):
                    col = 12 if p_ == 0 else 18
                    prog.do("tensor", lambda e, col=col: e.matmul(
                        pb, ones128f, smll[:, col:col + 1],
                        start=True, stop=True),
                        waits=[(sv, m_mh_v), (sv, m_evac_prev), (sg, m_init),
                               (sa, bias_a.get(0, 0))],
                        incs=[(st, 1)])
                    mt = st.n
                    prog.do("scalar", lambda e, p=p_: e.activation(
                        bias_gp[:, p:p + 1], pb, AF.Copy),
                        waits=[(st, mt)] + marks.pop("biastap", []))
                    bias_a[p_] = sa.n
                if "bias" in taps and g == 0:
                    mkb = sdma.issue(lambda e: e.dma_start(
                        taps["bias"], bias_gp), waits=[(sa, bias_a[1])])
                    marks["biastap"] = [mkb]

                def emit_warm():
                    prog.do("tensor", lambda e: e.matmul(
                        pwarm[:, 0:448], identb, xn_own[:, 0:448],
                        start=True, stop=True),
                        waits=[(sv, marks.get("stat_v", 0)), (sa, m_xn_a),
                               (sg, m_init)])

                for p in range(2):
                    m_bias_a = bias_a[1]

                    def emit_av(kt, eb, p=p):
                        vsl = vgt[:, 130 * kt + 65 * p:
                                  130 * kt + 65 * (p + 1)]
                        avw = [(sa, mk_e_rdy[eb])]
                        if kt == 0:
                            avw.append((sv, m_evac_prev))
                        prog.do("tensor", lambda e, vsl=vsl, eb=eb, kt=kt:
                                e.matmul(po0, vsl, e01[eb][:, 0:512],
                                         start=(kt == 0), stop=(kt == 31)),
                                waits=avw)
                        prog.do("tensor", lambda e, vsl=vsl, eb=eb, kt=kt:
                                e.matmul(po1, vsl, e01[eb][:, 512:1024],
                                         start=(kt == 0), stop=(kt == 31)),
                                incs=[(st, 1)])
                        mk_av[eb] = st.n

                    mk_e_rdy = {}
                    for gkt in range(130):
                        if gkt < 128:
                            qv, kt = divmod(gkt, 32)
                            if kt in (8, 10, 16, 18, 24, 26) and pending:
                                pending.pop(0)()
                            kp = gkt % 2
                            kp3 = gkt % 3
                            Sps = S01[kp]
                            emit_warm()
                            for e_i in range(2):
                                w0 = []
                                if e_i == 0:
                                    w0 = [(sa, mk_exp.get(kp, 0)),
                                          (sa, m_bias_a), (sv, m_vgt_v)]
                                tp = (64 * p + 32 * e_i, 0)
                                prog.do("tensor", lambda e, Sps=Sps, e_i=e_i,
                                        kt=kt, p=p, qv=qv, tp=tp: e.matmul(
                                            Sps[:, 512 * e_i:512 * (e_i + 1)],
                                            kg[64 * p + 32 * e_i:
                                               64 * p + 32 * (e_i + 1),
                                               128 * kt:128 * (kt + 1)],
                                            qg[64 * p + 32 * e_i:
                                               64 * p + 32 * (e_i + 1),
                                               512 * qv:512 * (qv + 1)],
                                            start=True, stop=True,
                                            tile_position=tp),
                                        waits=w0,
                                        incs=[(st, 1)] if e_i == 1 else [])
                            mS = st.n
                            prog.do("scalar", lambda e, kp3=kp3, Sps=Sps, p=p:
                                    e.activation(e01[kp3], Sps, AF.Exp,
                                                 bias=bias_gp[:, p:p + 1]),
                                    waits=[(st, mS), (st, mk_av.get(kp3, 0))],
                                    )
                            mk_exp[kp] = sa.n
                            mk_e_rdy[kp3] = sa.n
                        if gkt < 2:
                            continue
                        t_ = gkt - 2
                        tqv, tk = divmod(t_, 32)
                        emit_warm()
                        emit_av(tk, t_ % 3)
                        if tk != 31:
                            continue
                        m_av = st.n
                        grp_lastuse[g] = st.n
                        prev_c7 = marks.get("c7", 0)
                        qv = tqv

                        def mk_c1(m_av=m_av, g=g, p=p, qv=qv, prev_c7=prev_c7):
                            prog.do("vector", lambda e: e.tensor_copy(
                                o0s, po0),
                                waits=[(st, m_av), (sv, prev_c7)]
                                + marks.pop("tapwar", []))
                            prog.do("vector", lambda e: e.tensor_copy(
                                o1s, po1))
                            marks["evac"] = sv.n
                            ma = sv.n
                            if "o0" in taps and g == 0 and p == 0 and qv == 0:
                                mk1 = sdma.issue(lambda e: e.dma_start(
                                    taps["o0"], o0s), waits=[(sv, ma)])
                                mk2 = sdma.issue(lambda e: e.dma_start(
                                    taps["o1"], o1s), waits=[(sv, ma)])
                                marks["tapwar"] = [mk1, mk2]
                            prog.do("vector", lambda e: e.reciprocal(
                                rows[:, 512:1024], o1s[64:65, :]))
                            prog.do("vector", lambda e: e.tensor_copy(
                                rows[:, 2048:2560], o0s[64:65, :]))
                            prog.do("vector", lambda e: e.scalar_tensor_tensor(
                                rows[:, 1024:1536], rows[:, 2048:2560], lamA,
                                rows[:, 512:1024], OP.mult, OP.mult))
                            prog.do("vector", lambda e: e.tensor_copy(
                                rowsb[:, 0:512], rows[:, 1024:1536]))
                            prog.do("vector", lambda e: e.reciprocal(
                                rows[:, 0:512], o0s[64:65, :]))
                            marks["c1"] = sv.n

                        def mk_c2():
                            prog.do("tensor", lambda e: e.matmul(
                                sp64, ones64f, rowsb[:, 0:512],
                                start=True, stop=True),
                                waits=[(sv, marks["c1"]),
                                       (sv, marks.get("c7", 0)),
                                       (sv, marks.get("stat_v", 0))],
                                incs=[(st, 1)])
                            marks["c2"] = st.n

                        def mk_c3():
                            prog.do("vector", lambda e: e.tensor_mul(
                                t1s, o1s[0:64, :], sp64),
                                waits=[(st, marks["c2"])])
                            prog.do("vector", lambda e: e.tensor_sub(
                                cs, o0s[0:64, :], t1s))
                            prog.do("vector", lambda e: e.tensor_mul(csq, cs, cs),
                                    )
                            marks["c3"] = sv.n

                        def mk_c4():
                            prog.do("tensor", lambda e: e.matmul(
                                sp1, ones64c, csq, start=True, stop=True),
                                waits=[(sv, marks["c3"])], incs=[(st, 1)])
                            marks["c4"] = st.n

                        def mk_c5():
                            prog.do("vector", lambda e: e.tensor_mul(
                                rows[:, 1536:2048], sp1, rows[:, 0:512]),
                                waits=[(st, marks["c4"])])
                            prog.do("vector", lambda e: e.tensor_mul(
                                rows[:, 1536:2048], rows[:, 1536:2048],
                                rows[:, 0:512]))
                            mv5 = sv.n
                            prog.do("scalar", lambda e: e.activation(
                                rows[:, 1536:2048], rows[:, 1536:2048], AF.Ln,
                                bias=epsb[0:1, :], scale=1.0 / 64),
                                waits=[(sv, mv5)])
                            prog.do("scalar", lambda e: e.activation(
                                rows[:, 1536:2048], rows[:, 1536:2048], AF.Exp,
                                scale=-0.5))
                            ma5 = sa.n
                            prog.do("vector", lambda e: e.tensor_mul(
                                rows[:, 1536:2048], rows[:, 1536:2048],
                                rows[:, 0:512]),
                                waits=[(sa, ma5)])
                            prog.do("vector", lambda e: e.tensor_copy(
                                rowsb[:, 512:1024], rows[:, 1536:2048]))
                            marks["c5"] = sv.n

                        def mk_c6():
                            prog.do("tensor", lambda e: e.matmul(
                                sp64, ones64f, rowsb[:, 512:1024],
                                start=True, stop=True),
                                waits=[(sv, marks["c5"]), (sv, marks["c3"])],
                                incs=[(st, 1)])
                            marks["c6"] = st.n

                        def mk_c7(g=g, p=p, qv=qv):
                            dst = attn_cat[64 * p:64 * (p + 1),
                                           OWN * g + 512 * qv:
                                           OWN * g + 512 * (qv + 1)]
                            prog.do("vector", lambda e, dst=dst: e.tensor_mul(
                                dst, cs, sp64),
                                waits=[(st, marks["c6"])])
                            marks["c7"] = sv.n
                            cat_qv[qv] = sv.n

                        mk_c1()
                        m_evac_prev = marks["evac"]
                        pending.extend([mk_c2, mk_c3, mk_c4, mk_c5, mk_c6,
                                        mk_c7])
            m_exp_l = max(mk_exp.values())

            wo_a = []
            wo_dma = []
            for u, (qc, ot) in enumerate((qc, ot) for qc in range(4)
                                         for ot in range(8)):
                    emit_warm()
                    if u == 4:
                        while len(pending) > 2:
                            pending.pop(0)()
                    if u == 8:
                        while pending:
                            pending.pop(0)()
                        m_cat_v = sv.n
                    pj = ppj[u % 2]
                    war = [(sv, wo_a[-2])] if len(wo_a) >= 2 else []
                    for ic in range(8):
                        prog.do("tensor", lambda e, pj=pj, ic=ic, ot=ot, qc=qc:
                                e.matmul(
                                    pj,
                                    wob_all[:, 1024 * ic + 128 * ot:
                                            1024 * ic + 128 * (ot + 1)],
                                    attn_cat[:, OWN * ic + 512 * qc:
                                             OWN * ic + 512 * (qc + 1)],
                                    start=(ic == 0), stop=(ic == 7)),
                                waits=([(sv, cat_qv[qc]), (sa, m_exp_l)]
                                       + war) if ic == 0 else [],
                                incs=[(st, 1)] if ic == 7 else [])
                    mt = st.n
                    wslots = [wosb[:, 0:512], wosb[:, 512:1024],
                              wqb[0].bitcast(F32), wqb[1].bitcast(F32)]
                    wdst = wslots[u % 4]
                    aw = [(st, mt)]
                    if len(wo_dma) >= 4:
                        aw.append(wo_dma[-4])
                    prog.do("vector", lambda e, d=wdst, pj=pj: e.tensor_copy(
                        d, pj), waits=aw)
                    wo_a.append(sv.n)
                    ma = sv.n
                    mk = sdma.issue(lambda e, d=wdst, ot=ot, qc=qc: e.dma_start(
                        scr_d[128 * ot:128 * (ot + 1), 512 * qc:512 * (qc + 1)],
                        d), waits=[(sv, ma)])
                    wo_dma.append(mk)
            for _ in range(12):
                emit_warm()
            if "attn_cat" in taps:
                wdma.issue(lambda e: e.dma_start(taps["attn_cat"], attn_cat),
                           waits=[(sv, m_cat_v)])
            if "kg" in taps:
                wdma.issue(lambda e: e.dma_start(taps["kg"], kg),
                           waits=[(sv, m_cat_v)])
            if "qg" in taps:
                wdma.issue(lambda e: e.dma_start(taps["qg"], qg),
                           waits=[(sv, m_cat_v)])
            if "vgt" in taps:
                wdma.issue(lambda e: e.dma_start(taps["vgt"], vgt),
                           waits=[(sv, m_cat_v)])
            if "attn_scr" in taps:
                sdma.issue(lambda e: e.dma_start(taps["attn_scr"], scr_d),
                           waits=sdma.fin())

            w12_pre = {}
            for ot in range(2):
                par = ot
                w1sl = w12b[:, 2048 * par:2048 * par + 1024]
                w2sl = w12b[:, 2048 * par + 1024:2048 * (par + 1)]
                w12_pre[ot] = (
                    wdma.issue(lambda e, d=w1sl, ot=ot: e.dma_start(d, w1_d[ot])),
                    wdma.issue(lambda e, d=w2sl, ot=ot: e.dma_start(d, w2_d[ot])))

    scA.close()
    bar_marks = ([(st, st.n), (sv, sv.n), (sa, sa.n), (sg, sg.n)]
                 + wdma.fin() + xdma.fin() + sdma.fin() + odma.fin())
    for _e in ("sync", "scalar", "vector", "gpsimd"):
        prog.do(_e, None, waits=bar_marks)
    scr_fin = sdma.fin()

    with (
        nc.sbuf_tensor("a1", [128, 32 * TCH], BF16) as d1_,
        nc.sbuf_tensor("hbT", [128, 3 * 4096], BF16) as hb_top_,
        nc.sbuf_tensor("xldp", [128, 2 * 1024], F32) as xld_,
        nc.sbuf_tensor("identf", [128, 128], F32) as idf_,
        nc.psum_tensor("pM0", [128, 1024], F32) as e1_,
        nc.psum_tensor("pM1", [128, 1024], F32) as e2_,
        nc.psum_tensor("pM2", [128, 1024], F32) as e3_,
        nc.psum_tensor("pM3", [128, 1024], F32) as e4_,
    ):
        a1 = d1_.ap()
        hb = hb_top_.ap()
        xld = xld_.ap()
        identf = idf_.ap()
        pM = [e1_.ap(), e2_.ap(), e3_.ap(), e4_.ap()]
        ptrF = [pM[3][:, 0:128], pM[3][:, 512:640]]
        prog.do("gpsimd", lambda e: e.memset(identf, 0.0))
        prog.do("gpsimd", lambda e: e.affine_select(
            out=identf, in_=identf, compare_op=OP.not_equal, fill=1.0, base=0,
            pattern=[[-1, 128]], channel_multiplier=1))
        m_idf = sg.n

        w12_t, w12_a, w12_v = [], [], []
        h_t, h_a = [], []
        ring_t = []
        w3_t, w3_a, f3_v, f3_t, pc_v = [], [], [], [], []
        xadd_v = []
        out_done = []
        tap_gate = []

        for ch in range(NCH):
            coff = TCH * ch
            chunk_gate = []
            if ch > 0:
                chunk_gate = ([(st, w3_t[-1]), (sv, pc_v[-1]),
                               (sa, w3_a[-1])] + list(tap_gate))

            if "xnc" in taps and ch == 0:
                for icq in range(8):
                    tap_gate.append(wdma.issue(
                        lambda e, icq=icq: e.dma_start(
                            taps["xnc"][:, TCH * icq:TCH * (icq + 1)],
                            xn_own[:, OWN * icq + coff:
                                   OWN * icq + coff + TCH]), waits=[]))

            if True:
                with nc.sbuf_tensor(f"t0b{ch}", [128, 2 * 1024], F32) as t0b_:
                    t0b = t0b_.ap()
                    for ot in range(32):
                        par = ot % 2
                        w1sl = w12b[:, 2048 * par:2048 * par + 1024]
                        w2sl = w12b[:, 2048 * par + 1024:2048 * (par + 1)]
                        if ch == 0 and ot < 2:
                            mk1, mk2 = w12_pre[ot]
                        else:
                            war = ([(st, w12_t[-2])] if len(w12_t) >= 2
                                   else [])
                            mk1 = wdma.issue(lambda e, d=w1sl, ot=ot:
                                             e.dma_start(d, w1_d[ot]),
                                             waits=war)
                            mk2 = wdma.issue(lambda e, d=w2sl, ot=ot:
                                             e.dma_start(d, w2_d[ot]),
                                             waits=war)
                        pa = pM[0] if par == 0 else pM[2]
                        pb2 = pM[1] if par == 0 else pM[3]
                        war_a = [(sa, w12_a[-2])] if len(w12_a) >= 2 else []
                        war_v = [(sv, w12_v[-2])] if len(w12_v) >= 2 else []
                        w1w0, w1w1 = [], []
                        if ch == 0 and ot == 0:
                            w1w0 = [(sv, wo_a[-2])]
                            w1w1 = [(sv, wo_a[-1])]
                        elif ch > 0 and ot == 0:
                            w1w0 = [(sa, w3_a[-1])] + list(tap_gate)
                        elif ch > 0 and ot == 1:
                            w1w0 = [(sv, pc_v[-1])]
                        for ic in range(8):
                            for hc in range(2):
                                prog.do("tensor", lambda e, pa=pa, w=w1sl,
                                        ic=ic, hc=hc, coff=coff: e.matmul(
                                            pa[:, 512 * hc:512 * (hc + 1)],
                                            w[:, 128 * ic:128 * (ic + 1)],
                                            xn_own[:, OWN * ic + coff + 512 * hc:
                                                   OWN * ic + coff + 512 * (hc + 1)],
                                            start=(ic == 0), stop=(ic == 7)),
                                        waits=([mk1] + war_a + w1w0)
                                        if (ic == 0 and hc == 0) else
                                        (w1w1 if (ic == 0 and hc == 1) else []))
                        for ic in range(8):
                            for hc in range(2):
                                lastw = (ic == 7 and hc == 1)
                                prog.do("tensor", lambda e, pb2=pb2, w=w2sl,
                                        ic=ic, hc=hc, coff=coff: e.matmul(
                                            pb2[:, 512 * hc:512 * (hc + 1)],
                                            w[:, 128 * ic:128 * (ic + 1)],
                                            xn_own[:, OWN * ic + coff + 512 * hc:
                                                   OWN * ic + coff + 512 * (hc + 1)],
                                            start=(ic == 0), stop=(ic == 7)),
                                        waits=([mk2] + war_v)
                                        if (ic == 0 and hc == 0) else [],
                                        incs=[(st, 1)] if lastw else [])
                        w12_t.append(st.n)
                        mt = st.n
                        t0 = t0b[:, 1024 * par:1024 * (par + 1)]
                        prog.do("scalar", lambda e, t0=t0, pa=pa, ot=ot:
                                e.activation(t0, pa, AF.Silu,
                                             scale=s1s[:, ot:ot + 1]),
                                waits=[(st, mt)] + war_v)
                        w12_a.append(sa.n)
                        ma = sa.n
                        prog.do("vector", lambda e, pb2=pb2, t0=t0, ot=ot:
                                e.scalar_tensor_tensor(
                                    a1[:, TCH * ot:TCH * (ot + 1)], pb2,
                                    s2s[:, ot:ot + 1], t0, OP.mult, OP.mult),
                                waits=[(sa, ma)])
                        w12_v.append(sv.n)
                    m_a1_v = sv.n
                a1_tap_mk = []
                if "a1" in taps and ch == 0:
                    a1_tap_mk.append(wdma.issue(
                        lambda e: e.dma_start(taps["a1"], a1),
                        waits=[(sv, m_a1_v)]))

            with (
                nc.sbuf_tensor(f"a2{ch}", [128, 32 * TCH], BF16) as i1_,
            ):
                a2 = i1_.ap()

                def hstage(hd, srcap, dstap, ssc, src_ready, extra0=()):
                    for ot in range(32):
                        par = ot % 2
                        sl3 = len(ring_t) % 3
                        wsl = hb[:, 4096 * sl3:4096 * (sl3 + 1)]
                        war = [(st, ring_t[-3])] if len(ring_t) >= 3 else []
                        mk = wdma.issue(lambda e, d=wsl, ot=ot, hd=hd:
                                        e.dma_start(d, hd[ot]), waits=war)
                        pp = pM[par]
                        war_a = [(sa, h_a[-2])] if len(h_a) >= 2 else []
                        if not h_a:
                            war_a = [(sv, m_a1_v)]
                        if ot == 0:
                            war_a = war_a + list(extra0)
                        for ic in range(32):
                            for hc in range(2):
                                lastw = (ic == 31 and hc == 1)
                                prog.do("tensor", lambda e, pp=pp, w=wsl,
                                        ic=ic, hc=hc, srcap=srcap: e.matmul(
                                            pp[:, 512 * hc:512 * (hc + 1)],
                                            w[:, 128 * ic:128 * (ic + 1)],
                                            srcap[:, TCH * ic + 512 * hc:
                                                  TCH * ic + 512 * (hc + 1)],
                                            start=(ic == 0), stop=(ic == 31)),
                                        waits=([mk, src_ready] + war_a)
                                        if (ic == 0 and hc == 0) else [],
                                        incs=[(st, 1)] if lastw else [])
                        h_t.append(st.n)
                        ring_t.append(st.n)
                        mt = st.n
                        prog.do("scalar", lambda e, pp=pp, ot=ot, dstap=dstap,
                                ssc=ssc: e.activation(
                                    dstap[:, TCH * ot:TCH * (ot + 1)], pp,
                                    AF.Gelu, scale=ssc[:, ot:ot + 1]),
                                waits=[(st, mt)])
                        h_a.append(sa.n)
                    return sa.n

                m_a2_a = hstage(h1_d, a1, a2, h1ss, (sv, m_a1_v),
                                extra0=((list(out_done[-1])
                                         + [(sv, pc_v[-1])])
                                        if out_done else ()))
                if "a2" in taps and ch == 0:
                    tap_gate.append(wdma.issue(
                        lambda e, a2=a2: e.dma_start(taps["a2"], a2),
                        waits=[(sa, m_a2_a)]))
                m_a3_a = hstage(h2_d, a2, a1, h2ss, (sa, m_a2_a),
                                extra0=a1_tap_mk)

            with (
                nc.sbuf_tensor(f"f3s{ch}", [128, 2 * 1024], F32) as j2_,
                nc.sbuf_tensor(f"asld{ch}", [128, 2 * 1024], F32) as j3_,
                nc.sbuf_tensor(f"outT{ch}", [128, 8 * 1024], F32) as j5_,
            ):
                f3sb, asld, outT = (j2_.ap(), j3_.ap(), j5_.ap())
                row_mk = []
                for ot3 in range(8):
                    par = ot3 % 2
                    xwar = ([(sv, xadd_v[-2])] if len(xadd_v) >= 2 else [])
                    if ot3 < 2 and out_done:
                        xwar = xwar + list(out_done[-1])
                    xmk = []
                    for tj in range(8):
                        row0 = coff + 128 * tj
                        xmk.append(xdma.issue(
                            lambda e, tj=tj, row0=row0, par=par, ot3=ot3:
                            e.dma_start(
                                xld[:, 1024 * par + 128 * tj:
                                    1024 * par + 128 * (tj + 1)],
                                x_d[row0:row0 + 128,
                                    128 * ot3:128 * (ot3 + 1)]), waits=xwar))
                    sl3 = len(ring_t) % 3
                    wsl = hb[:, 4096 * sl3:4096 * (sl3 + 1)]
                    war = [(st, ring_t[-3])] if len(ring_t) >= 3 else []
                    mk = wdma.issue(lambda e, d=wsl, ot3=ot3: e.dma_start(
                        d, w3_d[ot3]), waits=war)
                    mka = sdma.issue(lambda e, ot3=ot3, par=par, coff=coff,
                                     asld=asld: e.dma_start(
                                         asld[:, 1024 * par:1024 * (par + 1)],
                                         scr_d[128 * ot3:128 * (ot3 + 1),
                                               coff:coff + TCH]),
                                     waits=scr_fin + (chunk_gate +
                                      [(sa, m_a3_a)] if ot3 < 2 else []) +
                                     ([(sv, f3_v[-2])] if len(f3_v) >= 2
                                      else [(sa, m_a3_a)]))
                    pp = pM[par]
                    war_a = [(sa, w3_a[-2])] if len(w3_a) >= 2 else []
                    for ic in range(32):
                        for hc in range(2):
                            lastw = (ic == 31 and hc == 1)
                            prog.do("tensor", lambda e, pp=pp, w=wsl, ic=ic,
                                    hc=hc: e.matmul(
                                        pp[:, 512 * hc:512 * (hc + 1)],
                                        w[:, 128 * ic:128 * (ic + 1)],
                                        a1[:, TCH * ic + 512 * hc:
                                            TCH * ic + 512 * (hc + 1)],
                                        start=(ic == 0), stop=(ic == 31)),
                                    waits=([mk, (sa, m_a3_a)] + war_a)
                                    if (ic == 0 and hc == 0) else [],
                                    incs=[(st, 1)] if lastw else [])
                    w3_t.append(st.n)
                    ring_t.append(st.n)
                    mt = st.n
                    f3 = f3sb[:, 1024 * par:1024 * (par + 1)]
                    fw = [(st, mt)]
                    if len(f3_t) >= 2:
                        fw.append((st, f3_t[-2]))
                    if ot3 < 2 and chunk_gate:
                        fw += chunk_gate
                    prog.do("scalar", lambda e, f3=f3, pp=pp, ot3=ot3:
                            e.activation(f3, pp, AF.Copy,
                                         scale=w3ss[:, ot3:ot3 + 1]),
                            waits=fw)
                    w3_a.append(sa.n)
                    ma = sa.n
                    prog.do("vector", lambda e, f3=f3, par=par, asld=asld:
                            e.tensor_add(
                        f3, f3, asld[:, 1024 * par:1024 * (par + 1)]),
                        waits=[(sa, ma), mka, (sv, m_a1_v)])
                    f3_v.append(sv.n)
                    mfv = sv.n
                    for tj in range(8):
                        jp = tj % 2
                        war2 = [(sv, pc_v[-2])] if len(pc_v) >= 2 else []
                        if not pc_v:
                            war2 = [(sg, m_idf)]
                        prog.do("tensor", lambda e, jp=jp, f3=f3, tj=tj:
                                e.transpose(ptrF[jp],
                                            f3[:, 128 * tj:128 * (tj + 1)],
                                            identf),
                                waits=[(sv, mfv)] + war2, incs=[(st, 1)])
                        mt2 = st.n
                        dst = outT[:, TCH * tj + 128 * ot3:
                                   TCH * tj + 128 * (ot3 + 1)]
                        xsl = xld[:, 1024 * par + 128 * tj:
                                  1024 * par + 128 * (tj + 1)]
                        prog.do("vector", lambda e, d=dst, jp=jp, xsl=xsl:
                                e.tensor_add(d, ptrF[jp], xsl),
                                waits=[(st, mt2), xmk[tj]])
                        pc_v.append(sv.n)
                        if ot3 == 7:
                            row_mk.append(sv.n)
                    xadd_v.append(sv.n)
                    f3_t.append(st.n)
                m_outT_v = sv.n
                for tj in range(8):
                    odma.issue(lambda e, tj=tj, coff=coff, outT=outT:
                               e.dma_start(
                        out_d[coff + 128 * tj:coff + 128 * (tj + 1), :],
                        outT[:, TCH * tj:TCH * (tj + 1)]),
                        waits=[(sv, row_mk[tj])])
                if "outT" in taps and ch == 0:
                    tap_gate.append(wdma.issue(
                        lambda e, outT=outT: e.dma_start(taps["outT"], outT),
                        waits=[(sv, m_outT_v)]))
                out_done.append(odma.fin())

        prog.do("sync", None, waits=odma.fin() + sdma.fin())
        prog.do("gpsimd", None, waits=wdma.fin())

    with nc.Block() as block:
        prog.replay(block)
    stack.close()
    return nc


def host_layout(inputs):
    W = lambda a: np.ascontiguousarray(np.asarray(a), dtype=np.float32)
    o = {}
    o["wq4"] = W(np.asarray(inputs["L0_attn_Wq"]).reshape(8, 128, 8, 128)
                 .transpose(0, 3, 2, 1).reshape(8, 128, 1024))
    o["wk4"] = W(np.asarray(inputs["L0_attn_Wk"]).reshape(8, 128, 8, 128)
                 .transpose(0, 3, 2, 1).reshape(8, 128, 1024))
    o["wv4"] = W(np.asarray(inputs["L0_attn_Wv"]).reshape(8, 128, 8, 128)
                 .transpose(0, 3, 2, 1).reshape(8, 128, 1024))
    o["wo4"] = W(np.asarray(inputs["L0_attn_Wo"]).reshape(8, 128, 8, 128)
                 .transpose(2, 3, 0, 1).reshape(8, 128, 1024))
    o["w14"] = W(np.asarray(inputs["L0_ffn_w1_w"]).reshape(32, 128, 8, 128)
                 .transpose(0, 3, 2, 1).reshape(32, 128, 1024))
    o["w24"] = W(np.asarray(inputs["L0_ffn_w2_w"]).reshape(32, 128, 8, 128)
                 .transpose(0, 3, 2, 1).reshape(32, 128, 1024))
    o["h14"] = W(np.asarray(inputs["L0_ffn_h1_w"]).reshape(32, 128, 32, 128)
                 .transpose(0, 3, 2, 1).reshape(32, 128, 4096))
    o["h24"] = W(np.asarray(inputs["L0_ffn_h2_w"]).reshape(32, 128, 32, 128)
                 .transpose(0, 3, 2, 1).reshape(32, 128, 4096))
    o["w34"] = W(np.asarray(inputs["L0_ffn_w3_w"]).reshape(8, 128, 32, 128)
                 .transpose(0, 3, 2, 1).reshape(8, 128, 4096))
    o["s1c"] = W(np.asarray(inputs["L0_ffn_w1_s"]).reshape(32, 128).T)
    o["s2c"] = W(np.asarray(inputs["L0_ffn_w2_s"]).reshape(32, 128).T)
    o["h1sc"] = W(np.asarray(inputs["L0_ffn_h1_s"]).reshape(32, 128).T)
    o["h2sc"] = W(np.asarray(inputs["L0_ffn_h2_s"]).reshape(32, 128).T)
    o["w3sc"] = W(np.asarray(inputs["L0_ffn_w3_s"]).reshape(8, 128).T)
    g = np.asarray(inputs["L0_attn_g"])
    o["g128"] = W(np.concatenate([g, g]).reshape(128, 1))
    o["lamvec"] = W(np.concatenate([
        np.asarray(inputs["L0_attn_lambda_q1"]),
        np.asarray(inputs["L0_attn_lambda_k1"]),
        np.asarray(inputs["L0_attn_lambda_q2"]),
        np.asarray(inputs["L0_attn_lambda_k2"])]).reshape(1, 128))
    return o


def kernel(**inputs):
    shared = host_layout(inputs)
    x = np.ascontiguousarray(np.asarray(inputs["L0_attn_x"]), dtype=np.float32)
    nc = build()
    in_maps = []
    for c in range(8):
        b, h = c // 2, c % 2
        xb = x[b]
        xc = np.ascontiguousarray(np.concatenate(
            [xb[h * OWN:(h + 1) * OWN], xb[(1 - h) * OWN:(2 - h) * OWN]]))
        m = dict(shared)
        m["x"] = xc
        in_maps.append(m)
    res = run_bass_kernel_spmd(nc, in_maps, core_ids=list(range(8)))
    out = np.empty((4, SEQ, DM), dtype=np.float32)
    for c in range(8):
        b, h = c // 2, c % 2
        out[b, h * OWN:(h + 1) * OWN] = res.results[c]["out"]
    return out
